# version 33
# baseline (speedup 1.0000x reference)
import numpy as np
import ml_dtypes
import concourse.bass as bass
import concourse.mybir as mybir
from concourse.bass import AP
from concourse.bass_utils import run_bass_kernel_spmd
from concourse.library_config import attn as _ATTN_LIB
from concourse.library_overlay import lower_extended_insts
from contextlib import ExitStack

F32 = mybir.dt.float32
BF16 = mybir.dt.bfloat16
AF = mybir.ActivationFunctionType
ALU = mybir.AluOpType
AX = mybir.AxisListType
BFNP = ml_dtypes.bfloat16

S = 2048
HID = 2048
NHC = 16
NT = 16
NOWN = 8
NKV = 8
THETA = 10000.0
NEG_BIG = -1.0e30
EPS = 1e-5
QKV_PITCH = 2 * 4 * 1024


def _tile_fm(wt):
    K, N = wt.shape
    return np.ascontiguousarray(wt.reshape(K // 128, 128, N).transpose(1, 0, 2).reshape(128, -1))


def _tile_stream(wt, ob=128):
    K, N = wt.shape
    return np.ascontiguousarray(
        wt.reshape(K // 128, 128, N // ob, ob).transpose(1, 2, 0, 3).reshape(128, -1))


def _storage_order(b):
    return [s ^ 1 if b == 0 else s for s in range(NT)]


def _host_prepare(inputs):
    x = np.asarray(inputs["L0_attn_hidden_states"], dtype=np.float32)
    q_w = np.asarray(inputs["L0_attn_q_w"], dtype=np.float32)
    kr_w = np.asarray(inputs["L0_attn_kr_w"], dtype=np.float32)
    down_w = np.asarray(inputs["L0_attn_down_w"], dtype=np.float32)
    upk_w = np.asarray(inputs["L0_attn_upk_w"], dtype=np.float32)
    upv_w = np.asarray(inputs["L0_attn_upv_w"], dtype=np.float32)
    o_w = np.asarray(inputs["L0_attn_o_w"], dtype=np.float32)
    ffn_w = np.asarray(inputs["L0_ffn_weight"], dtype=np.float32)
    gate_w = np.asarray(inputs["L0_ffn_gate_w"], dtype=np.float32)
    A_w = np.asarray(inputs["L0_ffn_A_w"], dtype=np.float32)
    B_w = np.asarray(inputs["L0_ffn_B_w"], dtype=np.float32)

    shared = {
        "qw_s": _tile_stream(q_w.T, 128),
        "fw_s": _tile_stream(ffn_w.T, 128),
        "ow_s": _tile_stream(o_w.T, 128),
        "down_t": _tile_fm(down_w.T),
        "upk_t": _tile_fm(upk_w.T),
        "upv_t": _tile_fm(upv_w.T),
        "krg_t": _tile_fm(np.concatenate([kr_w.T, gate_w.T], axis=1)),
        "acat_t": _tile_fm(A_w.reshape(128, HID).T),
        "bcat_t": np.ascontiguousarray(B_w.transpose(0, 2, 1).reshape(128, HID)),
        "trimask": np.ascontiguousarray(np.tril(np.ones((128, 128), dtype=BFNP)).T),
        "ones2c": np.concatenate([
            np.concatenate([np.ones((64, 1)), np.zeros((64, 1))], 0),
            np.concatenate([np.zeros((64, 1)), np.ones((64, 1))], 0)], 1).astype(BFNP),
        "e8": np.ascontiguousarray(np.repeat(np.eye(8, dtype=BFNP), 16, axis=1)),
        "identb": np.eye(128, dtype=BFNP),
    }

    import jax as _jax
    import jax.numpy as _jnp
    with _jax.default_device(_jax.devices("cpu")[0]):
        _pos = _jnp.arange(S, dtype=_jnp.float32)
        _invf = 1.0 / THETA ** (_jnp.arange(0, 64, 2, dtype=_jnp.float32) / 64)
        _freqs = _pos[:, None] * _invf[None, :]
        COS_T = np.asarray(_jnp.cos(_freqs), dtype=np.float32)
        SIN_T = np.asarray(_jnp.sin(_freqs), dtype=np.float32)

    in_maps = []
    for core in range(8):
        seq, b = core % 4, core // 4
        order = _storage_order(b)
        xs = x[seq].reshape(NT, 128, HID)
        x_kv = np.ascontiguousarray(xs[order].reshape(S, HID))

        pos_own = np.concatenate(
            [np.arange((2 * j + b) * 128, (2 * j + b + 1) * 128) for j in range(NOWN)]
        )
        d = np.arange(128) % 64
        sgn = np.where(d < 32, -1.0, 1.0).astype(np.float32)[:, None]
        cosq = (COS_T[pos_own][:, d % 32].T * np.float32(0.125))
        sinq = (SIN_T[pos_own][:, d % 32].T * np.float32(0.125) * sgn)

        cosk = np.empty((128, NT * 32), dtype=np.float32)
        sink = np.empty((128, NT * 32), dtype=np.float32)
        c = np.arange(32)
        sgnk = np.where(c < 16, -1.0, 1.0).astype(np.float32)[None, :]
        for s_ in range(NT):
            pos = order[s_] * 128 + np.arange(128)
            cosk[:, s_ * 32:(s_ + 1) * 32] = COS_T[pos][:, c % 16]
            sink[:, s_ * 32:(s_ + 1) * 32] = SIN_T[pos][:, c % 16] * sgnk

        flag = np.full((128, 1), 0.0 if b == 0 else 1.0, dtype=np.float32)
        m = {"x_kv": x_kv, "cosq": np.ascontiguousarray(cosq),
             "sinq": np.ascontiguousarray(sinq), "cosk": cosk, "sink": sink,
             "flag": flag}
        m.update(shared)
        in_maps.append(m)
    return in_maps


def _build(dbg=()):
    nc = bass.Bass("TRN2", target_bir_lowering=False)

    D = {}
    for name, shape, dt in [
        ("x_kv", [S, HID], F32),
        ("qw_s", [128, 16 * 2048], F32),
        ("fw_s", [128, 16 * 2048], F32),
        ("ow_s", [128, 16 * 2048], F32),
        ("down_t", [128, 16 * 256], F32),
        ("upk_t", [128, 2 * 256], F32),
        ("upv_t", [128, 2 * 512], F32),
        ("krg_t", [128, 16 * 264], F32),
        ("acat_t", [128, 16 * 128], F32),
        ("bcat_t", [128, HID], F32),
        ("trimask", [128, 128], BF16),
        ("ones2c", [128, 2], BF16),
        ("e8", [8, 128], BF16),
        ("identb", [128, 128], BF16),
        ("cosq", [128, 1024], F32),
        ("sinq", [128, 1024], F32),
        ("cosk", [128, NT * 32], F32),
        ("sink", [128, NT * 32], F32),
        ("flag", [128, 1], F32),
    ]:
        D[name] = nc.dram_tensor(name, shape, dt, kind="ExternalInput").ap()
    out_d = nc.dram_tensor("out", [NOWN * 128, HID], F32, kind="ExternalOutput").ap()

    OPS = {"t": [], "v": [], "s": [], "g": [], "y": []}
    CNT = {}
    SEMS = {}

    def I(sem, n=1):
        CNT[sem] = CNT.get(sem, 0) + n
        return CNT[sem]

    def N(sem):
        return CNT.get(sem, 0)

    def W(eng, sem, tgt):
        if tgt and tgt > 0:
            OPS[eng].append(lambda e, s=sem, t=tgt: e.wait_ge(SEMS[s], t))

    def OP(eng, fn, inc=None, incn=1):
        if inc is None:
            OPS[eng].append(fn)
            return None
        v = I(inc, incn)
        OPS[eng].append(lambda e, f=fn, s=inc, n=incn: f(e).then_inc(SEMS[s], n))
        return v

    def DR(eng):
        OPS[eng].append(lambda e: e.drain())

    def DBG(name, tensor, waits):
        if name not in dbg:
            return
        od = nc.dram_tensor("dbg_" + name, list(tensor.shape), tensor.dtype,
                            kind="ExternalOutput").ap()
        for eng_sem, tgt in waits:
            W("y", eng_sem, tgt)
        OP("y", lambda e, o=od, t2=tensor: e.dma_start(out=o[:], in_=t2[:]),
           inc="sDbg", incn=16)

    ctx = ExitStack()

    def alloc(st, name, shape, dt, side=None):
        if side:
            return st.enter_context(nc.sbuf_tensor(name, shape, dt, side=side))
        return st.enter_context(nc.sbuf_tensor(name, shape, dt))

    acat_sb = alloc(ctx, "acat_sb", [128, 16 * 128], BF16)
    bcat_sb = alloc(ctx, "bcat_sb", [128, HID], BF16)
    cosq_sb = alloc(ctx, "cosq_sb", [128, 1024], BF16)
    sinq_sb = alloc(ctx, "sinq_sb", [128, 1024], BF16)
    trim_sb = alloc(ctx, "trim_sb", [128, 128], BF16)
    ones2c_sb = alloc(ctx, "ones2c_sb", [128, 2], BF16)
    e8_sb = alloc(ctx, "e8_sb", [8, 128], BF16)
    identb_sb = alloc(ctx, "identb_sb", [128, 128], BF16)
    flag_sb = alloc(ctx, "flag_sb", [128, 1], F32)
    xn_odd = alloc(ctx, "xn_odd", [128, NHC * 1024], BF16)
    logits_sb = alloc(ctx, "logits_sb", [128, 64], F32)
    kmax8_sb = alloc(ctx, "kmax8_sb", [8, 1], F32)
    qmax8_sb = alloc(ctx, "qmax8_sb", [8, 1], F32)
    mtmp8_sb = alloc(ctx, "mtmp8_sb", [8, 1], F32)
    mtmp8b_sb = alloc(ctx, "mtmp8b_sb", [8, 1], BF16)
    negmrow_sb = alloc(ctx, "negmrow_sb", [1, 8], F32)
    negm_sb = alloc(ctx, "negm_sb", [128, 8], F32)
    kmax8b_sb = alloc(ctx, "kmax8b_sb", [8, 1], BF16)
    kmaxrow_sb = alloc(ctx, "kmaxrow_sb", [1, 8], F32)
    u_sb = alloc(ctx, "u_sb", [128, 1024], BF16)
    t_sb = alloc(ctx, "t_sb", [128, 1024], BF16)
    cwb_sb = alloc(ctx, "cwb_sb", [128, 64], BF16)
    cwf_sb = alloc(ctx, "cwf_sb", [8, 128], BF16)
    gs1 = alloc(ctx, "gs1", [128, 8], F32)
    gs2 = alloc(ctx, "gs2", [128, 8], F32)
    gm1 = alloc(ctx, "gm1", [128, 1], F32)
    gm1n = alloc(ctx, "gm1n", [128, 1], F32)
    gm2 = alloc(ctx, "gm2", [128, 1], F32)
    gden = alloc(ctx, "gden", [128, 1], F32)
    qacc = alloc(ctx, "qacc", [2, 8], F32)
    qnt = alloc(ctx, "qnt", [2, 1], F32)
    mtmpA = alloc(ctx, "mtmpA", [1, 1], F32)
    qpad = alloc(ctx, "qpad", [32, 32], F32)
    qpadT = alloc(ctx, "qpadT", [32, 32], F32)
    g_fm = alloc(ctx, "g_fm", [128, 16 * 1024], BF16)

    p12 = ExitStack()
    p01 = ExitStack()
    Kp_sb = alloc(p12, "Kp_sb", [128, 4 * S], BF16, side="right")
    V_sb = alloc(p12, "V_sb", [128, NKV * NT * 65], BF16, side="right")
    qkv_sb = alloc(p12, "qkv_sb", [128, QKV_PITCH], BF16, side="right")
    down_sb = alloc(p01, "down_sb", [128, 16 * 256], BF16, side="right")
    upk_sb = alloc(p01, "upk_sb", [128, 2 * 256], BF16, side="right")
    upv_sb = alloc(p01, "upv_sb", [128, 2 * 512], BF16, side="right")
    krg_sb = alloc(p01, "krg_sb", [128, 16 * 264], BF16, side="right")
    cosk_sb = alloc(p01, "cosk_sb", [128, NT * 32], F32, side="right")
    sink_sb = alloc(p01, "sink_sb", [128, NT * 32], F32, side="right")

    sem_names = ["sWsm", "sTbl", "sXa", "sXb", "sQa", "sQb", "sFa", "sFb",
                 "sOWa", "sOWb", "sXFa", "sXFb", "sXFc", "sXFd", "sOa", "sOb",
                 "sKTa", "sKTb", "sDbg", "sT", "sV", "sS", "sG"]
    for n_, h_ in zip(sem_names, [ctx.enter_context(nc.semaphore(n_)) for n_ in sem_names]):
        SEMS[n_] = h_

    OP("g", lambda e: e.load_library(_ATTN_LIB))
    for dst, src in [(down_sb, "down_t"), (upk_sb, "upk_t"), (upv_sb, "upv_t"),
                     (krg_sb, "krg_t"), (acat_sb, "acat_t"), (bcat_sb, "bcat_t")]:
        OP("g", lambda e, d=dst, s=src: e.dma_start(out=d[:], in_=D[s]), inc="sWsm", incn=16)
    for dst, src in [(trim_sb, "trimask"), (ones2c_sb, "ones2c"),
                     (e8_sb, "e8"), (identb_sb, "identb"), (flag_sb, "flag"),
                     (cosk_sb, "cosk"), (sink_sb, "sink")]:
        OP("y", lambda e, d=dst, s=src: e.dma_start(out=d[:], in_=D[s]), inc="sTbl", incn=16)
    for dst, src in [(cosq_sb, "cosq"), (sinq_sb, "sinq")]:
        OP("g", lambda e, d=dst, s=src: e.dma_start(out=d[:], in_=D[s]), inc="sWsm", incn=16)
    w_smallw = N("sWsm")
    w_tables = N("sTbl")

    p1 = ExitStack()
    xbuf = [alloc(p1, f"xbuf{i}", [128, HID], BF16, side="right") for i in range(2)]
    xn_grp = alloc(p1, "xn_grp", [128, NHC * 512], BF16, side="right")
    kvK = [alloc(p1, f"kvK{t}", [128, 512], BF16, side="right") for t in range(4)]
    ckv_sb = alloc(p1, "ckv_sb", [128, 1024], BF16, side="right")
    krt1 = alloc(p1, "krt1", [128, 256], BF16, side="right")
    krt2 = alloc(p1, "krt2", [128, 256], BF16, side="right")
    sqb = [alloc(p1, f"sqb{i}", [128, HID], BF16, side="right") for i in range(3)]
    stat1 = alloc(p1, "stat1", [128, 24], F32, side="right")
    stat2 = alloc(p1, "stat2", [128, 2], F32, side="right")
    stat3 = alloc(p1, "stat3", [128, 1], F32, side="right")
    stat4 = alloc(p1, "stat4", [128, 1], F32, side="right")
    ksq_sb = alloc(p1, "ksq_sb", [128, 512], BF16, side="right")
    kn_acc = alloc(p1, "kn_acc", [128, 8], BF16, side="right")
    kntmp = alloc(p1, "kntmp", [128, 8], F32, side="right")

    psum1 = ExitStack()
    psT = [psum1.enter_context(nc.psum_tensor(f"psT{i}", [128, 1024], BF16)) for i in range(4)]
    psKG = [psum1.enter_context(nc.psum_tensor(f"psKG{i}", [128, 264], F32)) for i in range(2)]
    psC = psum1.enter_context(nc.psum_tensor("psC", [128, 512], F32))
    psD = psum1.enter_context(nc.psum_tensor("psD", [128, 512], F32))

    OP("v", lambda e: e.memset(
        AP(V_sb, 64, [[NKV * NT * 65, 128], [65, NKV * NT], [1, 1]]), 1.0))
    OP("v", lambda e: e.memset(qacc[:], 0.0))
    OP("v", lambda e: e.memset(qpad[:], 0.0))
    OP("v", lambda e: e.memset(kn_acc[:], 0.0), inc="sV")
    v_ones = N("sV")

    qproj_done = [0] * 16
    qevac_done = [0] * 16
    rope_done_q = [0] * 16
    v_sq = [0] * 16
    t_qnorm = [0] * 16
    v_qred = [0] * 16
    qw_ld = [0] * 16
    mh_ready = [0] * NKV
    scores_done = [0] * NKV

    ln_done = [0] * NT
    vstats_done = [0] * NT
    stat2_free = [0] * NT
    xn_done = [0] * NT
    xnT_done = [0] * NT
    rope_done = [0] * NT
    psT_free = [0, 0, 0, 0]
    psKG_free = [0, 0]
    kvk_done = [0] * 4
    kvKT_done = [None, None, None, None]
    vav_done = [0] * NT
    K_done = [0] * NT
    t_upkv_last = [0]
    t_down_last = [0]
    s_ckv_box = [0]

    def tile_of(s_):
        g_, t_ = divmod(s_, 4)
        if t_ % 2 == 1:
            return ("odd", 2 * g_ + (t_ // 2))
        return ("grp", t_ // 2)

    def xn_stat_ap(s_, hc):
        kind, idx = tile_of(s_)
        if kind == "odd":
            return xn_odd[:, hc * 1024 + idx * 128: hc * 1024 + idx * 128 + 128]
        return xn_grp[:, hc * 512 + idx * 128: hc * 512 + idx * 128 + 128]

    W("t", "sTbl", w_tables)
    W("t", "sWsm", w_smallw)

    def p1_head_front(s_):
        g, t = divmod(s_, 4)
        sx = ["sXa", "sXb"][s_ % 2]
        if s_ >= 2:
            W("g", "sV", vstats_done[s_ - 2])
            W("g", "sS", ln_done[s_ - 2])
        OP("g", lambda e, d=xbuf[s_ % 2], s2=s_: e.dma_start(
            out=d[:], in_=D["x_kv"][s2 * 128:(s2 + 1) * 128, :]), inc=sx, incn=16)
        w_x = N(sx)

        W("v", sx, w_x)
        xb = xbuf[s_ % 2]
        for i4 in range(4):
            OP("v", lambda e, x=xb, i4=i4: e.bn_stats(
                stat1[:, i4 * 6:(i4 + 1) * 6], x[:, i4 * 512:(i4 + 1) * 512]))
        DR("v")
        if s_ >= 1:
            W("v", "sS", stat2_free[s_ - 1])
        OP("v", lambda e: e.bn_aggr(stat2[:], stat1[:]))
        DR("v")
        OP("v", lambda e: e.tensor_scalar_add(stat2[:, 1:2], stat2[:, 1:2], EPS),
           inc="sV")
        vstats_done[s_] = N("sV")
        W("s", "sV", vstats_done[s_])
        if s_ >= 3:
            W("s", "sT", xnT_done[s_ - 3])
        OP("s", lambda e: e.activation(out=stat3[:], in_=stat2[:, 1:2], func=AF.Ln))
        DR("s")
        OP("s", lambda e: e.activation(out=stat3[:], in_=stat3[:], func=AF.Exp,
                                       scale=-0.5))
        DR("s")
        OP("s", lambda e: e.activation(out=stat4[:], in_=stat2[:, 0:1],
                                       func=AF.Copy, scale=stat3[:, 0:1]), inc="sS")
        stat2_free[s_] = N("sS")
        DR("s")
        OP("s", lambda e: e.activation(out=stat4[:], in_=stat4[:],
                                       func=AF.Copy, scale=-1.0))
        DR("s")
        OP("s", lambda e, x=xb, s2=s_: e.activation(
            out=sqb[s2 % 3][:], in_=x[:], func=AF.Identity,
            scale=stat3[:, 0:1], bias=stat4[:, 0:1]), inc="sS")
        ln_done[s_] = N("sS")
        DR("s")

    def p1_head_tr(s_):
        g, t = divmod(s_, 4)
        W("t", "sS", ln_done[s_])
        kind, idx = tile_of(s_)
        t_tr = 0
        for batch in range(2):
            slot = (s_ % 2) * 2 + batch
            W("t", "sS", psT_free[slot])
            for k in range(8):
                hc = batch * 8 + k
                OP("t", lambda e, hc=hc, b=slot, s2=s_: e.transpose(
                    psT[b][:, (hc % 8) * 128:(hc % 8) * 128 + 128],
                    sqb[s2 % 3][:, hc * 128:hc * 128 + 128], identb_sb[:]),
                   inc=("sT" if k == 7 else None))
            t_tr = N("sT")
            W("s", "sT", t_tr)
            if kind == "grp" and g >= 1:
                W("s", "sT", t_down_last[0])
            if kind == "odd":
                dst = AP(xn_odd, (batch * 8) * 1024 + idx * 128,
                         [[NHC * 1024, 128], [1024, 8], [1, 128]])
            else:
                dst = AP(xn_grp, (batch * 8) * 512 + idx * 128,
                         [[NHC * 512, 128], [512, 8], [1, 128]])
            OP("s", lambda e, d=dst, b=slot: e.copy(out=d, in_=psT[b][:, 0:1024]),
               inc="sS")
            psT_free[slot] = N("sS")
        xn_done[s_] = N("sS")
        xnT_done[s_] = t_tr

    v_cw_done = [0] * NOWN

    def emit_gate_softmax(j):
        lg = logits_sb[:, j * 8:j * 8 + 8]
        DR("v")
        OP("v", lambda e, l=lg: e.reduce_max(gm1[:], l, axis=AX.X))
        OP("v", lambda e, l=lg: e.reduce_max(gm1n[:], l, axis=AX.X, negate=True))
        DR("v")
        OP("v", lambda e, l=lg: e.tensor_scalar(out=gs1[:], in0=l, scalar1=gm1[:],
                                                scalar2=None, op0=ALU.is_ge))
        DR("v")
        OP("v", lambda e, l=lg: e.scalar_tensor_tensor(
            out=gs2[:], in0=gs1[:], scalar=NEG_BIG, in1=l, op0=ALU.mult, op1=ALU.add))
        DR("v")
        OP("v", lambda e: e.reduce_max(gm2[:], gs2[:], axis=AX.X))
        DR("v")
        OP("v", lambda e, l=lg: e.tensor_scalar(out=gs2[:], in0=l, scalar1=gm2[:],
                                                scalar2=None, op0=ALU.is_ge), inc="sV")
        v_g1 = N("sV")
        W("s", "sV", v_g1)
        OP("s", lambda e, l=lg: e.activation(out=gs1[:], in_=l, func=AF.Exp,
                                             bias=gm1n[:]), inc="sS")
        s_g = N("sS")
        W("v", "sS", s_g)
        OP("v", lambda e: e.tensor_mul(gs1[:], gs1[:], gs2[:]))
        DR("v")
        OP("v", lambda e: e.reduce_sum(gden[:], gs1[:], axis=AX.X))
        DR("v")
        OP("v", lambda e: e.reciprocal(gden[:], gden[:]))
        DR("v")
        OP("v", lambda e, j=j: e.tensor_scalar(
            out=cwb_sb[:, j * 8:j * 8 + 8], in0=gs1[:], scalar1=gden[:],
            scalar2=16.0, op0=ALU.mult, op1=ALU.mult), inc="sV")
        v_cw_done[j] = N("sV")

    def p1_krgate_rope(s_):
        g, t = divmod(s_, 4)
        kind, idx = tile_of(s_)
        W("t", "sS", xn_done[s_])
        W("t", "sV", psKG_free[s_ % 2])
        for hc in range(NHC):
            OP("t", lambda e, hc=hc, s2=s_: e.matmul(
                psKG[s2 % 2][:, 0:264], xn_stat_ap(s2, hc),
                krg_sb[:, hc * 264:hc * 264 + 264],
                start=(hc == 0), stop=(hc == NHC - 1)),
               inc=("sT" if hc == NHC - 1 else None))
        krg_done = N("sT")

        W("v", "sT", krg_done)
        W("v", "sTbl", w_tables)
        if kvKT_done[t]:
            W("v", kvKT_done[t][0], kvKT_done[t][1])
        pk = psKG[s_ % 2]
        cosrep = AP(cosk_sb, s_ * 32, [[NT * 32, 128], [0, 8], [1, 32]])
        sinrep = AP(sink_sb, s_ * 32, [[NT * 32, 128], [0, 8], [1, 32]])
        OP("v", lambda e, p=pk, c=cosrep: e.tensor_mul(krt1[:], p[:, 0:256], c))
        shuf = AP(pk, 16, [[264, 128], [32, 8], [-16, 2], [1, 16]])
        OP("v", lambda e, sh=shuf, sr=sinrep: e.tensor_mul(krt2[:], sh, sr))
        DR("v")
        kdst = AP(kvK[t], 0, [[512, 128], [64, 8], [32, 2], [1, 16]])
        OP("v", lambda e, d=kdst: e.tensor_add(d, krt1[:], krt2[:]))
        if kind == "odd":
            OP("v", lambda e, p=pk, j=idx: e.tensor_copy(
                logits_sb[:, j * 8:j * 8 + 8], p[:, 256:264]))
        OP("v", lambda e: e.engine_nop(), inc="sV")
        rope_done[s_] = N("sV")
        psKG_free[s_ % 2] = rope_done[s_]
        if kind == "odd":
            emit_gate_softmax(idx)

    def p1_down(g):
        for s_ in range(4 * g, 4 * g + 4):
            W("t", "sS", xn_done[s_])
        if g >= 1:
            W("t", "sV", vav_done[4 * g - 1])
        for cc in range(2):
            tgt = psC if cc == 0 else psD
            for hc in range(NHC):
                stat = down_sb[:, hc * 256 + cc * 128: hc * 256 + cc * 128 + 128]
                mov_o = xn_odd[:, hc * 1024 + 2 * g * 128: hc * 1024 + 2 * g * 128 + 256]
                mov_g = xn_grp[:, hc * 512: hc * 512 + 256]
                OP("t", lambda e, p=tgt, st=stat, m=mov_o, hc=hc: e.matmul(
                    p[:, 0:256], st, m, start=(hc == 0), stop=False))
                OP("t", lambda e, p=tgt, st=stat, m=mov_g, hc=hc: e.matmul(
                    p[:, 256:512], st, m, start=False, stop=(hc == NHC - 1)),
                   inc=("sT" if hc == NHC - 1 else None))
        t_down_last[0] = N("sT")

        W("s", "sT", t_down_last[0])
        if g >= 1:
            W("s", "sT", t_upkv_last[0])
        OP("s", lambda e: e.copy(out=ckv_sb[:, 0:512], in_=psC[:]))
        DR("s")
        OP("s", lambda e: e.copy(out=ckv_sb[:, 512:1024], in_=psD[:]), inc="sS")
        s_ckv_box[0] = N("sS")

    def ckv_col(t_):
        return (t_ // 2) * 128 if t_ % 2 == 1 else 256 + (t_ // 2) * 128

    def p1_upkv(g, t):
        s_ = 4 * g + t
        pk_ = psKG[t % 2]
        pv_ = psC if t % 2 == 0 else psD
        W("t", "sS", s_ckv_box[0])
        W("t", "sV", rope_done[s_])
        if 4 * g + t + 2 < NT:
            W("t", "sV", rope_done[4 * g + t + 2])
        if t >= 2:
            W("t", "sV", vav_done[4 * g + t - 2])
        for cc in range(2):
            stat = ckv_sb[:, cc * 512 + ckv_col(t): cc * 512 + ckv_col(t) + 128]
            OP("t", lambda e, p=pk_, st=stat, cc=cc: e.matmul(
                p[:, 0:256], st, upk_sb[:, cc * 256:cc * 256 + 256],
                start=(cc == 0), stop=(cc == 1)))
            OP("t", lambda e, p=pv_, st=stat, cc=cc: e.matmul(
                p[:, 0:512], st, upv_sb[:, cc * 512:cc * 512 + 512],
                start=(cc == 0), stop=(cc == 1)),
               inc=("sT" if cc == 1 else None))
        t_upkv_last[0] = N("sT")

        W("v", "sT", t_upkv_last[0])
        W("v", "sV", v_ones)
        kcdst = AP(kvK[t], 16, [[512, 128], [64, 8], [32, 2], [1, 16]])
        OP("v", lambda e, d=kcdst, p=pk_: e.tensor_copy(d, p[:, 0:256]), inc="sV")
        kvk_done[t] = N("sV")
        psKG_free[t % 2] = kvk_done[t]
        DR("v")
        OP("v", lambda e, t2=t: e.tensor_mul(ksq_sb[:], kvK[t2][:], kvK[t2][:]))
        DR("v")
        OP("v", lambda e: e.reduce_sum(
            kntmp[:], AP(ksq_sb, 0, [[512, 128], [64, 8], [1, 64]]), axis=AX.X))
        DR("v")
        OP("v", lambda e: e.tensor_max(kn_acc[:], kn_acc[:], kntmp[:]))
        DR("v")
        for h in range(NKV):
            OP("v", lambda e, p=pv_, h=h, s2=s_: e.tensor_copy(
                V_sb[:, (h * NT + s2) * 65:(h * NT + s2) * 65 + 64],
                p[:, h * 64:h * 64 + 64]),
               inc=("sV" if h == NKV - 1 else None))
        vav_done[s_] = N("sV")

        se_ = "sKTa" if t % 2 == 0 else "sKTb"
        W("y", "sV", kvk_done[t])
        for p in range(4):
            OP("y", lambda e, p=p, t2=t, s2=s_: e.dma_start_transpose(
                out=Kp_sb[:, p * S + s2 * 128: p * S + s2 * 128 + 128],
                in_=kvK[t2][:, p * 128:p * 128 + 128]),
               inc=se_, incn=16)
        kvKT_done[t] = (se_, N(se_))
        K_done[s_] = kvKT_done[t]

    for i in range(21):
        if i < 16:
            p1_head_front(i)
        if 1 <= i <= 16:
            p1_head_tr(i - 1)
        if i >= 5 and (i - 5) // 4 <= 3:
            p1_upkv((i - 5) // 4, (i - 5) % 4)
        if i >= 4 and (i - 4) % 4 == 0 and i <= 16:
            p1_down((i - 4) // 4)
        if 2 <= i <= 17:
            p1_krgate_rope(i - 2)

    W("t", "sV", vav_done[NT - 1])
    W("t", "sS", psT_free[0])
    OP("t", lambda e: e.transpose(psT[0][0:8, 0:128], kn_acc[:], identb_sb[:]), inc="sT")
    t_kn = N("sT")
    W("v", "sT", t_kn)
    OP("v", lambda e: e.reduce_max(kmax8_sb[:], psT[0][0:8, 0:128], axis=AX.X))
    DR("v")
    OP("v", lambda e: e.tensor_copy(kmax8b_sb[:], kmax8_sb[:]), inc="sV")
    v_nk = N("sV")
    W("t", "sV", v_nk)
    OP("t", lambda e: e.transpose(psT[0][0:1, 256:264], kmax8b_sb[:],
                                  identb_sb[0:8, 0:8]), inc="sT")
    t_kr = N("sT")
    W("v", "sT", t_kr)
    OP("v", lambda e: e.tensor_copy(kmaxrow_sb[:], psT[0][0:1, 256:264]), inc="sV")
    v_nkr = N("sV")

    W("t", "sV", vav_done[NT - 1])
    W("t", "sS", xn_done[NT - 1])
    for g2 in range(2):
        tgtA = psC if g2 == 0 else psD
        for hc in range(NHC):
            stat = acat_sb[:, hc * 128:hc * 128 + 128]
            mov = xn_odd[:, hc * 1024 + g2 * 512: hc * 1024 + g2 * 512 + 512]
            OP("t", lambda e, p=tgtA, st=stat, m=mov, hc=hc: e.matmul(
                p[:, 0:512], st, m, start=(hc == 0), stop=(hc == NHC - 1)),
               inc=("sT" if hc == NHC - 1 else None))
    t_u = N("sT")
    W("s", "sT", t_u)
    OP("s", lambda e: e.copy(out=u_sb[:, 0:512], in_=psC[:]))
    DR("s")
    OP("s", lambda e: e.copy(out=u_sb[:, 512:1024], in_=psD[:]), inc="sS")
    s_u = N("sS")

    QWPRE_COL = {0: 14336, 1: 12288}
    qraw_pre = [g_fm[:, 10240 + 1024 * qc: 10240 + 1024 * qc + 1024] for qc in range(2)]
    sqq_pre = [g_fm[:, 8192 + 1024 * qc: 8192 + 1024 * qc + 1024] for qc in range(2)]
    qpart_pre = g_fm[:, 7168:8192]
    for qc in range(2):
        sq_ = "sQa" if qc == 0 else "sQb"
        OP("g", lambda e, qc=qc: e.dma_start(
            out=g_fm[:, QWPRE_COL[qc]:QWPRE_COL[qc] + 2048],
            in_=D["qw_s"][:, qc * 2048:(qc + 1) * 2048]), inc=sq_, incn=16)
        qw_ld[qc] = N(sq_)

    pre_evac = [[0, 0], [0, 0]]
    v_rope_pre = [0, 0]

    def pre_qp(qc, g2):
        sq_ = "sQa" if qc == 0 else "sQb"
        tgt = psC if g2 == 0 else psD
        W("t", sq_, qw_ld[qc])
        if qc == 0 and g2 == 0:
            W("t", "sS", xn_done[NT - 1])
        W("t", "sS", s_u)
        if qc == 1:
            W("t", "sS", pre_evac[0][g2])
        for hc in range(NHC):
            stat = g_fm[:, QWPRE_COL[qc] + hc * 128: QWPRE_COL[qc] + hc * 128 + 128]
            mov = xn_odd[:, hc * 1024 + g2 * 512: hc * 1024 + g2 * 512 + 512]
            OP("t", lambda e, p=tgt, st=stat, m=mov, hc=hc: e.matmul(
                p[:, 0:512], st, m, start=(hc == 0), stop=(hc == NHC - 1)),
               inc=("sT" if hc == NHC - 1 else None))
        qp_h = N("sT")
        if g2 == 1:
            qproj_done[qc] = qp_h
        W("s", "sT", qp_h)
        OP("s", lambda e, qc=qc, g2=g2, p=tgt: e.copy(
            out=qraw_pre[qc][:, g2 * 512:g2 * 512 + 512], in_=p[:, 0:512]),
           inc="sS")
        pre_evac[qc][g2] = N("sS")
        if g2 == 1:
            qevac_done[qc] = pre_evac[qc][1]

    def pre_shuffle_sq(qc):
        W("v", "sS", pre_evac[qc][1])
        for blk in range(4):
            OP("v", lambda e, qc=qc, b=blk: e.tensor_copy(
                qpart_pre[b * 32:b * 32 + 32, :],
                qraw_pre[qc][(b ^ 1) * 32:(b ^ 1) * 32 + 32, :]))
        OP("v", lambda e, qc=qc: e.tensor_mul(
            sqq_pre[qc][:], qraw_pre[qc][:], qraw_pre[qc][:]))
        DR("v")
        OP("v", lambda e, qc=qc: e.tensor_max(
            sqq_pre[qc][:, 0:512], sqq_pre[qc][:, 0:512], sqq_pre[qc][:, 512:1024]),
           inc="sV")
        v_sq[qc] = N("sV")
        DR("v")

    def pre_rope(qc):
        if qc == 0:
            W("v", "sWsm", w_smallw)
        OP("v", lambda e, qc=qc: e.tensor_mul(qraw_pre[qc][:], qraw_pre[qc][:],
                                              cosq_sb[:]))
        OP("v", lambda e: e.tensor_mul(qpart_pre[:], qpart_pre[:], sinq_sb[:]))
        DR("v")
        for half in range(2):
            gl = 2 * qc + half
            dst = qkv_sb[0:64, gl * 1024: gl * 1024 + 1024]
            OP("v", lambda e, qc=qc, d=dst, hf=half: e.tensor_add(
                d, qraw_pre[qc][hf * 64:hf * 64 + 64, :],
                qpart_pre[hf * 64:hf * 64 + 64, :]))
        DR("v")
        OP("v", lambda e: e.engine_nop(), inc="sV")
        rope_done_q[qc] = N("sV")

    pre_chunks = [
        lambda: pre_qp(0, 0), lambda: pre_qp(0, 1),
        lambda: pre_qp(1, 0), lambda: pre_qp(1, 1),
        lambda: pre_shuffle_sq(0), lambda: pre_rope(0),
        lambda: pre_shuffle_sq(1), lambda: pre_rope(1),
    ]

    tsc_done = [0] * NOWN
    for j in range(NOWN):
        if j < len(pre_chunks):
            pre_chunks[j]()
        W("t", "sV", v_cw_done[j])
        if j == 0:
            W("t", "sV", v_nkr)
        OP("t", lambda e, j=j: e.transpose(
            psT[0][0:8, 128:256], cwb_sb[:, j * 8:j * 8 + 8], identb_sb[:]),
           inc="sT")
        t_cwT = N("sT")
        W("v", "sT", t_cwT)
        OP("v", lambda e: e.tensor_copy(cwf_sb[:], psT[0][0:8, 128:256]), inc="sV")
        v_cwf = N("sV")
        W("t", "sV", v_cwf)
        if j == 0:
            W("t", "sV", kvk_done[3])
        OP("t", lambda e: e.matmul(psKG[0][:, 0:128], e8_sb[:], cwf_sb[:],
                                   start=True, stop=True), inc="sT")
        t_e8 = N("sT")
        W("v", "sT", t_e8)
        W("v", "sS", s_u)
        OP("v", lambda e, j=j: e.tensor_mul(
            t_sb[:, j * 128:j * 128 + 128], u_sb[:, j * 128:j * 128 + 128],
            psKG[0][:, 0:128]), inc="sV")
        tsc_done[j] = N("sV")
    for ch in pre_chunks[NOWN:]:
        ch()

    for qc in range(2):
        tgt = psC if qc == 0 else psD
        W("t", "sV", v_sq[qc])
        W("t", "sS", pre_evac[1][qc])
        OP("t", lambda e, qc=qc, p=tgt: e.matmul(
            p[0:2, 0:512], ones2c_sb[:], sqq_pre[qc][:, 0:512],
            start=True, stop=True), inc="sT")
        t_qnorm[qc] = N("sT")
        W("v", "sT", t_qnorm[qc])
        OP("v", lambda e, p=tgt: e.reduce_max(qnt[:], p[0:2, 0:512], axis=AX.X))
        DR("v")
        OP("v", lambda e: e.tensor_max(qacc[:, 0:1], qacc[:, 0:1], qnt[:]))
        DR("v")
        OP("v", lambda e: e.engine_nop(), inc="sV")
        v_qred[qc] = N("sV")
    OP("v", lambda e: e.tensor_copy(qpad[0:2, 0:1], qacc[:, 0:1]))
    DR("v")
    OP("v", lambda e: e.transpose(qpadT[:], qpad[:]))
    DR("v")
    OP("v", lambda e: e.reduce_max(mtmpA[:], qpadT[0:1, 0:2], axis=AX.X))
    DR("v")
    OP("v", lambda e: e.tensor_mul(mtmpA[:], mtmpA[:], kmaxrow_sb[0:1, 0:1]),
       inc="sV")
    v_m1 = N("sV")
    W("s", "sV", v_m1)
    OP("s", lambda e: e.activation(out=mtmpA[:], in_=mtmpA[:], func=AF.Ln))
    DR("s")
    OP("s", lambda e: e.activation(out=mtmpA[:], in_=mtmpA[:], func=AF.Exp,
                                   scale=0.5), inc="sS")
    s_m0 = N("sS")
    W("v", "sS", s_m0)
    OP("v", lambda e: e.tensor_scalar_mul(mtmpA[:], mtmpA[:], -0.1259765625),
       inc="sV")
    v_m2 = N("sV")
    W("g", "sV", v_m2)
    OP("g", lambda e: e.partition_broadcast(negm_sb[:, 0:1], mtmpA[0:1, 0:1]),
       inc="sG")
    mh_ready[0] = N("sG")

    DBG("xn_odd", xn_odd, [("sV", xn_done[NT - 1])])
    DBG("Kp", Kp_sb, [K_done[NT - 1]])
    DBG("V", V_sb, [("sV", vav_done[NT - 1])])
    DBG("logits", logits_sb, [("sV", rope_done[NT - 1])])

    kta_total = N("sKTa")
    ktb_total = N("sKTb")
    p1_t_end = N("sT")
    p1_v_end = N("sV")
    p1_s_end = N("sS")
    psum1.close()
    p1.close()
    p01.close()

    p2 = ExitStack()
    PT = [alloc(p2, f"PT{i}", [128, 1024], BF16) for i in range(4)]
    attn_fm = alloc(p2, "attn_fm", [128, 16 * 1024], BF16)
    rlbc_sb = alloc(p2, "rlbc_sb", [64, 512], F32)
    lpad = alloc(p2, "lpad", [32, 512], F32)
    lT = alloc(p2, "lT", [32, 512], F32)
    rT = alloc(p2, "rT", [32, 512], F32)
    rrow = alloc(p2, "rrow", [32, 512], F32)
    fwbuf = [alloc(p12, f"fwbuf{i}", [128, 2048], BF16, side="right") for i in range(2)]
    qwbuf = [alloc(p12, f"qwbuf{i}", [128, 2048], BF16, side="right") for i in range(2)]
    sqq = [alloc(p12, f"sqq{i}", [128, 1024], BF16, side="right") for i in range(2)]
    qraw = [alloc(p12, f"qraw{i}", [128, 1024], BF16, side="right") for i in range(2)]
    qpart = alloc(p2, "qpart", [128, 1024], BF16)

    QR_PITCH = 8192

    psum2 = ExitStack()
    psS_ = [psum2.enter_context(nc.psum_tensor(f"psS{i}", [128, 1024], F32)) for i in range(2)]
    psAV = [psum2.enter_context(nc.psum_tensor(f"psAV{i}", [65, 512], F32))
            for i in range(2)]
    psQP = psum2.enter_context(nc.psum_tensor("psQP", [128, 512], F32))
    psDen = psum2.enter_context(nc.psum_tensor("psDen", [128, 512], F32))

    OP("v", lambda e: e.memset(lpad[:], 1.0))
    OP("v", lambda e: e.memset(rT[:], 1.0), inc="sV")
    v_rlinit = N("sV")

    psQP_vfree = [0]

    def emit_qw_dma(qc):
        sq_ = "sQa" if qc % 2 == 0 else "sQb"
        if qc < 2:
            W("g", "sT", p1_t_end)
            W("g", "sV", N("sV"))
            W("g", "sS", N("sS"))
        else:
            W("g", "sT", qproj_done[qc - 2])
        OP("g", lambda e, qc=qc: e.dma_start(
            out=qwbuf[qc % 2][:], in_=D["qw_s"][:, qc * 2048:(qc + 1) * 2048]),
           inc=sq_, incn=16)
        qw_ld[qc] = N(sq_)

    def gen_qpair(qc):
        h = qc // 2
        sq_ = "sQa" if qc % 2 == 0 else "sQb"
        qevac_half = [0, 0]
        for g2 in range(2):
            for u in range(4):
                if u == 0:
                    if g2 == 0:
                        W("t", sq_, qw_ld[qc])
                        if qc == 0:
                            W("t", "sT", p1_t_end)
                        if qc >= 1:
                            W("t", "sS", qevac_done[qc - 1])
                        W("t", "sV", psQP_vfree[0])
                    else:
                        W("t", "sS", qevac_half[0])
                for hc in range(4 * u, 4 * u + 4):
                    stat = qwbuf[qc % 2][:, hc * 128:hc * 128 + 128]
                    mov = xn_odd[:, hc * 1024 + g2 * 512: hc * 1024 + g2 * 512 + 512]
                    OP("t", lambda e, st=stat, m=mov, hc=hc: e.matmul(
                        psQP[:, 0:512], st, m,
                        start=(hc == 0), stop=(hc == NHC - 1)),
                       inc=("sT" if hc == NHC - 1 else None))
                if u == 3:
                    qp_half = N("sT")
                    if g2 == 1:
                        qproj_done[qc] = qp_half
                    W("s", "sT", qp_half)
                    if g2 == 0 and qc >= 2:
                        W("s", "sV", rope_done_q[qc - 2])
                    OP("s", lambda e, qc=qc, g2=g2: e.copy(
                        out=qraw[qc % 2][:, g2 * 512:g2 * 512 + 512],
                        in_=psQP[:, 0:512]), inc="sS")
                    qevac_half[g2] = N("sS")
                yield
        qevac_done[qc] = qevac_half[1]

        W("v", "sS", qevac_done[qc])
        if qc >= 2:
            W("v", "sT", t_qnorm[qc - 2])
        if h >= 4:
            W("v", "sT", scores_done[h - 4])
        for blk in range(4):
            OP("v", lambda e, qc=qc, b=blk: e.tensor_copy(
                qpart[b * 32:b * 32 + 32, :],
                qraw[qc % 2][(b ^ 1) * 32:(b ^ 1) * 32 + 32, :]))
        yield
        OP("v", lambda e, qc=qc: e.tensor_mul(
            sqq[qc % 2][:], qraw[qc % 2][:], qraw[qc % 2][:]))
        DR("v")
        OP("v", lambda e, qc=qc: e.tensor_max(
            sqq[qc % 2][:, 0:512], sqq[qc % 2][:, 0:512], sqq[qc % 2][:, 512:1024]),
           inc="sV")
        v_sq[qc] = N("sV")
        DR("v")
        yield
        OP("v", lambda e, qc=qc: e.tensor_mul(qraw[qc % 2][:], qraw[qc % 2][:],
                                              cosq_sb[:]))
        OP("v", lambda e: e.tensor_mul(qpart[:], qpart[:], sinq_sb[:]))
        DR("v")
        rofs_q = (h % 2) * 64
        colb = ((h % 4) // 2) * 4096
        for half in range(2):
            gl = 2 * (qc % 2) + half
            dst = qkv_sb[rofs_q:rofs_q + 64, colb + gl * 1024: colb + gl * 1024 + 1024]
            OP("v", lambda e, qc=qc, d=dst, hf=half: e.tensor_add(
                d, qraw[qc % 2][hf * 64:hf * 64 + 64, :],
                qpart[hf * 64:hf * 64 + 64, :]))
        DR("v")
        OP("v", lambda e: e.engine_nop(), inc="sV")
        rope_done_q[qc] = N("sV")
        yield

    def emit_norm(qc, evac_tgt):
        W("t", "sV", v_sq[qc])
        W("t", "sS", evac_tgt)
        W("t", "sV", psQP_vfree[0])
        OP("t", lambda e, qc=qc: e.matmul(
            psQP[0:2, 0:512], ones2c_sb[:], sqq[qc % 2][:, 0:512],
            start=True, stop=True), inc="sT")
        t_qnorm[qc] = N("sT")
        W("v", "sT", t_qnorm[qc])
        OP("v", lambda e: e.reduce_max(qnt[:], psQP[0:2, 0:512], axis=AX.X))
        DR("v")
        hcol = qc // 2
        OP("v", lambda e, hp=hcol: e.tensor_max(
            qacc[:, hp:hp + 1], qacc[:, hp:hp + 1], qnt[:]))
        DR("v")
        OP("v", lambda e: e.engine_nop(), inc="sV")
        v_qred = N("sV")
        psQP_vfree[0] = v_qred

    def emit_mhat(hh):
        DR("v")
        OP("v", lambda e, hh=hh: e.tensor_copy(qpad[0:2, 0:1], qacc[:, hh:hh + 1]))
        DR("v")
        OP("v", lambda e: e.transpose(qpadT[:], qpad[:]))
        DR("v")
        OP("v", lambda e: e.reduce_max(mtmpA[:], qpadT[0:1, 0:2], axis=AX.X))
        DR("v")
        OP("v", lambda e, hh=hh: e.tensor_mul(
            mtmpA[:], mtmpA[:], kmaxrow_sb[0:1, hh:hh + 1]), inc="sV")
        v_m1 = N("sV")
        W("s", "sV", v_m1)
        OP("s", lambda e: e.activation(out=mtmpA[:], in_=mtmpA[:], func=AF.Ln))
        DR("s")
        OP("s", lambda e: e.activation(out=mtmpA[:], in_=mtmpA[:], func=AF.Exp,
                                       scale=0.5), inc="sS")
        s_m = N("sS")
        W("v", "sS", s_m)
        OP("v", lambda e: e.tensor_scalar_mul(mtmpA[:], mtmpA[:], -0.1259765625),
           inc="sV")
        v_m2 = N("sV")
        W("g", "sV", v_m2)
        OP("g", lambda e, hh=hh: e.partition_broadcast(
            negm_sb[:, hh:hh + 1], mtmpA[0:1, 0:1]), inc="sG")
        mh_ready[hh] = N("sG")

    def gen_qblock(h):
        yield from gen_qpair(2 * h + 2)
        if 2 * h + 4 < 16:
            emit_qw_dma(2 * h + 4)
        yield from gen_qpair(2 * h + 3)
        if 2 * h + 5 < 16:
            emit_qw_dma(2 * h + 5)
        yield
        yield
        yield
        yield
        emit_norm(2 * h + 2, qevac_done[2 * h + 3])
        yield
        emit_norm(2 * h + 3, qevac_done[2 * h + 3])
        yield
        emit_mhat(h + 1)
        yield

    dense_done = [0] * 16
    evac_g = [None] * 16

    fw_ld = [0] * 16

    def emit_fw_dma(oc):
        sf = "sFa" if oc % 2 == 0 else "sFb"
        if oc >= 2:
            W("g", "sT", dense_done[oc - 2])
        else:
            W("g", "sT", p1_t_end)
            W("g", "sV", N("sV"))
            W("g", "sS", N("sS"))
        OP("g", lambda e, oc=oc: e.dma_start(
            out=fwbuf[oc % 2][:], in_=D["fw_s"][:, oc * 2048:(oc + 1) * 2048]),
           inc=sf, incn=16)
        fw_ld[oc] = N(sf)

    def dense_units():
        emit_fw_dma(0)
        emit_fw_dma(1)
        for oc in range(16):
            sf = "sFa" if oc % 2 == 0 else "sFb"
            v_half = [0, 0]
            for g2 in range(2):
                for u in range(8):
                    if u == 0:
                        if g2 == 0:
                            W("t", sf, fw_ld[oc])
                            if oc == 0:
                                W("t", "sV", tsc_done[NOWN - 1])
                            else:
                                W("t", evac_g[oc - 1][0], evac_g[oc - 1][1])
                        else:
                            W("t", v_half[0][0], v_half[0][1])
                    for hc in range(2 * u, 2 * u + 2):
                        stat = fwbuf[oc % 2][:, hc * 128:hc * 128 + 128]
                        mov = xn_odd[:, hc * 1024 + g2 * 512:
                                     hc * 1024 + g2 * 512 + 512]
                        OP("t", lambda e, st=stat, m=mov, hc=hc: e.matmul(
                            psDen[:, 0:512], st, m, start=(hc == 0), stop=False),
                           inc=("sT" if (hc == 15 and g2 == 1) else None))
                    yield
                if g2 == 1:
                    dense_done[oc] = N("sT")
                    if oc + 2 < 16:
                        emit_fw_dma(oc + 2)
                statB = bcat_sb[:, oc * 128:oc * 128 + 128]
                movB = t_sb[:, g2 * 512:g2 * 512 + 512]
                OP("t", lambda e, st=statB, m=movB: e.matmul(
                    psDen[:, 0:512], st, m, start=False, stop=True), inc="sT")
                t_b = N("sT")
                W("s", "sT", t_b)
                if oc in (8, 9):
                    W("s", "sT", t_qnorm[oc - 8])
                elif oc >= 12:
                    W("s", "sT", qproj_done[1])
                OP("s", lambda e, oc=oc, g2=g2: e.copy(
                    out=g_fm[:, oc * 1024 + g2 * 512: oc * 1024 + g2 * 512 + 512],
                    in_=psDen[:, 0:512]), inc="sS")
                v_half[g2] = ("sS", N("sS"))
                yield
                if g2 == 0:
                    yield
                    yield
            evac_g[oc] = v_half[1]

    dense_it = dense_units()
    dense_emitted = [0]
    DENSE_TOTAL = 16 * 20
    TOTAL_ITEMS = NKV * (NOWN * (NOWN + 1) // 2)

    exp_cnt_half = [0, 0]
    av_cnt_pt = [0, 0, 0, 0]
    mask_cnt = {}
    exp_item = {}
    evac_done = [0, 0]
    slot_par = {}
    av_last = {}
    first_item_of_head = [True] * NKV
    first_exp_of_head = [True] * NKV
    last_bc_box = [0]
    lpad_init_box = [v_rlinit]
    nslots = 0

    def emit_scores(i, it):
        h, j, sub = it
        p_pair = h // 2
        rofs = (h % 2) * 64
        half = i % 2
        W("t", "sS", exp_cnt_half[half])
        if first_item_of_head[h]:
            W("t", "sV", rope_done_q[2 * h + 1])
            if h == 0:
                W("t", "sKTa", kta_total)
                W("t", "sKTb", ktb_total)
            first_item_of_head[h] = False
        for ktl in range(2):
            kt = 2 * sub + ktl
            stat = Kp_sb[rofs:rofs + 64,
                         p_pair * S + kt * 128: p_pair * S + kt * 128 + 128]
            mov = AP(qkv_sb, (h % 2) * 64 * QR_PITCH + ((h % 4) // 2) * 4096 + j * 128,
                     [[QR_PITCH, 64], [1024, 4], [1, 128]])
            OP("t", lambda e, st=stat, m=mov, hf=half, k=ktl: e.matmul(
                psS_[hf][:, k * 512:k * 512 + 512], st, m, start=True, stop=True),
               inc=("sT" if ktl == 1 else None))
        t_sc = N("sT")
        scores_done[h] = t_sc
        W("s", "sT", t_sc)
        W("s", "sT", av_cnt_pt[i % 4])
        if first_exp_of_head[h]:
            W("s", "sG", mh_ready[h])
            first_exp_of_head[h] = False
        OP("s", lambda e, hf=half, pr=i % 4, h2=h: e.activation(
            out=PT[pr][:, 0:1024], in_=psS_[hf][:, 0:1024], func=AF.Exp,
            bias=negm_sb[:, h2:h2 + 1], scale=1.0), inc="sS")
        exp_cnt_half[half] = N("sS")
        exp_item[it] = exp_cnt_half[half]
        if sub == j:
            W("v", "sS", exp_item[it])
            OP("v", lambda e, pr=i % 4: e.tensor_scalar(
                out=PT[pr][:, 0:512], in0=PT[pr][:, 0:512],
                scalar1=flag_sb[:], scalar2=None, op0=ALU.mult))
            trirep = AP(trim_sb, 0, [[128, 128], [0, 4], [1, 128]])
            DR("v")
            OP("v", lambda e, pr=i % 4, tr=trirep: e.tensor_mul(
                PT[pr][:, 512:1024], PT[pr][:, 512:1024], tr), inc="sV")
            mask_cnt[(h, j)] = N("sV")

    def emit_av(i, it):
        h, j, sub = it
        L = 2 * j + 2
        par = slot_par[(h, j)]
        if sub == j:
            W("t", "sV", mask_cnt[(h, j)])
        else:
            W("t", "sS", exp_item[it])
        if sub == 0:
            W("t", "sV", evac_done[par])
        for ktl in range(2):
            kt = 2 * sub + ktl
            vstat = V_sb[:, (h * NT + kt) * 65:(h * NT + kt) * 65 + 65]
            OP("t", lambda e, vs=vstat, pr=i % 4, k=ktl, pa=par,
               st=(kt == 0), sp=(kt == L - 1): e.matmul(
                psAV[pa][:, 0:512], vs, PT[pr][:, k * 512:k * 512 + 512],
                start=st, stop=sp),
               inc=("sT" if ktl == 1 else None))
        av_cnt_pt[i % 4] = N("sT")

    def emit_slotend(h, j):
        par = slot_par[(h, j)]
        t_av = av_last[(h, j)]
        W("v", "sT", t_av)
        W("v", "sG", last_bc_box[0])
        W("v", "sV", lpad_init_box[0])
        OP("v", lambda e, pa=par: e.tensor_copy(lpad[0:1, 0:512],
                                                psAV[pa][64:65, 0:512]))
        DR("v")
        OP("v", lambda e: e.transpose(lT[:], lpad[:]))
        DR("v")
        OP("v", lambda e: e.reciprocal(AP(rT, 0, [[512, 32], [32, 16]]),
                                       AP(lT, 0, [[512, 32], [32, 16]])))
        DR("v")
        OP("v", lambda e: e.transpose(rrow[:], rT[:]), inc="sV")
        v_rl = N("sV")
        W("g", "sV", v_rl)
        OP("g", lambda e: e.partition_broadcast(rlbc_sb[:], rrow[0:1, 0:512]),
           inc="sG")
        last_bc_box[0] = N("sG")
        W("v", "sG", last_bc_box[0])
        for g_ in range(4):
            c = 2 * h + g_ // 2
            pofs = (g_ % 2) * 64
            OP("v", lambda e, g2=g_, c=c, po=pofs, j=j, pa=par: e.tensor_mul(
                attn_fm[po:po + 64, c * 1024 + j * 128: c * 1024 + j * 128 + 128],
                psAV[pa][0:64, g2 * 128:g2 * 128 + 128],
                rlbc_sb[0:64, g2 * 128:g2 * 128 + 128]),
               inc=("sV" if g_ == 3 else None))
        evac_done[par] = N("sV")

    from collections import deque
    pend = deque()

    def flush_av(all_=False):
        while len(pend) > (0 if all_ else 3):
            pi, pit = pend.popleft()
            emit_av(pi, pit)
            av_last[(pit[0], pit[1])] = N("sT")
            if pit[2] == pit[1]:
                emit_slotend(pit[0], pit[1])

    emit_qw_dma(2)
    emit_qw_dma(3)

    item_idx = 0
    for h in range(NKV):
        qgen = gen_qblock(h) if h <= 6 else None
        for j in range(NOWN):
            for sub in range(j + 1):
                it = (h, j, sub)
                if sub == 0:
                    slot_par[(h, j)] = nslots % 2
                    nslots += 1
                i = len(exp_item)
                flush_av()
                emit_scores(i, it)
                pend.append((i, it))
                if qgen is not None and next(qgen, "END") == "END":
                    qgen = None
                target = DENSE_TOTAL * (item_idx + 1) // TOTAL_ITEMS
                while dense_emitted[0] < target and \
                        next(dense_it, "END") != "END":
                    dense_emitted[0] += 1
                item_idx += 1
        while qgen is not None and next(qgen, "END") != "END":
            pass
    flush_av(all_=True)
    for _ in dense_it:
        pass
    v_attn_done = N("sV")

    DBG("attn", attn_fm, [("sV", v_attn_done)])
    DBG("g", g_fm, [evac_g[15]])

    p2_t_end = N("sT")
    psum2.close()
    p12.close()

    p4 = ExitStack()
    owbuf = [alloc(p4, f"owbuf{i}", [128, 2048], BF16) for i in range(2)]
    out_sb = [alloc(p4, f"out_sb{i}", [128, HID], F32) for i in range(2)]
    xf32 = [alloc(p4, f"xf32_{i}", [128, HID], F32, side="right") for i in range(4)]

    psum4 = ExitStack()
    psO = [psum4.enter_context(nc.psum_tensor(f"psO{i}", [128, 1024], F32)) for i in range(2)]
    psG = [psum4.enter_context(nc.psum_tensor(f"psG{i}", [128, 2048], BF16)) for i in range(2)]

    silu_done = [0] * 16
    for oc in range(16):
        W("s", evac_g[oc][0], evac_g[oc][1])
        OP("s", lambda e, oc=oc: e.activation(
            out=g_fm[:, oc * 1024:(oc + 1) * 1024],
            in_=g_fm[:, oc * 1024:(oc + 1) * 1024], func=AF.Silu), inc="sS")
        silu_done[oc] = N("sS")

    sXF = ["sXFa", "sXFb", "sXFc", "sXFd"]
    x_ld = [0] * NOWN
    comb_done = [0] * NOWN
    out_dma_cnt = [0] * NOWN

    def emit_xload(j):
        if j < 4:
            W("y", "sT", p2_t_end)
            W("y", "sV", v_attn_done)
        else:
            W("y", "sV", comb_done[j - 4])
        OP("y", lambda e, j=j: e.dma_start(
            out=xf32[j % 4][:], in_=D["x_kv"][(2 * j + 1) * 128:(2 * j + 2) * 128, :]),
           inc=sXF[j % 4], incn=16)
        x_ld[j] = N(sXF[j % 4])

    for j in range(4):
        emit_xload(j)

    oproj_done = [0] * 16
    sg_done = [0] * 16
    psG_free = [0, 0]
    for oc in range(16):
        so_ = "sOWa" if oc % 2 == 0 else "sOWb"
        if oc >= 2:
            W("g", "sT", oproj_done[oc - 2])
        else:
            W("g", "sV", rope_done_q[15])
            W("g", "sT", t_qnorm[15])
            W("g", "sS", qevac_done[15])
        OP("g", lambda e, oc=oc: e.dma_start(
            out=owbuf[oc % 2][:], in_=D["ow_s"][:, oc * 2048:(oc + 1) * 2048]),
           inc=so_, incn=16)
        ow_ld = N(so_)

        W("t", so_, ow_ld)
        if oc == 0:
            W("t", "sS", silu_done[0])
            W("t", "sV", v_attn_done)
            W("t", "sT", p2_t_end)
        if oc >= 2:
            W("t", "sV", sg_done[oc - 2])
        for ac in range(16):
            stat = owbuf[oc % 2][:, ac * 128:ac * 128 + 128]
            for g2 in range(2):
                mov = attn_fm[:, ac * 1024 + g2 * 512: ac * 1024 + g2 * 512 + 512]
                OP("t", lambda e, st=stat, m=mov, g2=g2, oc=oc, ac=ac: e.matmul(
                    psO[oc % 2][:, g2 * 512:g2 * 512 + 512], st, m,
                    start=(ac == 0), stop=(ac == 15)),
                   inc=("sT" if (ac == 15 and g2 == 1) else None))
        oproj_done[oc] = N("sT")
        W("v", "sT", oproj_done[oc])
        W("v", "sS", silu_done[oc])
        OP("v", lambda e, oc=oc: e.tensor_mul(
            g_fm[:, oc * 1024:(oc + 1) * 1024], psO[oc % 2][:, 0:1024],
            g_fm[:, oc * 1024:(oc + 1) * 1024]), inc="sV")
        sg_done[oc] = N("sV")
        oc2 = oc - 1
        if oc2 >= 0:
            W("t", "sV", sg_done[oc2])
            for j01 in range(2):
                OP("t", lambda e, oc2=oc2, j=j01: e.transpose(
                    psG[j][:, oc2 * 128:oc2 * 128 + 128],
                    g_fm[:, oc2 * 1024 + j * 128: oc2 * 1024 + j * 128 + 128],
                    identb_sb[:]))

    for j in range(NOWN):
        W("t", "sV", sg_done[15])
        W("t", "sV", psG_free[j % 2])
        ocs = [15] if j < 2 else list(range(16))
        for k2, oc in enumerate(ocs):
            OP("t", lambda e, oc=oc, j=j: e.transpose(
                psG[j % 2][:, oc * 128:oc * 128 + 128],
                g_fm[:, oc * 1024 + j * 128: oc * 1024 + j * 128 + 128],
                identb_sb[:]),
               inc=("sT" if k2 == len(ocs) - 1 else None))
        t_tr4 = N("sT")
        W("v", "sT", t_tr4)
        W("v", sXF[j % 4], x_ld[j])
        if j >= 2:
            W("v", "sOa" if j % 2 == 0 else "sOb", out_dma_cnt[j - 2])
        OP("v", lambda e, j=j: e.tensor_add(
            out_sb[j % 2][:], psG[j % 2][:, 0:2048], xf32[j % 4][:]), inc="sV")
        comb_done[j] = N("sV")
        psG_free[j % 2] = comb_done[j]
        W("y", "sV", comb_done[j])
        sout = "sOa" if j % 2 == 0 else "sOb"
        OP("y", lambda e, j=j: e.dma_start(
            out=out_d[j * 128:(j + 1) * 128, :], in_=out_sb[j % 2][:]),
           inc=sout, incn=16)
        out_dma_cnt[j] = N(sout)
        if j + 4 < NOWN:
            emit_xload(j + 4)

    W("y", "sOa", N("sOa"))
    W("y", "sOb", N("sOb"))

    psum4.close()
    p4.close()
    p2.close()

    with nc.Block() as blk:
        @blk.tensor
        def _(e):
            for f in OPS["t"]:
                f(e)

        @blk.vector
        def _(e):
            for f in OPS["v"]:
                f(e)

        @blk.scalar
        def _(e):
            for f in OPS["s"]:
                f(e)

        @blk.gpsimd
        def _(e):
            for f in OPS["g"]:
                f(e)

        @blk.sync
        def _(e):
            for f in OPS["y"]:
                f(e)

    ctx.close()
    lower_extended_insts(nc)
    return nc


_NC_CACHE = {}


def _get_nc(dbg=()):
    key = tuple(sorted(dbg))
    if key not in _NC_CACHE:
        _NC_CACHE[key] = _build(key)
    return _NC_CACHE[key]


def kernel(**inputs):
    in_maps = _host_prepare(inputs)
    nc = _get_nc()
    res = run_bass_kernel_spmd(nc, in_maps, core_ids=list(range(8)))
    odt = np.result_type(np.float32, res.results[0]["out"].dtype)
    out = np.empty((4, S, HID), dtype=odt)
    for core in range(8):
        seq, b = core % 4, core // 4
        o = res.results[core]["out"]
        for j in range(NOWN):
            out[seq, (2 * j + b) * 128:(2 * j + b + 1) * 128, :] = o[j * 128:(j + 1) * 128, :]
    return out


# version 34
# speedup vs baseline: 1.0011x; 1.0011x over previous
import numpy as np
import ml_dtypes
import concourse.bass as bass
import concourse.mybir as mybir
from concourse.bass import AP
from concourse.bass_utils import run_bass_kernel_spmd
from concourse.library_config import attn as _ATTN_LIB
from concourse.library_overlay import lower_extended_insts
from contextlib import ExitStack

F32 = mybir.dt.float32
BF16 = mybir.dt.bfloat16
AF = mybir.ActivationFunctionType
ALU = mybir.AluOpType
AX = mybir.AxisListType
BFNP = ml_dtypes.bfloat16

S = 2048
HID = 2048
NHC = 16
NT = 16
NOWN = 8
NKV = 8
THETA = 10000.0
NEG_BIG = -1.0e30
EPS = 1e-5
QKV_PITCH = 2 * 4 * 1024


def _tile_fm(wt):
    K, N = wt.shape
    return np.ascontiguousarray(wt.reshape(K // 128, 128, N).transpose(1, 0, 2).reshape(128, -1))


def _tile_stream(wt, ob=128):
    K, N = wt.shape
    return np.ascontiguousarray(
        wt.reshape(K // 128, 128, N // ob, ob).transpose(1, 2, 0, 3).reshape(128, -1))


def _storage_order(b):
    return [s ^ 1 if b == 0 else s for s in range(NT)]


def _host_prepare(inputs):
    x = np.asarray(inputs["L0_attn_hidden_states"], dtype=np.float32)
    q_w = np.asarray(inputs["L0_attn_q_w"], dtype=np.float32)
    kr_w = np.asarray(inputs["L0_attn_kr_w"], dtype=np.float32)
    down_w = np.asarray(inputs["L0_attn_down_w"], dtype=np.float32)
    upk_w = np.asarray(inputs["L0_attn_upk_w"], dtype=np.float32)
    upv_w = np.asarray(inputs["L0_attn_upv_w"], dtype=np.float32)
    o_w = np.asarray(inputs["L0_attn_o_w"], dtype=np.float32)
    ffn_w = np.asarray(inputs["L0_ffn_weight"], dtype=np.float32)
    gate_w = np.asarray(inputs["L0_ffn_gate_w"], dtype=np.float32)
    A_w = np.asarray(inputs["L0_ffn_A_w"], dtype=np.float32)
    B_w = np.asarray(inputs["L0_ffn_B_w"], dtype=np.float32)

    shared = {
        "qw_s": _tile_stream(q_w.T, 128),
        "fw_s": _tile_stream(ffn_w.T, 128),
        "ow_s": _tile_stream(o_w.T, 128),
        "down_t": _tile_fm(down_w.T),
        "upk_t": _tile_fm(upk_w.T),
        "upv_t": _tile_fm(upv_w.T),
        "krg_t": _tile_fm(np.concatenate([kr_w.T, gate_w.T], axis=1)),
        "acat_t": _tile_fm(A_w.reshape(128, HID).T),
        "bcat_t": np.ascontiguousarray(B_w.transpose(0, 2, 1).reshape(128, HID)),
        "trimask": np.ascontiguousarray(np.tril(np.ones((128, 128), dtype=BFNP)).T),
        "ones2c": np.concatenate([
            np.concatenate([np.ones((64, 1)), np.zeros((64, 1))], 0),
            np.concatenate([np.zeros((64, 1)), np.ones((64, 1))], 0)], 1).astype(BFNP),
        "e8": np.ascontiguousarray(np.repeat(np.eye(8, dtype=BFNP), 16, axis=1)),
        "identb": np.eye(128, dtype=BFNP),
    }

    import jax as _jax
    import jax.numpy as _jnp
    with _jax.default_device(_jax.devices("cpu")[0]):
        _pos = _jnp.arange(S, dtype=_jnp.float32)
        _invf = 1.0 / THETA ** (_jnp.arange(0, 64, 2, dtype=_jnp.float32) / 64)
        _freqs = _pos[:, None] * _invf[None, :]
        COS_T = np.asarray(_jnp.cos(_freqs), dtype=np.float32)
        SIN_T = np.asarray(_jnp.sin(_freqs), dtype=np.float32)

    in_maps = []
    for core in range(8):
        seq, b = core % 4, core // 4
        order = _storage_order(b)
        xs = x[seq].reshape(NT, 128, HID)
        x_kv = np.ascontiguousarray(xs[order].reshape(S, HID))

        pos_own = np.concatenate(
            [np.arange((2 * j + b) * 128, (2 * j + b + 1) * 128) for j in range(NOWN)]
        )
        d = np.arange(128) % 64
        sgn = np.where(d < 32, -1.0, 1.0).astype(np.float32)[:, None]
        cosq = (COS_T[pos_own][:, d % 32].T * np.float32(0.125))
        sinq = (SIN_T[pos_own][:, d % 32].T * np.float32(0.125) * sgn)

        cosk = np.empty((128, NT * 32), dtype=np.float32)
        sink = np.empty((128, NT * 32), dtype=np.float32)
        c = np.arange(32)
        sgnk = np.where(c < 16, -1.0, 1.0).astype(np.float32)[None, :]
        for s_ in range(NT):
            pos = order[s_] * 128 + np.arange(128)
            cosk[:, s_ * 32:(s_ + 1) * 32] = COS_T[pos][:, c % 16]
            sink[:, s_ * 32:(s_ + 1) * 32] = SIN_T[pos][:, c % 16] * sgnk

        flag = np.full((128, 1), 0.0 if b == 0 else 1.0, dtype=np.float32)
        m = {"x_kv": x_kv, "cosq": np.ascontiguousarray(cosq),
             "sinq": np.ascontiguousarray(sinq), "cosk": cosk, "sink": sink,
             "flag": flag}
        m.update(shared)
        in_maps.append(m)
    return in_maps


def _build(dbg=()):
    nc = bass.Bass("TRN2", target_bir_lowering=False)

    D = {}
    for name, shape, dt in [
        ("x_kv", [S, HID], F32),
        ("qw_s", [128, 16 * 2048], F32),
        ("fw_s", [128, 16 * 2048], F32),
        ("ow_s", [128, 16 * 2048], F32),
        ("down_t", [128, 16 * 256], F32),
        ("upk_t", [128, 2 * 256], F32),
        ("upv_t", [128, 2 * 512], F32),
        ("krg_t", [128, 16 * 264], F32),
        ("acat_t", [128, 16 * 128], F32),
        ("bcat_t", [128, HID], F32),
        ("trimask", [128, 128], BF16),
        ("ones2c", [128, 2], BF16),
        ("e8", [8, 128], BF16),
        ("identb", [128, 128], BF16),
        ("cosq", [128, 1024], F32),
        ("sinq", [128, 1024], F32),
        ("cosk", [128, NT * 32], F32),
        ("sink", [128, NT * 32], F32),
        ("flag", [128, 1], F32),
    ]:
        D[name] = nc.dram_tensor(name, shape, dt, kind="ExternalInput").ap()
    out_d = nc.dram_tensor("out", [NOWN * 128, HID], F32, kind="ExternalOutput").ap()

    OPS = {"t": [], "v": [], "s": [], "g": [], "y": []}
    CNT = {}
    SEMS = {}

    def I(sem, n=1):
        CNT[sem] = CNT.get(sem, 0) + n
        return CNT[sem]

    def N(sem):
        return CNT.get(sem, 0)

    def W(eng, sem, tgt):
        if tgt and tgt > 0:
            OPS[eng].append(lambda e, s=sem, t=tgt: e.wait_ge(SEMS[s], t))

    def OP(eng, fn, inc=None, incn=1):
        if inc is None:
            OPS[eng].append(fn)
            return None
        v = I(inc, incn)
        OPS[eng].append(lambda e, f=fn, s=inc, n=incn: f(e).then_inc(SEMS[s], n))
        return v

    def DR(eng):
        OPS[eng].append(lambda e: e.drain())

    def DBG(name, tensor, waits):
        if name not in dbg:
            return
        od = nc.dram_tensor("dbg_" + name, list(tensor.shape), tensor.dtype,
                            kind="ExternalOutput").ap()
        for eng_sem, tgt in waits:
            W("y", eng_sem, tgt)
        OP("y", lambda e, o=od, t2=tensor: e.dma_start(out=o[:], in_=t2[:]),
           inc="sDbg", incn=16)

    ctx = ExitStack()

    def alloc(st, name, shape, dt, side=None):
        if side:
            return st.enter_context(nc.sbuf_tensor(name, shape, dt, side=side))
        return st.enter_context(nc.sbuf_tensor(name, shape, dt))

    acat_sb = alloc(ctx, "acat_sb", [128, 16 * 128], BF16)
    bcat_sb = alloc(ctx, "bcat_sb", [128, HID], BF16)
    cosq_sb = alloc(ctx, "cosq_sb", [128, 1024], BF16)
    sinq_sb = alloc(ctx, "sinq_sb", [128, 1024], BF16)
    trim_sb = alloc(ctx, "trim_sb", [128, 128], BF16)
    ones2c_sb = alloc(ctx, "ones2c_sb", [128, 2], BF16)
    e8_sb = alloc(ctx, "e8_sb", [8, 128], BF16)
    identb_sb = alloc(ctx, "identb_sb", [128, 128], BF16)
    flag_sb = alloc(ctx, "flag_sb", [128, 1], F32)
    xn_odd = alloc(ctx, "xn_odd", [128, NHC * 1024], BF16)
    logits_sb = alloc(ctx, "logits_sb", [128, 64], F32)
    kmax8_sb = alloc(ctx, "kmax8_sb", [8, 1], F32)
    qmax8_sb = alloc(ctx, "qmax8_sb", [8, 1], F32)
    mtmp8_sb = alloc(ctx, "mtmp8_sb", [8, 1], F32)
    mtmp8b_sb = alloc(ctx, "mtmp8b_sb", [8, 1], BF16)
    negmrow_sb = alloc(ctx, "negmrow_sb", [1, 8], F32)
    negm_sb = alloc(ctx, "negm_sb", [128, 8], F32)
    kmax8b_sb = alloc(ctx, "kmax8b_sb", [8, 1], BF16)
    kmaxrow_sb = alloc(ctx, "kmaxrow_sb", [1, 8], F32)
    u_sb = alloc(ctx, "u_sb", [128, 1024], BF16)
    t_sb = alloc(ctx, "t_sb", [128, 1024], BF16)
    cwb_sb = alloc(ctx, "cwb_sb", [128, 64], BF16)
    cwf_sb = alloc(ctx, "cwf_sb", [8, 256], BF16)
    gs1 = alloc(ctx, "gs1", [128, 8], F32)
    gs2 = alloc(ctx, "gs2", [128, 8], F32)
    gm1 = alloc(ctx, "gm1", [128, 1], F32)
    gm1n = alloc(ctx, "gm1n", [128, 1], F32)
    gm2 = alloc(ctx, "gm2", [128, 1], F32)
    gden = alloc(ctx, "gden", [128, 1], F32)
    qacc = alloc(ctx, "qacc", [2, 8], F32)
    qnt = alloc(ctx, "qnt", [2, 1], F32)
    mtmpA = alloc(ctx, "mtmpA", [1, 1], F32)
    qpad = alloc(ctx, "qpad", [32, 32], F32)
    qpadT = alloc(ctx, "qpadT", [32, 32], F32)
    g_fm = alloc(ctx, "g_fm", [128, 16 * 1024], BF16)

    p12 = ExitStack()
    p01 = ExitStack()
    Kp_sb = alloc(p12, "Kp_sb", [128, 4 * S], BF16, side="right")
    V_sb = alloc(p12, "V_sb", [128, NKV * NT * 65], BF16, side="right")
    qkv_sb = alloc(p12, "qkv_sb", [128, QKV_PITCH], BF16, side="right")
    down_sb = alloc(p01, "down_sb", [128, 16 * 256], BF16, side="right")
    upk_sb = alloc(p01, "upk_sb", [128, 2 * 256], BF16, side="right")
    upv_sb = alloc(p01, "upv_sb", [128, 2 * 512], BF16, side="right")
    krg_sb = alloc(p01, "krg_sb", [128, 16 * 264], BF16, side="right")
    cosk_sb = alloc(p01, "cosk_sb", [128, NT * 32], F32, side="right")
    sink_sb = alloc(p01, "sink_sb", [128, NT * 32], F32, side="right")

    sem_names = ["sWsm", "sTbl", "sXa", "sXb", "sQa", "sQb", "sFa", "sFb",
                 "sOWa", "sOWb", "sXFa", "sXFb", "sXFc", "sXFd", "sOa", "sOb",
                 "sKTa", "sKTb", "sDbg", "sT", "sV", "sS", "sG"]
    for n_, h_ in zip(sem_names, [ctx.enter_context(nc.semaphore(n_)) for n_ in sem_names]):
        SEMS[n_] = h_

    OP("g", lambda e: e.load_library(_ATTN_LIB))
    for dst, src in [(down_sb, "down_t"), (upk_sb, "upk_t"), (upv_sb, "upv_t"),
                     (krg_sb, "krg_t"), (acat_sb, "acat_t"), (bcat_sb, "bcat_t")]:
        OP("g", lambda e, d=dst, s=src: e.dma_start(out=d[:], in_=D[s]), inc="sWsm", incn=16)
    for dst, src in [(trim_sb, "trimask"), (ones2c_sb, "ones2c"),
                     (e8_sb, "e8"), (identb_sb, "identb"), (flag_sb, "flag"),
                     (cosk_sb, "cosk"), (sink_sb, "sink")]:
        OP("y", lambda e, d=dst, s=src: e.dma_start(out=d[:], in_=D[s]), inc="sTbl", incn=16)
    for dst, src in [(cosq_sb, "cosq"), (sinq_sb, "sinq")]:
        OP("g", lambda e, d=dst, s=src: e.dma_start(out=d[:], in_=D[s]), inc="sWsm", incn=16)
    w_smallw = N("sWsm")
    w_tables = N("sTbl")

    p1 = ExitStack()
    xbuf = [alloc(p1, f"xbuf{i}", [128, HID], BF16, side="right") for i in range(2)]
    xn_grp = alloc(p1, "xn_grp", [128, NHC * 512], BF16, side="right")
    kvK = [alloc(p1, f"kvK{t}", [128, 512], BF16, side="right") for t in range(4)]
    ckv_sb = alloc(p1, "ckv_sb", [128, 1024], BF16, side="right")
    krt1 = alloc(p1, "krt1", [128, 256], BF16, side="right")
    krt2 = alloc(p1, "krt2", [128, 256], BF16, side="right")
    sqb = [alloc(p1, f"sqb{i}", [128, HID], BF16, side="right") for i in range(3)]
    stat1 = alloc(p1, "stat1", [128, 24], F32, side="right")
    stat2 = alloc(p1, "stat2", [128, 2], F32, side="right")
    stat3 = alloc(p1, "stat3", [128, 1], F32, side="right")
    stat4 = alloc(p1, "stat4", [128, 1], F32, side="right")
    ksq_sb = alloc(p1, "ksq_sb", [128, 512], BF16, side="right")
    kn_acc = alloc(p1, "kn_acc", [128, 8], BF16, side="right")
    kntmp = alloc(p1, "kntmp", [128, 8], F32, side="right")

    psum1 = ExitStack()
    psT = [psum1.enter_context(nc.psum_tensor(f"psT{i}", [128, 1024], BF16)) for i in range(4)]
    psKG = [psum1.enter_context(nc.psum_tensor(f"psKG{i}", [128, 264], F32)) for i in range(2)]
    psC = psum1.enter_context(nc.psum_tensor("psC", [128, 512], F32))
    psD = psum1.enter_context(nc.psum_tensor("psD", [128, 512], F32))

    OP("v", lambda e: e.memset(
        AP(V_sb, 64, [[NKV * NT * 65, 128], [65, NKV * NT], [1, 1]]), 1.0))
    OP("v", lambda e: e.memset(qacc[:], 0.0))
    OP("v", lambda e: e.memset(qpad[:], 0.0))
    OP("v", lambda e: e.memset(kn_acc[:], 0.0), inc="sV")
    v_ones = N("sV")

    qproj_done = [0] * 16
    qevac_done = [0] * 16
    rope_done_q = [0] * 16
    v_sq = [0] * 16
    t_qnorm = [0] * 16
    v_qred = [0] * 16
    qw_ld = [0] * 16
    mh_ready = [0] * NKV
    scores_done = [0] * NKV

    ln_done = [0] * NT
    vstats_done = [0] * NT
    stat2_free = [0] * NT
    xn_done = [[] for _ in range(NT)]
    xnT_done = [0] * NT
    rope_done = [0] * NT
    psT_free = [None, None, None, None]
    psKG_free = [0, 0]
    kvk_done = [0] * 4
    kvKT_done = [None, None, None, None]
    vav_done = [0] * NT
    K_done = [0] * NT
    t_upkv_last = [0]
    t_down_last = [0]
    s_ckv_box = [0]

    def tile_of(s_):
        g_, t_ = divmod(s_, 4)
        if t_ % 2 == 1:
            return ("odd", 2 * g_ + (t_ // 2))
        return ("grp", t_ // 2)

    def xn_stat_ap(s_, hc):
        kind, idx = tile_of(s_)
        if kind == "odd":
            return xn_odd[:, hc * 1024 + idx * 128: hc * 1024 + idx * 128 + 128]
        return xn_grp[:, hc * 512 + idx * 128: hc * 512 + idx * 128 + 128]

    W("t", "sTbl", w_tables)
    W("t", "sWsm", w_smallw)

    def p1_head_front(s_):
        g, t = divmod(s_, 4)
        sx = ["sXa", "sXb"][s_ % 2]
        if s_ >= 2:
            W("g", "sV", vstats_done[s_ - 2])
            W("g", "sS", ln_done[s_ - 2])
        OP("g", lambda e, d=xbuf[s_ % 2], s2=s_: e.dma_start(
            out=d[:], in_=D["x_kv"][s2 * 128:(s2 + 1) * 128, :]), inc=sx, incn=16)
        w_x = N(sx)

        W("v", sx, w_x)
        xb = xbuf[s_ % 2]
        for i4 in range(4):
            OP("v", lambda e, x=xb, i4=i4: e.bn_stats(
                stat1[:, i4 * 6:(i4 + 1) * 6], x[:, i4 * 512:(i4 + 1) * 512]))
        DR("v")
        if s_ >= 1:
            W("v", "sS", stat2_free[s_ - 1])
        OP("v", lambda e: e.bn_aggr(stat2[:], stat1[:]))
        DR("v")
        OP("v", lambda e: e.tensor_scalar_add(stat2[:, 1:2], stat2[:, 1:2], EPS),
           inc="sV")
        vstats_done[s_] = N("sV")
        W("s", "sV", vstats_done[s_])
        if s_ >= 3:
            W("s", "sT", xnT_done[s_ - 3])
        OP("s", lambda e: e.activation(out=stat3[:], in_=stat2[:, 1:2], func=AF.Ln))
        DR("s")
        OP("s", lambda e: e.activation(out=stat3[:], in_=stat3[:], func=AF.Exp,
                                       scale=-0.5))
        DR("s")
        OP("s", lambda e: e.activation(out=stat4[:], in_=stat2[:, 0:1],
                                       func=AF.Copy, scale=stat3[:, 0:1]), inc="sS")
        stat2_free[s_] = N("sS")
        DR("s")
        OP("s", lambda e: e.activation(out=stat4[:], in_=stat4[:],
                                       func=AF.Copy, scale=-1.0))
        DR("s")
        OP("s", lambda e, x=xb, s2=s_: e.activation(
            out=sqb[s2 % 3][:], in_=x[:], func=AF.Identity,
            scale=stat3[:, 0:1], bias=stat4[:, 0:1]), inc="sS")
        ln_done[s_] = N("sS")
        DR("s")

    def p1_head_tr(s_):
        g, t = divmod(s_, 4)
        W("t", "sS", ln_done[s_])
        kind, idx = tile_of(s_)
        t_tr = 0
        for batch in range(2):
            slot = (s_ % 2) * 2 + batch
            if psT_free[slot]:
                W("t", psT_free[slot][0], psT_free[slot][1])
            for k in range(8):
                hc = batch * 8 + k
                OP("t", lambda e, hc=hc, b=slot, s2=s_: e.transpose(
                    psT[b][:, (hc % 8) * 128:(hc % 8) * 128 + 128],
                    sqb[s2 % 3][:, hc * 128:hc * 128 + 128], identb_sb[:]),
                   inc=("sT" if k == 7 else None))
            t_tr = N("sT")
            ee = "v" if batch == 0 else "s"
            W(ee, "sT", t_tr)
            if kind == "grp" and g >= 1:
                W(ee, "sT", t_down_last[0])
            if kind == "odd":
                dst = AP(xn_odd, (batch * 8) * 1024 + idx * 128,
                         [[NHC * 1024, 128], [1024, 8], [1, 128]])
            else:
                dst = AP(xn_grp, (batch * 8) * 512 + idx * 128,
                         [[NHC * 512, 128], [512, 8], [1, 128]])
            if batch == 0:
                OP("v", lambda e, d=dst, b=slot: e.tensor_copy(d, psT[b][:, 0:1024]),
                   inc="sV")
                psT_free[slot] = ("sV", N("sV"))
            else:
                OP("s", lambda e, d=dst, b=slot: e.copy(out=d, in_=psT[b][:, 0:1024]),
                   inc="sS")
                psT_free[slot] = ("sS", N("sS"))
        xn_done[s_] = [psT_free[(s_ % 2) * 2], psT_free[(s_ % 2) * 2 + 1]]
        xnT_done[s_] = t_tr

    v_cw_done = [0] * NOWN

    def emit_gate_softmax(j):
        lg = logits_sb[:, j * 8:j * 8 + 8]
        DR("v")
        OP("v", lambda e, l=lg: e.reduce_max(gm1[:], l, axis=AX.X))
        OP("v", lambda e, l=lg: e.reduce_max(gm1n[:], l, axis=AX.X, negate=True))
        DR("v")
        OP("v", lambda e, l=lg: e.tensor_scalar(out=gs1[:], in0=l, scalar1=gm1[:],
                                                scalar2=None, op0=ALU.is_ge))
        DR("v")
        OP("v", lambda e, l=lg: e.scalar_tensor_tensor(
            out=gs2[:], in0=gs1[:], scalar=NEG_BIG, in1=l, op0=ALU.mult, op1=ALU.add))
        DR("v")
        OP("v", lambda e: e.reduce_max(gm2[:], gs2[:], axis=AX.X))
        DR("v")
        OP("v", lambda e, l=lg: e.tensor_scalar(out=gs2[:], in0=l, scalar1=gm2[:],
                                                scalar2=None, op0=ALU.is_ge), inc="sV")
        v_g1 = N("sV")
        W("s", "sV", v_g1)
        OP("s", lambda e, l=lg: e.activation(out=gs1[:], in_=l, func=AF.Exp,
                                             bias=gm1n[:]), inc="sS")
        s_g = N("sS")
        W("v", "sS", s_g)
        OP("v", lambda e: e.tensor_mul(gs1[:], gs1[:], gs2[:]))
        DR("v")
        OP("v", lambda e: e.reduce_sum(gden[:], gs1[:], axis=AX.X))
        DR("v")
        OP("v", lambda e: e.reciprocal(gden[:], gden[:]))
        DR("v")
        OP("v", lambda e, j=j: e.tensor_scalar(
            out=cwb_sb[:, j * 8:j * 8 + 8], in0=gs1[:], scalar1=gden[:],
            scalar2=16.0, op0=ALU.mult, op1=ALU.mult), inc="sV")
        v_cw_done[j] = N("sV")

    def p1_krgate_rope(s_):
        g, t = divmod(s_, 4)
        kind, idx = tile_of(s_)
        for se2, v2 in xn_done[s_]:
            W("t", se2, v2)
        W("t", "sV", psKG_free[s_ % 2])
        for hc in range(NHC):
            OP("t", lambda e, hc=hc, s2=s_: e.matmul(
                psKG[s2 % 2][:, 0:264], xn_stat_ap(s2, hc),
                krg_sb[:, hc * 264:hc * 264 + 264],
                start=(hc == 0), stop=(hc == NHC - 1)),
               inc=("sT" if hc == NHC - 1 else None))
        krg_done = N("sT")

        W("v", "sT", krg_done)
        W("v", "sTbl", w_tables)
        if kvKT_done[t]:
            W("v", kvKT_done[t][0], kvKT_done[t][1])
        pk = psKG[s_ % 2]
        cosrep = AP(cosk_sb, s_ * 32, [[NT * 32, 128], [0, 8], [1, 32]])
        sinrep = AP(sink_sb, s_ * 32, [[NT * 32, 128], [0, 8], [1, 32]])
        OP("v", lambda e, p=pk, c=cosrep: e.tensor_mul(krt1[:], p[:, 0:256], c))
        shuf = AP(pk, 16, [[264, 128], [32, 8], [-16, 2], [1, 16]])
        OP("v", lambda e, sh=shuf, sr=sinrep: e.tensor_mul(krt2[:], sh, sr))
        DR("v")
        kdst = AP(kvK[t], 0, [[512, 128], [64, 8], [32, 2], [1, 16]])
        OP("v", lambda e, d=kdst: e.tensor_add(d, krt1[:], krt2[:]))
        if kind == "odd":
            OP("v", lambda e, p=pk, j=idx: e.tensor_copy(
                logits_sb[:, j * 8:j * 8 + 8], p[:, 256:264]))
        OP("v", lambda e: e.engine_nop(), inc="sV")
        rope_done[s_] = N("sV")
        psKG_free[s_ % 2] = rope_done[s_]
        if kind == "odd":
            emit_gate_softmax(idx)

    def p1_down(g):
        for s_ in range(4 * g, 4 * g + 4):
            for se2, v2 in xn_done[s_]:
                W("t", se2, v2)
        for cc in range(2):
            if g >= 1:
                W("t", "sV", vav_done[4 * g - 2 + cc])
            tgt = psC if cc == 0 else psD
            for hc in range(NHC):
                stat = down_sb[:, hc * 256 + cc * 128: hc * 256 + cc * 128 + 128]
                mov_o = xn_odd[:, hc * 1024 + 2 * g * 128: hc * 1024 + 2 * g * 128 + 256]
                mov_g = xn_grp[:, hc * 512: hc * 512 + 256]
                OP("t", lambda e, p=tgt, st=stat, m=mov_o, hc=hc: e.matmul(
                    p[:, 0:256], st, m, start=(hc == 0), stop=False))
                OP("t", lambda e, p=tgt, st=stat, m=mov_g, hc=hc: e.matmul(
                    p[:, 256:512], st, m, start=False, stop=(hc == NHC - 1)),
                   inc=("sT" if hc == NHC - 1 else None))
        t_down_last[0] = N("sT")

        W("s", "sT", t_down_last[0])
        if g >= 1:
            W("s", "sT", t_upkv_last[0])
        OP("s", lambda e: e.copy(out=ckv_sb[:, 0:512], in_=psC[:]))
        DR("s")
        OP("s", lambda e: e.copy(out=ckv_sb[:, 512:1024], in_=psD[:]), inc="sS")
        s_ckv_box[0] = N("sS")

    def ckv_col(t_):
        return (t_ // 2) * 128 if t_ % 2 == 1 else 256 + (t_ // 2) * 128

    def p1_upkv(g, t):
        s_ = 4 * g + t
        pk_ = psKG[t % 2]
        pv_ = psC if t % 2 == 0 else psD
        W("t", "sS", s_ckv_box[0])
        W("t", "sV", rope_done[s_])
        if 4 * g + t + 2 < NT:
            W("t", "sV", rope_done[4 * g + t + 2])
        if t >= 2:
            W("t", "sV", vav_done[4 * g + t - 2])
        for cc in range(2):
            stat = ckv_sb[:, cc * 512 + ckv_col(t): cc * 512 + ckv_col(t) + 128]
            OP("t", lambda e, p=pk_, st=stat, cc=cc: e.matmul(
                p[:, 0:256], st, upk_sb[:, cc * 256:cc * 256 + 256],
                start=(cc == 0), stop=(cc == 1)))
            OP("t", lambda e, p=pv_, st=stat, cc=cc: e.matmul(
                p[:, 0:512], st, upv_sb[:, cc * 512:cc * 512 + 512],
                start=(cc == 0), stop=(cc == 1)),
               inc=("sT" if cc == 1 else None))
        t_upkv_last[0] = N("sT")

        W("v", "sT", t_upkv_last[0])
        W("v", "sV", v_ones)
        kcdst = AP(kvK[t], 16, [[512, 128], [64, 8], [32, 2], [1, 16]])
        OP("v", lambda e, d=kcdst, p=pk_: e.tensor_copy(d, p[:, 0:256]), inc="sV")
        kvk_done[t] = N("sV")
        psKG_free[t % 2] = kvk_done[t]
        DR("v")
        OP("v", lambda e, t2=t: e.tensor_mul(ksq_sb[:], kvK[t2][:], kvK[t2][:]))
        DR("v")
        OP("v", lambda e: e.reduce_sum(
            kntmp[:], AP(ksq_sb, 0, [[512, 128], [64, 8], [1, 64]]), axis=AX.X))
        DR("v")
        OP("v", lambda e: e.tensor_max(kn_acc[:], kn_acc[:], kntmp[:]))
        DR("v")
        for h in range(NKV):
            OP("v", lambda e, p=pv_, h=h, s2=s_: e.tensor_copy(
                V_sb[:, (h * NT + s2) * 65:(h * NT + s2) * 65 + 64],
                p[:, h * 64:h * 64 + 64]),
               inc=("sV" if h == NKV - 1 else None))
        vav_done[s_] = N("sV")

        se_ = "sKTa" if t % 2 == 0 else "sKTb"
        W("y", "sV", kvk_done[t])
        for p in range(4):
            OP("y", lambda e, p=p, t2=t, s2=s_: e.dma_start_transpose(
                out=Kp_sb[:, p * S + s2 * 128: p * S + s2 * 128 + 128],
                in_=kvK[t2][:, p * 128:p * 128 + 128]),
               inc=se_, incn=16)
        kvKT_done[t] = (se_, N(se_))
        K_done[s_] = kvKT_done[t]

    for i in range(21):
        if i < 16:
            p1_head_front(i)
        if 1 <= i <= 16:
            p1_head_tr(i - 1)
        if i >= 5 and (i - 5) // 4 <= 3:
            p1_upkv((i - 5) // 4, (i - 5) % 4)
        if 2 <= i <= 17:
            p1_krgate_rope(i - 2)
        if i >= 4 and (i - 4) % 4 == 0 and i <= 16:
            p1_down((i - 4) // 4)

    W("t", "sV", vav_done[NT - 1])
    if psT_free[0]:
        W("t", psT_free[0][0], psT_free[0][1])
    OP("t", lambda e: e.transpose(psT[0][0:8, 0:128], kn_acc[:], identb_sb[:]), inc="sT")
    t_kn = N("sT")
    W("v", "sT", t_kn)
    OP("v", lambda e: e.reduce_max(kmax8_sb[:], psT[0][0:8, 0:128], axis=AX.X))
    DR("v")
    OP("v", lambda e: e.tensor_copy(kmax8b_sb[:], kmax8_sb[:]), inc="sV")
    v_nk = N("sV")
    W("t", "sV", v_nk)
    OP("t", lambda e: e.transpose(psT[0][0:1, 256:264], kmax8b_sb[:],
                                  identb_sb[0:8, 0:8]), inc="sT")
    t_kr = N("sT")
    W("v", "sT", t_kr)
    OP("v", lambda e: e.tensor_copy(kmaxrow_sb[:], psT[0][0:1, 256:264]), inc="sV")
    v_nkr = N("sV")

    W("t", "sV", vav_done[NT - 1])
    for se2, v2 in xn_done[NT - 1]:
        W("t", se2, v2)
    for g2 in range(2):
        tgtA = psC if g2 == 0 else psD
        for hc in range(NHC):
            stat = acat_sb[:, hc * 128:hc * 128 + 128]
            mov = xn_odd[:, hc * 1024 + g2 * 512: hc * 1024 + g2 * 512 + 512]
            OP("t", lambda e, p=tgtA, st=stat, m=mov, hc=hc: e.matmul(
                p[:, 0:512], st, m, start=(hc == 0), stop=(hc == NHC - 1)),
               inc=("sT" if hc == NHC - 1 else None))
    t_u = N("sT")
    W("s", "sT", t_u)
    OP("s", lambda e: e.copy(out=u_sb[:, 0:512], in_=psC[:]))
    DR("s")
    OP("s", lambda e: e.copy(out=u_sb[:, 512:1024], in_=psD[:]), inc="sS")
    s_u = N("sS")

    QWPRE_COL = {0: 14336, 1: 12288}
    qraw_pre = [g_fm[:, 10240 + 1024 * qc: 10240 + 1024 * qc + 1024] for qc in range(2)]
    sqq_pre = [g_fm[:, 8192 + 1024 * qc: 8192 + 1024 * qc + 1024] for qc in range(2)]
    qpart_pre = g_fm[:, 7168:8192]
    for qc in range(2):
        sq_ = "sQa" if qc == 0 else "sQb"
        OP("g", lambda e, qc=qc: e.dma_start(
            out=g_fm[:, QWPRE_COL[qc]:QWPRE_COL[qc] + 2048],
            in_=D["qw_s"][:, qc * 2048:(qc + 1) * 2048]), inc=sq_, incn=16)
        qw_ld[qc] = N(sq_)

    pre_evac = [[0, 0], [0, 0]]
    v_rope_pre = [0, 0]

    def pre_qp(qc, g2):
        sq_ = "sQa" if qc == 0 else "sQb"
        tgt = psC if g2 == 0 else psD
        W("t", sq_, qw_ld[qc])
        if qc == 0 and g2 == 0:
            for se2, v2 in xn_done[NT - 1]:
                W("t", se2, v2)
        W("t", "sS", s_u)
        if qc == 1:
            W("t", "sS", pre_evac[0][g2])
        for hc in range(NHC):
            stat = g_fm[:, QWPRE_COL[qc] + hc * 128: QWPRE_COL[qc] + hc * 128 + 128]
            mov = xn_odd[:, hc * 1024 + g2 * 512: hc * 1024 + g2 * 512 + 512]
            OP("t", lambda e, p=tgt, st=stat, m=mov, hc=hc: e.matmul(
                p[:, 0:512], st, m, start=(hc == 0), stop=(hc == NHC - 1)),
               inc=("sT" if hc == NHC - 1 else None))
        qp_h = N("sT")
        if g2 == 1:
            qproj_done[qc] = qp_h
        W("s", "sT", qp_h)
        OP("s", lambda e, qc=qc, g2=g2, p=tgt: e.copy(
            out=qraw_pre[qc][:, g2 * 512:g2 * 512 + 512], in_=p[:, 0:512]),
           inc="sS")
        pre_evac[qc][g2] = N("sS")
        if g2 == 1:
            qevac_done[qc] = pre_evac[qc][1]

    def pre_shuffle_sq(qc):
        W("v", "sS", pre_evac[qc][1])
        for blk in range(4):
            OP("v", lambda e, qc=qc, b=blk: e.tensor_copy(
                qpart_pre[b * 32:b * 32 + 32, :],
                qraw_pre[qc][(b ^ 1) * 32:(b ^ 1) * 32 + 32, :]))
        OP("v", lambda e, qc=qc: e.tensor_mul(
            sqq_pre[qc][:], qraw_pre[qc][:], qraw_pre[qc][:]))
        DR("v")
        OP("v", lambda e, qc=qc: e.tensor_max(
            sqq_pre[qc][:, 0:512], sqq_pre[qc][:, 0:512], sqq_pre[qc][:, 512:1024]),
           inc="sV")
        v_sq[qc] = N("sV")
        DR("v")

    def pre_rope(qc):
        if qc == 0:
            W("v", "sWsm", w_smallw)
        OP("v", lambda e, qc=qc: e.tensor_mul(qraw_pre[qc][:], qraw_pre[qc][:],
                                              cosq_sb[:]))
        OP("v", lambda e: e.tensor_mul(qpart_pre[:], qpart_pre[:], sinq_sb[:]))
        DR("v")
        for half in range(2):
            gl = 2 * qc + half
            dst = qkv_sb[0:64, gl * 1024: gl * 1024 + 1024]
            OP("v", lambda e, qc=qc, d=dst, hf=half: e.tensor_add(
                d, qraw_pre[qc][hf * 64:hf * 64 + 64, :],
                qpart_pre[hf * 64:hf * 64 + 64, :]))
        DR("v")
        OP("v", lambda e: e.engine_nop(), inc="sV")
        rope_done_q[qc] = N("sV")

    pre_chunks = [
        lambda: pre_qp(0, 0), lambda: pre_qp(0, 1),
        lambda: pre_qp(1, 0), lambda: pre_qp(1, 1),
        lambda: pre_shuffle_sq(0), lambda: pre_rope(0),
        lambda: pre_shuffle_sq(1), lambda: pre_rope(1),
    ]

    tsc_done = [0] * NOWN
    tsc_vread = [0, 0]
    for j in range(NOWN):
        if j < len(pre_chunks):
            pre_chunks[j]()
        bj = j % 2
        W("t", "sV", v_cw_done[j])
        if j == 0:
            W("t", "sV", v_nkr)
        if j >= 2:
            W("t", "sV", tsc_vread[bj])
        OP("t", lambda e, j=j, bj=bj: e.transpose(
            psT[bj][0:8, 128:256], cwb_sb[:, j * 8:j * 8 + 8], identb_sb[:]),
           inc="sT")
        t_cwT = N("sT")
        W("v", "sT", t_cwT)
        OP("v", lambda e, bj=bj: e.tensor_copy(
            cwf_sb[0:8, bj * 128:bj * 128 + 128], psT[bj][0:8, 128:256]), inc="sV")
        v_cwf = N("sV")
        W("t", "sV", v_cwf)
        if j == 0:
            W("t", "sV", kvk_done[3])
        if j == 1:
            W("t", "sV", kvk_done[2])
        OP("t", lambda e, bj=bj: e.matmul(
            psKG[bj][:, 0:128], e8_sb[:], cwf_sb[0:8, bj * 128:bj * 128 + 128],
            start=True, stop=True), inc="sT")
        t_e8 = N("sT")
        W("v", "sT", t_e8)
        W("v", "sS", s_u)
        OP("v", lambda e, j=j, bj=bj: e.tensor_mul(
            t_sb[:, j * 128:j * 128 + 128], u_sb[:, j * 128:j * 128 + 128],
            psKG[bj][:, 0:128]), inc="sV")
        tsc_done[j] = N("sV")
        tsc_vread[bj] = tsc_done[j]
    for ch in pre_chunks[NOWN:]:
        ch()

    for qc in range(2):
        tgt = psC if qc == 0 else psD
        W("t", "sV", v_sq[qc])
        W("t", "sS", pre_evac[1][qc])
        OP("t", lambda e, qc=qc, p=tgt: e.matmul(
            p[0:2, 0:512], ones2c_sb[:], sqq_pre[qc][:, 0:512],
            start=True, stop=True), inc="sT")
        t_qnorm[qc] = N("sT")
        W("v", "sT", t_qnorm[qc])
        OP("v", lambda e, p=tgt: e.reduce_max(qnt[:], p[0:2, 0:512], axis=AX.X))
        DR("v")
        OP("v", lambda e: e.tensor_max(qacc[:, 0:1], qacc[:, 0:1], qnt[:]))
        DR("v")
        OP("v", lambda e: e.engine_nop(), inc="sV")
        v_qred[qc] = N("sV")
    OP("v", lambda e: e.tensor_copy(qpad[0:2, 0:1], qacc[:, 0:1]))
    DR("v")
    OP("v", lambda e: e.transpose(qpadT[:], qpad[:]))
    DR("v")
    OP("v", lambda e: e.reduce_max(mtmpA[:], qpadT[0:1, 0:2], axis=AX.X))
    DR("v")
    OP("v", lambda e: e.tensor_mul(mtmpA[:], mtmpA[:], kmaxrow_sb[0:1, 0:1]),
       inc="sV")
    v_m1 = N("sV")
    W("s", "sV", v_m1)
    OP("s", lambda e: e.activation(out=mtmpA[:], in_=mtmpA[:], func=AF.Ln))
    DR("s")
    OP("s", lambda e: e.activation(out=mtmpA[:], in_=mtmpA[:], func=AF.Exp,
                                   scale=0.5), inc="sS")
    s_m0 = N("sS")
    W("v", "sS", s_m0)
    OP("v", lambda e: e.tensor_scalar_mul(mtmpA[:], mtmpA[:], -0.1259765625),
       inc="sV")
    v_m2 = N("sV")
    W("g", "sV", v_m2)
    OP("g", lambda e: e.partition_broadcast(negm_sb[:, 0:1], mtmpA[0:1, 0:1]),
       inc="sG")
    mh_ready[0] = N("sG")

    DBG("xn_odd", xn_odd, list(xn_done[NT - 1]))
    DBG("Kp", Kp_sb, [K_done[NT - 1]])
    DBG("V", V_sb, [("sV", vav_done[NT - 1])])
    DBG("logits", logits_sb, [("sV", rope_done[NT - 1])])

    kta_total = N("sKTa")
    ktb_total = N("sKTb")
    p1_t_end = N("sT")
    p1_v_end = N("sV")
    p1_s_end = N("sS")
    psum1.close()
    p1.close()
    p01.close()

    p2 = ExitStack()
    PT = [alloc(p2, f"PT{i}", [128, 1024], BF16) for i in range(4)]
    attn_fm = alloc(p2, "attn_fm", [128, 16 * 1024], BF16)
    rlbc_sb = alloc(p2, "rlbc_sb", [64, 512], F32)
    lpad = alloc(p2, "lpad", [32, 512], F32)
    lT = alloc(p2, "lT", [32, 512], F32)
    rT = alloc(p2, "rT", [32, 512], F32)
    rrow = alloc(p2, "rrow", [32, 512], F32)
    fwbuf = [alloc(p12, f"fwbuf{i}", [128, 2048], BF16, side="right") for i in range(2)]
    qwbuf = [alloc(p12, f"qwbuf{i}", [128, 2048], BF16, side="right") for i in range(2)]
    sqq = [alloc(p12, f"sqq{i}", [128, 1024], BF16, side="right") for i in range(2)]
    qraw = [alloc(p12, f"qraw{i}", [128, 1024], BF16, side="right") for i in range(2)]
    qpart = alloc(p2, "qpart", [128, 1024], BF16)

    QR_PITCH = 8192

    psum2 = ExitStack()
    psS_ = [psum2.enter_context(nc.psum_tensor(f"psS{i}", [128, 1024], F32)) for i in range(2)]
    psAV = [psum2.enter_context(nc.psum_tensor(f"psAV{i}", [65, 512], F32))
            for i in range(2)]
    psQP = psum2.enter_context(nc.psum_tensor("psQP", [128, 512], F32))
    psDen = psum2.enter_context(nc.psum_tensor("psDen", [128, 512], F32))

    OP("v", lambda e: e.memset(lpad[:], 1.0))
    OP("v", lambda e: e.memset(rT[:], 1.0), inc="sV")
    v_rlinit = N("sV")

    psQP_vfree = [0]

    def emit_qw_dma(qc):
        sq_ = "sQa" if qc % 2 == 0 else "sQb"
        if qc < 2:
            W("g", "sT", p1_t_end)
            W("g", "sV", N("sV"))
            W("g", "sS", N("sS"))
        else:
            W("g", "sT", qproj_done[qc - 2])
        OP("g", lambda e, qc=qc: e.dma_start(
            out=qwbuf[qc % 2][:], in_=D["qw_s"][:, qc * 2048:(qc + 1) * 2048]),
           inc=sq_, incn=16)
        qw_ld[qc] = N(sq_)

    def gen_qpair(qc):
        h = qc // 2
        sq_ = "sQa" if qc % 2 == 0 else "sQb"
        qevac_half = [0, 0]
        for g2 in range(2):
            for u in range(4):
                if u == 0:
                    if g2 == 0:
                        W("t", sq_, qw_ld[qc])
                        if qc == 0:
                            W("t", "sT", p1_t_end)
                        if qc >= 1:
                            W("t", "sS", qevac_done[qc - 1])
                        W("t", "sV", psQP_vfree[0])
                    else:
                        W("t", "sS", qevac_half[0])
                for hc in range(4 * u, 4 * u + 4):
                    stat = qwbuf[qc % 2][:, hc * 128:hc * 128 + 128]
                    mov = xn_odd[:, hc * 1024 + g2 * 512: hc * 1024 + g2 * 512 + 512]
                    OP("t", lambda e, st=stat, m=mov, hc=hc: e.matmul(
                        psQP[:, 0:512], st, m,
                        start=(hc == 0), stop=(hc == NHC - 1)),
                       inc=("sT" if hc == NHC - 1 else None))
                if u == 3:
                    qp_half = N("sT")
                    if g2 == 1:
                        qproj_done[qc] = qp_half
                    W("s", "sT", qp_half)
                    if g2 == 0 and qc >= 2:
                        W("s", "sV", rope_done_q[qc - 2])
                    OP("s", lambda e, qc=qc, g2=g2: e.copy(
                        out=qraw[qc % 2][:, g2 * 512:g2 * 512 + 512],
                        in_=psQP[:, 0:512]), inc="sS")
                    qevac_half[g2] = N("sS")
                yield
        qevac_done[qc] = qevac_half[1]

        W("v", "sS", qevac_done[qc])
        if qc >= 2:
            W("v", "sT", t_qnorm[qc - 2])
        if h >= 4:
            W("v", "sT", scores_done[h - 4])
        for blk in range(4):
            OP("v", lambda e, qc=qc, b=blk: e.tensor_copy(
                qpart[b * 32:b * 32 + 32, :],
                qraw[qc % 2][(b ^ 1) * 32:(b ^ 1) * 32 + 32, :]))
        yield
        OP("v", lambda e, qc=qc: e.tensor_mul(
            sqq[qc % 2][:], qraw[qc % 2][:], qraw[qc % 2][:]))
        DR("v")
        OP("v", lambda e, qc=qc: e.tensor_max(
            sqq[qc % 2][:, 0:512], sqq[qc % 2][:, 0:512], sqq[qc % 2][:, 512:1024]),
           inc="sV")
        v_sq[qc] = N("sV")
        DR("v")
        yield
        OP("v", lambda e, qc=qc: e.tensor_mul(qraw[qc % 2][:], qraw[qc % 2][:],
                                              cosq_sb[:]))
        OP("v", lambda e: e.tensor_mul(qpart[:], qpart[:], sinq_sb[:]))
        DR("v")
        rofs_q = (h % 2) * 64
        colb = ((h % 4) // 2) * 4096
        for half in range(2):
            gl = 2 * (qc % 2) + half
            dst = qkv_sb[rofs_q:rofs_q + 64, colb + gl * 1024: colb + gl * 1024 + 1024]
            OP("v", lambda e, qc=qc, d=dst, hf=half: e.tensor_add(
                d, qraw[qc % 2][hf * 64:hf * 64 + 64, :],
                qpart[hf * 64:hf * 64 + 64, :]))
        DR("v")
        OP("v", lambda e: e.engine_nop(), inc="sV")
        rope_done_q[qc] = N("sV")
        yield

    def emit_norm(qc, evac_tgt):
        W("t", "sV", v_sq[qc])
        W("t", "sS", evac_tgt)
        W("t", "sV", psQP_vfree[0])
        OP("t", lambda e, qc=qc: e.matmul(
            psQP[0:2, 0:512], ones2c_sb[:], sqq[qc % 2][:, 0:512],
            start=True, stop=True), inc="sT")
        t_qnorm[qc] = N("sT")
        W("v", "sT", t_qnorm[qc])
        OP("v", lambda e: e.reduce_max(qnt[:], psQP[0:2, 0:512], axis=AX.X))
        DR("v")
        hcol = qc // 2
        OP("v", lambda e, hp=hcol: e.tensor_max(
            qacc[:, hp:hp + 1], qacc[:, hp:hp + 1], qnt[:]))
        DR("v")
        OP("v", lambda e: e.engine_nop(), inc="sV")
        v_qred = N("sV")
        psQP_vfree[0] = v_qred

    def emit_mhat(hh):
        DR("v")
        OP("v", lambda e, hh=hh: e.tensor_copy(qpad[0:2, 0:1], qacc[:, hh:hh + 1]))
        DR("v")
        OP("v", lambda e: e.transpose(qpadT[:], qpad[:]))
        DR("v")
        OP("v", lambda e: e.reduce_max(mtmpA[:], qpadT[0:1, 0:2], axis=AX.X))
        DR("v")
        OP("v", lambda e, hh=hh: e.tensor_mul(
            mtmpA[:], mtmpA[:], kmaxrow_sb[0:1, hh:hh + 1]), inc="sV")
        v_m1 = N("sV")
        W("s", "sV", v_m1)
        OP("s", lambda e: e.activation(out=mtmpA[:], in_=mtmpA[:], func=AF.Ln))
        DR("s")
        OP("s", lambda e: e.activation(out=mtmpA[:], in_=mtmpA[:], func=AF.Exp,
                                       scale=0.5), inc="sS")
        s_m = N("sS")
        W("v", "sS", s_m)
        OP("v", lambda e: e.tensor_scalar_mul(mtmpA[:], mtmpA[:], -0.1259765625),
           inc="sV")
        v_m2 = N("sV")
        W("g", "sV", v_m2)
        OP("g", lambda e, hh=hh: e.partition_broadcast(
            negm_sb[:, hh:hh + 1], mtmpA[0:1, 0:1]), inc="sG")
        mh_ready[hh] = N("sG")

    def gen_qblock(h):
        yield from gen_qpair(2 * h + 2)
        if 2 * h + 4 < 16:
            emit_qw_dma(2 * h + 4)
        yield from gen_qpair(2 * h + 3)
        if 2 * h + 5 < 16:
            emit_qw_dma(2 * h + 5)
        yield
        yield
        yield
        yield
        emit_norm(2 * h + 2, qevac_done[2 * h + 3])
        yield
        emit_norm(2 * h + 3, qevac_done[2 * h + 3])
        yield
        emit_mhat(h + 1)
        yield

    dense_done = [0] * 16
    evac_g = [None] * 16

    fw_ld = [0] * 16

    def emit_fw_dma(oc):
        sf = "sFa" if oc % 2 == 0 else "sFb"
        if oc >= 2:
            W("g", "sT", dense_done[oc - 2])
        else:
            W("g", "sT", p1_t_end)
            W("g", "sV", N("sV"))
            W("g", "sS", N("sS"))
        OP("g", lambda e, oc=oc: e.dma_start(
            out=fwbuf[oc % 2][:], in_=D["fw_s"][:, oc * 2048:(oc + 1) * 2048]),
           inc=sf, incn=16)
        fw_ld[oc] = N(sf)

    def dense_units():
        emit_fw_dma(0)
        emit_fw_dma(1)
        for oc in range(16):
            sf = "sFa" if oc % 2 == 0 else "sFb"
            v_half = [0, 0]
            for g2 in range(2):
                for u in range(8):
                    if u == 0:
                        if g2 == 0:
                            W("t", sf, fw_ld[oc])
                            if oc == 0:
                                W("t", "sV", tsc_done[NOWN - 1])
                            else:
                                W("t", evac_g[oc - 1][0], evac_g[oc - 1][1])
                        else:
                            W("t", v_half[0][0], v_half[0][1])
                    for hc in range(2 * u, 2 * u + 2):
                        stat = fwbuf[oc % 2][:, hc * 128:hc * 128 + 128]
                        mov = xn_odd[:, hc * 1024 + g2 * 512:
                                     hc * 1024 + g2 * 512 + 512]
                        OP("t", lambda e, st=stat, m=mov, hc=hc: e.matmul(
                            psDen[:, 0:512], st, m, start=(hc == 0), stop=False),
                           inc=("sT" if (hc == 15 and g2 == 1) else None))
                    yield
                if g2 == 1:
                    dense_done[oc] = N("sT")
                    if oc + 2 < 16:
                        emit_fw_dma(oc + 2)
                statB = bcat_sb[:, oc * 128:oc * 128 + 128]
                movB = t_sb[:, g2 * 512:g2 * 512 + 512]
                OP("t", lambda e, st=statB, m=movB: e.matmul(
                    psDen[:, 0:512], st, m, start=False, stop=True), inc="sT")
                t_b = N("sT")
                W("s", "sT", t_b)
                if oc in (8, 9):
                    W("s", "sT", t_qnorm[oc - 8])
                elif oc >= 12:
                    W("s", "sT", qproj_done[1])
                OP("s", lambda e, oc=oc, g2=g2: e.copy(
                    out=g_fm[:, oc * 1024 + g2 * 512: oc * 1024 + g2 * 512 + 512],
                    in_=psDen[:, 0:512]), inc="sS")
                v_half[g2] = ("sS", N("sS"))
                yield
                if g2 == 0:
                    yield
                    yield
            evac_g[oc] = v_half[1]

    dense_it = dense_units()
    dense_emitted = [0]
    DENSE_TOTAL = 16 * 20
    TOTAL_ITEMS = NKV * (NOWN * (NOWN + 1) // 2)

    exp_cnt_half = [0, 0]
    av_cnt_pt = [0, 0, 0, 0]
    mask_cnt = {}
    exp_item = {}
    evac_done = [0, 0]
    slot_par = {}
    av_last = {}
    first_item_of_head = [True] * NKV
    first_exp_of_head = [True] * NKV
    last_bc_box = [0]
    lpad_init_box = [v_rlinit]
    nslots = 0

    def emit_scores(i, it):
        h, j, sub = it
        p_pair = h // 2
        rofs = (h % 2) * 64
        half = i % 2
        W("t", "sS", exp_cnt_half[half])
        if first_item_of_head[h]:
            W("t", "sV", rope_done_q[2 * h + 1])
            if h == 0:
                W("t", "sKTa", kta_total)
                W("t", "sKTb", ktb_total)
            first_item_of_head[h] = False
        for ktl in range(2):
            kt = 2 * sub + ktl
            stat = Kp_sb[rofs:rofs + 64,
                         p_pair * S + kt * 128: p_pair * S + kt * 128 + 128]
            mov = AP(qkv_sb, (h % 2) * 64 * QR_PITCH + ((h % 4) // 2) * 4096 + j * 128,
                     [[QR_PITCH, 64], [1024, 4], [1, 128]])
            OP("t", lambda e, st=stat, m=mov, hf=half, k=ktl: e.matmul(
                psS_[hf][:, k * 512:k * 512 + 512], st, m, start=True, stop=True),
               inc=("sT" if ktl == 1 else None))
        t_sc = N("sT")
        scores_done[h] = t_sc
        W("s", "sT", t_sc)
        W("s", "sT", av_cnt_pt[i % 4])
        if first_exp_of_head[h]:
            W("s", "sG", mh_ready[h])
            first_exp_of_head[h] = False
        OP("s", lambda e, hf=half, pr=i % 4, h2=h: e.activation(
            out=PT[pr][:, 0:1024], in_=psS_[hf][:, 0:1024], func=AF.Exp,
            bias=negm_sb[:, h2:h2 + 1], scale=1.0), inc="sS")
        exp_cnt_half[half] = N("sS")
        exp_item[it] = exp_cnt_half[half]
        if sub == j:
            W("v", "sS", exp_item[it])
            OP("v", lambda e, pr=i % 4: e.tensor_scalar(
                out=PT[pr][:, 0:512], in0=PT[pr][:, 0:512],
                scalar1=flag_sb[:], scalar2=None, op0=ALU.mult))
            trirep = AP(trim_sb, 0, [[128, 128], [0, 4], [1, 128]])
            DR("v")
            OP("v", lambda e, pr=i % 4, tr=trirep: e.tensor_mul(
                PT[pr][:, 512:1024], PT[pr][:, 512:1024], tr), inc="sV")
            mask_cnt[(h, j)] = N("sV")

    def emit_av(i, it):
        h, j, sub = it
        L = 2 * j + 2
        par = slot_par[(h, j)]
        if sub == j:
            W("t", "sV", mask_cnt[(h, j)])
        else:
            W("t", "sS", exp_item[it])
        if sub == 0:
            W("t", "sV", evac_done[par])
        for ktl in range(2):
            kt = 2 * sub + ktl
            vstat = V_sb[:, (h * NT + kt) * 65:(h * NT + kt) * 65 + 65]
            OP("t", lambda e, vs=vstat, pr=i % 4, k=ktl, pa=par,
               st=(kt == 0), sp=(kt == L - 1): e.matmul(
                psAV[pa][:, 0:512], vs, PT[pr][:, k * 512:k * 512 + 512],
                start=st, stop=sp),
               inc=("sT" if ktl == 1 else None))
        av_cnt_pt[i % 4] = N("sT")

    def emit_slotend(h, j):
        par = slot_par[(h, j)]
        t_av = av_last[(h, j)]
        W("v", "sT", t_av)
        W("v", "sG", last_bc_box[0])
        W("v", "sV", lpad_init_box[0])
        OP("v", lambda e, pa=par: e.tensor_copy(lpad[0:1, 0:512],
                                                psAV[pa][64:65, 0:512]))
        DR("v")
        OP("v", lambda e: e.transpose(lT[:], lpad[:]))
        DR("v")
        OP("v", lambda e: e.reciprocal(AP(rT, 0, [[512, 32], [32, 16]]),
                                       AP(lT, 0, [[512, 32], [32, 16]])))
        DR("v")
        OP("v", lambda e: e.transpose(rrow[:], rT[:]), inc="sV")
        v_rl = N("sV")
        W("g", "sV", v_rl)
        OP("g", lambda e: e.partition_broadcast(rlbc_sb[:], rrow[0:1, 0:512]),
           inc="sG")
        last_bc_box[0] = N("sG")
        W("v", "sG", last_bc_box[0])
        for g_ in range(4):
            c = 2 * h + g_ // 2
            pofs = (g_ % 2) * 64
            OP("v", lambda e, g2=g_, c=c, po=pofs, j=j, pa=par: e.tensor_mul(
                attn_fm[po:po + 64, c * 1024 + j * 128: c * 1024 + j * 128 + 128],
                psAV[pa][0:64, g2 * 128:g2 * 128 + 128],
                rlbc_sb[0:64, g2 * 128:g2 * 128 + 128]),
               inc=("sV" if g_ == 3 else None))
        evac_done[par] = N("sV")

    from collections import deque
    pend = deque()

    def flush_av(all_=False):
        while len(pend) > (0 if all_ else 3):
            pi, pit = pend.popleft()
            emit_av(pi, pit)
            av_last[(pit[0], pit[1])] = N("sT")
            if pit[2] == pit[1]:
                emit_slotend(pit[0], pit[1])

    emit_qw_dma(2)
    emit_qw_dma(3)

    item_idx = 0
    for h in range(NKV):
        qgen = gen_qblock(h) if h <= 6 else None
        for j in range(NOWN):
            for sub in range(j + 1):
                it = (h, j, sub)
                if sub == 0:
                    slot_par[(h, j)] = nslots % 2
                    nslots += 1
                i = len(exp_item)
                flush_av()
                emit_scores(i, it)
                pend.append((i, it))
                if qgen is not None and next(qgen, "END") == "END":
                    qgen = None
                target = DENSE_TOTAL * (item_idx + 1) // TOTAL_ITEMS
                while dense_emitted[0] < target and \
                        next(dense_it, "END") != "END":
                    dense_emitted[0] += 1
                item_idx += 1
        while qgen is not None and next(qgen, "END") != "END":
            pass
    flush_av(all_=True)
    for _ in dense_it:
        pass
    v_attn_done = N("sV")

    DBG("attn", attn_fm, [("sV", v_attn_done)])
    DBG("g", g_fm, [evac_g[15]])

    p2_t_end = N("sT")
    psum2.close()
    p12.close()

    p4 = ExitStack()
    owbuf = [alloc(p4, f"owbuf{i}", [128, 2048], BF16) for i in range(2)]
    out_sb = [alloc(p4, f"out_sb{i}", [128, HID], F32) for i in range(2)]
    xf32 = [alloc(p4, f"xf32_{i}", [128, HID], F32, side="right") for i in range(4)]

    psum4 = ExitStack()
    psO = [psum4.enter_context(nc.psum_tensor(f"psO{i}", [128, 1024], F32)) for i in range(2)]
    psG = [psum4.enter_context(nc.psum_tensor(f"psG{i}", [128, 2048], BF16)) for i in range(2)]

    silu_done = [0] * 16
    for oc in range(16):
        W("s", evac_g[oc][0], evac_g[oc][1])
        OP("s", lambda e, oc=oc: e.activation(
            out=g_fm[:, oc * 1024:(oc + 1) * 1024],
            in_=g_fm[:, oc * 1024:(oc + 1) * 1024], func=AF.Silu), inc="sS")
        silu_done[oc] = N("sS")

    sXF = ["sXFa", "sXFb", "sXFc", "sXFd"]
    x_ld = [0] * NOWN
    comb_done = [0] * NOWN
    out_dma_cnt = [0] * NOWN

    def emit_xload(j):
        if j < 4:
            W("y", "sT", p2_t_end)
            W("y", "sV", v_attn_done)
        else:
            W("y", "sV", comb_done[j - 4])
        OP("y", lambda e, j=j: e.dma_start(
            out=xf32[j % 4][:], in_=D["x_kv"][(2 * j + 1) * 128:(2 * j + 2) * 128, :]),
           inc=sXF[j % 4], incn=16)
        x_ld[j] = N(sXF[j % 4])

    for j in range(4):
        emit_xload(j)

    oproj_done = [0] * 16
    sg_done = [0] * 16
    psG_free = [0, 0]
    for oc in range(16):
        so_ = "sOWa" if oc % 2 == 0 else "sOWb"
        if oc >= 2:
            W("g", "sT", oproj_done[oc - 2])
        else:
            W("g", "sV", rope_done_q[15])
            W("g", "sT", t_qnorm[15])
            W("g", "sS", qevac_done[15])
        OP("g", lambda e, oc=oc: e.dma_start(
            out=owbuf[oc % 2][:], in_=D["ow_s"][:, oc * 2048:(oc + 1) * 2048]),
           inc=so_, incn=16)
        ow_ld = N(so_)

        W("t", so_, ow_ld)
        if oc == 0:
            W("t", "sS", silu_done[0])
            W("t", "sV", v_attn_done)
            W("t", "sT", p2_t_end)
        if oc >= 2:
            W("t", "sV", sg_done[oc - 2])
        for ac in range(16):
            stat = owbuf[oc % 2][:, ac * 128:ac * 128 + 128]
            for g2 in range(2):
                mov = attn_fm[:, ac * 1024 + g2 * 512: ac * 1024 + g2 * 512 + 512]
                OP("t", lambda e, st=stat, m=mov, g2=g2, oc=oc, ac=ac: e.matmul(
                    psO[oc % 2][:, g2 * 512:g2 * 512 + 512], st, m,
                    start=(ac == 0), stop=(ac == 15)),
                   inc=("sT" if (ac == 15 and g2 == 1) else None))
        oproj_done[oc] = N("sT")
        W("v", "sT", oproj_done[oc])
        W("v", "sS", silu_done[oc])
        OP("v", lambda e, oc=oc: e.tensor_mul(
            g_fm[:, oc * 1024:(oc + 1) * 1024], psO[oc % 2][:, 0:1024],
            g_fm[:, oc * 1024:(oc + 1) * 1024]), inc="sV")
        sg_done[oc] = N("sV")
        oc2 = oc - 1
        if oc2 >= 0:
            W("t", "sV", sg_done[oc2])
            for j01 in range(2):
                OP("t", lambda e, oc2=oc2, j=j01: e.transpose(
                    psG[j][:, oc2 * 128:oc2 * 128 + 128],
                    g_fm[:, oc2 * 1024 + j * 128: oc2 * 1024 + j * 128 + 128],
                    identb_sb[:]))

    for j in range(NOWN):
        W("t", "sV", sg_done[15])
        W("t", "sV", psG_free[j % 2])
        ocs = [15] if j < 2 else list(range(16))
        for k2, oc in enumerate(ocs):
            OP("t", lambda e, oc=oc, j=j: e.transpose(
                psG[j % 2][:, oc * 128:oc * 128 + 128],
                g_fm[:, oc * 1024 + j * 128: oc * 1024 + j * 128 + 128],
                identb_sb[:]),
               inc=("sT" if k2 == len(ocs) - 1 else None))
        t_tr4 = N("sT")
        W("v", "sT", t_tr4)
        W("v", sXF[j % 4], x_ld[j])
        if j >= 2:
            W("v", "sOa" if j % 2 == 0 else "sOb", out_dma_cnt[j - 2])
        OP("v", lambda e, j=j: e.tensor_add(
            out_sb[j % 2][:], psG[j % 2][:, 0:2048], xf32[j % 4][:]), inc="sV")
        comb_done[j] = N("sV")
        psG_free[j % 2] = comb_done[j]
        W("y", "sV", comb_done[j])
        sout = "sOa" if j % 2 == 0 else "sOb"
        OP("y", lambda e, j=j: e.dma_start(
            out=out_d[j * 128:(j + 1) * 128, :], in_=out_sb[j % 2][:]),
           inc=sout, incn=16)
        out_dma_cnt[j] = N(sout)
        if j + 4 < NOWN:
            emit_xload(j + 4)

    W("y", "sOa", N("sOa"))
    W("y", "sOb", N("sOb"))

    psum4.close()
    p4.close()
    p2.close()

    with nc.Block() as blk:
        @blk.tensor
        def _(e):
            for f in OPS["t"]:
                f(e)

        @blk.vector
        def _(e):
            for f in OPS["v"]:
                f(e)

        @blk.scalar
        def _(e):
            for f in OPS["s"]:
                f(e)

        @blk.gpsimd
        def _(e):
            for f in OPS["g"]:
                f(e)

        @blk.sync
        def _(e):
            for f in OPS["y"]:
                f(e)

    ctx.close()
    lower_extended_insts(nc)
    return nc


_NC_CACHE = {}


def _get_nc(dbg=()):
    key = tuple(sorted(dbg))
    if key not in _NC_CACHE:
        _NC_CACHE[key] = _build(key)
    return _NC_CACHE[key]


def kernel(**inputs):
    in_maps = _host_prepare(inputs)
    nc = _get_nc()
    res = run_bass_kernel_spmd(nc, in_maps, core_ids=list(range(8)))
    odt = np.result_type(np.float32, res.results[0]["out"].dtype)
    out = np.empty((4, S, HID), dtype=odt)
    for core in range(8):
        seq, b = core % 4, core // 4
        o = res.results[core]["out"]
        for j in range(NOWN):
            out[seq, (2 * j + b) * 128:(2 * j + b + 1) * 128, :] = o[j * 128:(j + 1) * 128, :]
    return out


# version 36
# speedup vs baseline: 1.0185x; 1.0174x over previous
import numpy as np
import ml_dtypes
import concourse.bass as bass
import concourse.mybir as mybir
from concourse.bass import AP
from concourse.bass_utils import run_bass_kernel_spmd
from concourse.library_config import attn as _ATTN_LIB
from concourse.library_overlay import lower_extended_insts
from contextlib import ExitStack

F32 = mybir.dt.float32
BF16 = mybir.dt.bfloat16
AF = mybir.ActivationFunctionType
ALU = mybir.AluOpType
AX = mybir.AxisListType
BFNP = ml_dtypes.bfloat16

S = 2048
HID = 2048
NHC = 16
NT = 16
NOWN = 8
NKV = 8
THETA = 10000.0
NEG_BIG = -1.0e30
EPS = 1e-5
QKV_PITCH = 2 * 4 * 1024


def _tile_fm(wt):
    K, N = wt.shape
    return np.ascontiguousarray(wt.reshape(K // 128, 128, N).transpose(1, 0, 2).reshape(128, -1))


def _tile_stream(wt, ob=128):
    K, N = wt.shape
    return np.ascontiguousarray(
        wt.reshape(K // 128, 128, N // ob, ob).transpose(1, 2, 0, 3).reshape(128, -1))


def _storage_order(b):
    return [s ^ 1 if b == 0 else s for s in range(NT)]


def _host_prepare(inputs):
    x = np.asarray(inputs["L0_attn_hidden_states"], dtype=np.float32)
    q_w = np.asarray(inputs["L0_attn_q_w"], dtype=np.float32)
    kr_w = np.asarray(inputs["L0_attn_kr_w"], dtype=np.float32)
    down_w = np.asarray(inputs["L0_attn_down_w"], dtype=np.float32)
    upk_w = np.asarray(inputs["L0_attn_upk_w"], dtype=np.float32)
    upv_w = np.asarray(inputs["L0_attn_upv_w"], dtype=np.float32)
    o_w = np.asarray(inputs["L0_attn_o_w"], dtype=np.float32)
    ffn_w = np.asarray(inputs["L0_ffn_weight"], dtype=np.float32)
    gate_w = np.asarray(inputs["L0_ffn_gate_w"], dtype=np.float32)
    A_w = np.asarray(inputs["L0_ffn_A_w"], dtype=np.float32)
    B_w = np.asarray(inputs["L0_ffn_B_w"], dtype=np.float32)

    shared = {
        "qw_s": _tile_stream(q_w.T, 128),
        "fw_s": _tile_stream(ffn_w.T, 128),
        "ow_s": _tile_stream(o_w.T, 128),
        "down_t": _tile_fm(down_w.T),
        "upk_t": _tile_fm(upk_w.T),
        "upv_t": _tile_fm(upv_w.T),
        "krg_t": _tile_fm(np.concatenate([kr_w.T, gate_w.T], axis=1)),
        "acat_t": _tile_fm(A_w.reshape(128, HID).T),
        "bcat_t": np.ascontiguousarray(B_w.transpose(0, 2, 1).reshape(128, HID)),
        "trimask": np.ascontiguousarray(np.tril(np.ones((128, 128), dtype=BFNP)).T),
        "ones2c": np.concatenate([
            np.concatenate([np.ones((64, 1)), np.zeros((64, 1))], 0),
            np.concatenate([np.zeros((64, 1)), np.ones((64, 1))], 0)], 1).astype(BFNP),
        "e8": np.ascontiguousarray(np.repeat(np.eye(8, dtype=BFNP), 16, axis=1)),
        "identb": np.eye(128, dtype=BFNP),
    }

    import jax as _jax
    import jax.numpy as _jnp
    with _jax.default_device(_jax.devices("cpu")[0]):
        _pos = _jnp.arange(S, dtype=_jnp.float32)
        _invf = 1.0 / THETA ** (_jnp.arange(0, 64, 2, dtype=_jnp.float32) / 64)
        _freqs = _pos[:, None] * _invf[None, :]
        COS_T = np.asarray(_jnp.cos(_freqs), dtype=np.float32)
        SIN_T = np.asarray(_jnp.sin(_freqs), dtype=np.float32)

    in_maps = []
    for core in range(8):
        seq, b = core % 4, core // 4
        order = _storage_order(b)
        xs = x[seq].reshape(NT, 128, HID)
        x_kv = np.ascontiguousarray(xs[order].reshape(S, HID))

        pos_own = np.concatenate(
            [np.arange((2 * j + b) * 128, (2 * j + b + 1) * 128) for j in range(NOWN)]
        )
        d = np.arange(128) % 64
        sgn = np.where(d < 32, -1.0, 1.0).astype(np.float32)[:, None]
        cosq = (COS_T[pos_own][:, d % 32].T * np.float32(0.125))
        sinq = (SIN_T[pos_own][:, d % 32].T * np.float32(0.125) * sgn)

        cosk = np.empty((128, NT * 32), dtype=np.float32)
        sink = np.empty((128, NT * 32), dtype=np.float32)
        c = np.arange(32)
        sgnk = np.where(c < 16, -1.0, 1.0).astype(np.float32)[None, :]
        for s_ in range(NT):
            pos = order[s_] * 128 + np.arange(128)
            cosk[:, s_ * 32:(s_ + 1) * 32] = COS_T[pos][:, c % 16]
            sink[:, s_ * 32:(s_ + 1) * 32] = SIN_T[pos][:, c % 16] * sgnk

        flag = np.full((128, 1), 0.0 if b == 0 else 1.0, dtype=np.float32)
        m = {"x_kv": x_kv, "cosq": np.ascontiguousarray(cosq),
             "sinq": np.ascontiguousarray(sinq), "cosk": cosk, "sink": sink,
             "flag": flag}
        m.update(shared)
        in_maps.append(m)
    return in_maps


def _build(dbg=()):
    nc = bass.Bass("TRN2", target_bir_lowering=False)

    D = {}
    for name, shape, dt in [
        ("x_kv", [S, HID], F32),
        ("qw_s", [128, 16 * 2048], F32),
        ("fw_s", [128, 16 * 2048], F32),
        ("ow_s", [128, 16 * 2048], F32),
        ("down_t", [128, 16 * 256], F32),
        ("upk_t", [128, 2 * 256], F32),
        ("upv_t", [128, 2 * 512], F32),
        ("krg_t", [128, 16 * 264], F32),
        ("acat_t", [128, 16 * 128], F32),
        ("bcat_t", [128, HID], F32),
        ("trimask", [128, 128], BF16),
        ("ones2c", [128, 2], BF16),
        ("e8", [8, 128], BF16),
        ("identb", [128, 128], BF16),
        ("cosq", [128, 1024], F32),
        ("sinq", [128, 1024], F32),
        ("cosk", [128, NT * 32], F32),
        ("sink", [128, NT * 32], F32),
        ("flag", [128, 1], F32),
    ]:
        D[name] = nc.dram_tensor(name, shape, dt, kind="ExternalInput").ap()
    out_d = nc.dram_tensor("out", [NOWN * 128, HID], F32, kind="ExternalOutput").ap()

    OPS = {"t": [], "v": [], "s": [], "g": [], "y": []}
    CNT = {}
    SEMS = {}

    def I(sem, n=1):
        CNT[sem] = CNT.get(sem, 0) + n
        return CNT[sem]

    def N(sem):
        return CNT.get(sem, 0)

    def W(eng, sem, tgt):
        if tgt and tgt > 0:
            OPS[eng].append(lambda e, s=sem, t=tgt: e.wait_ge(SEMS[s], t))

    def OP(eng, fn, inc=None, incn=1):
        if inc is None:
            OPS[eng].append(fn)
            return None
        v = I(inc, incn)
        OPS[eng].append(lambda e, f=fn, s=inc, n=incn: f(e).then_inc(SEMS[s], n))
        return v

    def DR(eng):
        OPS[eng].append(lambda e: e.drain())

    def DBG(name, tensor, waits):
        if name not in dbg:
            return
        od = nc.dram_tensor("dbg_" + name, list(tensor.shape), tensor.dtype,
                            kind="ExternalOutput").ap()
        for eng_sem, tgt in waits:
            W("y", eng_sem, tgt)
        OP("y", lambda e, o=od, t2=tensor: e.dma_start(out=o[:], in_=t2[:]),
           inc="sDbg", incn=16)

    ctx = ExitStack()

    def alloc(st, name, shape, dt, side=None):
        if side:
            return st.enter_context(nc.sbuf_tensor(name, shape, dt, side=side))
        return st.enter_context(nc.sbuf_tensor(name, shape, dt))

    acat_sb = alloc(ctx, "acat_sb", [128, 16 * 128], BF16)
    bcat_sb = alloc(ctx, "bcat_sb", [128, HID], BF16)
    cosq_sb = alloc(ctx, "cosq_sb", [128, 1024], BF16)
    sinq_sb = alloc(ctx, "sinq_sb", [128, 1024], BF16)
    trim_sb = alloc(ctx, "trim_sb", [128, 128], BF16)
    ones2c_sb = alloc(ctx, "ones2c_sb", [128, 2], BF16)
    e8_sb = alloc(ctx, "e8_sb", [8, 128], BF16)
    identb_sb = alloc(ctx, "identb_sb", [128, 128], BF16)
    flag_sb = alloc(ctx, "flag_sb", [128, 1], F32)
    xn_odd = alloc(ctx, "xn_odd", [128, NHC * 1024], BF16)
    logits_sb = alloc(ctx, "logits_sb", [128, 64], F32)
    kmax8_sb = alloc(ctx, "kmax8_sb", [8, 1], F32)
    qmax8_sb = alloc(ctx, "qmax8_sb", [8, 1], F32)
    mtmp8_sb = alloc(ctx, "mtmp8_sb", [8, 1], F32)
    mtmp8b_sb = alloc(ctx, "mtmp8b_sb", [8, 1], BF16)
    negmrow_sb = alloc(ctx, "negmrow_sb", [1, 8], F32)
    negm_sb = alloc(ctx, "negm_sb", [128, 8], F32)
    kmax8b_sb = alloc(ctx, "kmax8b_sb", [8, 1], BF16)
    kmaxrow_sb = alloc(ctx, "kmaxrow_sb", [1, 8], F32)
    u_sb = alloc(ctx, "u_sb", [128, 1024], BF16)
    t_sb = alloc(ctx, "t_sb", [128, 1024], BF16)
    cwb_sb = alloc(ctx, "cwb_sb", [128, 64], BF16)
    cwf_sb = alloc(ctx, "cwf_sb", [8, 256], BF16)
    gs1 = alloc(ctx, "gs1", [128, 8], F32)
    gs2 = alloc(ctx, "gs2", [128, 8], F32)
    gm1 = alloc(ctx, "gm1", [128, 1], F32)
    gm1n = alloc(ctx, "gm1n", [128, 1], F32)
    gm2 = alloc(ctx, "gm2", [128, 1], F32)
    gden = alloc(ctx, "gden", [128, 1], F32)
    qacc = alloc(ctx, "qacc", [2, 8], F32)
    qnt = alloc(ctx, "qnt", [2, 1], F32)
    mtmpA = alloc(ctx, "mtmpA", [1, 1], F32)
    qpad = alloc(ctx, "qpad", [32, 32], F32)
    qpadT = alloc(ctx, "qpadT", [32, 32], F32)
    g_fm = alloc(ctx, "g_fm", [128, 16 * 1024], BF16)

    p12 = ExitStack()
    p01 = ExitStack()
    Kp_sb = alloc(p12, "Kp_sb", [128, 4 * S], BF16, side="right")
    V_sb = alloc(p12, "V_sb", [128, NKV * NT * 65], BF16, side="right")
    qkv_sb = alloc(p12, "qkv_sb", [128, QKV_PITCH], BF16, side="right")
    down_sb = alloc(p01, "down_sb", [128, 16 * 256], BF16, side="right")
    upk_sb = alloc(p01, "upk_sb", [128, 2 * 256], BF16, side="right")
    upv_sb = alloc(p01, "upv_sb", [128, 2 * 512], BF16, side="right")
    krg_sb = alloc(p01, "krg_sb", [128, 16 * 264], BF16, side="right")
    cosk_sb = alloc(p01, "cosk_sb", [128, NT * 32], F32, side="right")
    sink_sb = alloc(p01, "sink_sb", [128, NT * 32], F32, side="right")

    sem_names = ["sWsm", "sTbl", "sXa", "sXb", "sQa", "sQb", "sFa", "sFb",
                 "sOWa", "sOWb", "sXFa", "sXFb", "sXFc", "sXFd", "sOa", "sOb",
                 "sKTa", "sKTb", "sDbg", "sT", "sV", "sS", "sG"]
    for n_, h_ in zip(sem_names, [ctx.enter_context(nc.semaphore(n_)) for n_ in sem_names]):
        SEMS[n_] = h_

    OP("g", lambda e: e.load_library(_ATTN_LIB))
    for dst, src in [(down_sb, "down_t"), (upk_sb, "upk_t"), (upv_sb, "upv_t"),
                     (krg_sb, "krg_t"), (acat_sb, "acat_t"), (bcat_sb, "bcat_t")]:
        OP("g", lambda e, d=dst, s=src: e.dma_start(out=d[:], in_=D[s]), inc="sWsm", incn=16)
    for dst, src in [(trim_sb, "trimask"), (ones2c_sb, "ones2c"),
                     (e8_sb, "e8"), (identb_sb, "identb"), (flag_sb, "flag"),
                     (cosk_sb, "cosk"), (sink_sb, "sink")]:
        OP("y", lambda e, d=dst, s=src: e.dma_start(out=d[:], in_=D[s]), inc="sTbl", incn=16)
    for dst, src in [(cosq_sb, "cosq"), (sinq_sb, "sinq")]:
        OP("g", lambda e, d=dst, s=src: e.dma_start(out=d[:], in_=D[s]), inc="sWsm", incn=16)
    w_smallw = N("sWsm")
    w_tables = N("sTbl")

    p1 = ExitStack()
    xbuf = [alloc(p1, f"xbuf{i}", [128, HID], BF16, side="right") for i in range(2)]
    xn_grp = alloc(p1, "xn_grp", [128, NHC * 512], BF16, side="right")
    kvK = [alloc(p1, f"kvK{t}", [128, 512], BF16, side="right") for t in range(4)]
    ckv_sb = alloc(p1, "ckv_sb", [128, 1024], BF16, side="right")
    krt1 = alloc(p1, "krt1", [128, 256], BF16, side="right")
    krt2 = alloc(p1, "krt2", [128, 256], BF16, side="right")
    sqb = [alloc(p1, f"sqb{i}", [128, HID], BF16, side="right") for i in range(3)]
    stat1 = alloc(p1, "stat1", [128, 24], F32, side="right")
    stat2 = alloc(p1, "stat2", [128, 2], F32, side="right")
    stat3 = alloc(p1, "stat3", [128, 1], F32, side="right")
    stat4 = alloc(p1, "stat4", [128, 1], F32, side="right")
    ksq_sb = alloc(p1, "ksq_sb", [128, 512], BF16, side="right")
    kn_acc = alloc(p1, "kn_acc", [128, 8], BF16, side="right")
    kntmp = alloc(p1, "kntmp", [128, 8], F32, side="right")

    psum1 = ExitStack()
    psT = [psum1.enter_context(nc.psum_tensor(f"psT{i}", [128, 1024], BF16)) for i in range(4)]
    psKG = [psum1.enter_context(nc.psum_tensor(f"psKG{i}", [128, 264], F32)) for i in range(2)]
    psC = psum1.enter_context(nc.psum_tensor("psC", [128, 512], F32))
    psD = psum1.enter_context(nc.psum_tensor("psD", [128, 512], F32))

    OP("v", lambda e: e.memset(
        AP(V_sb, 64, [[NKV * NT * 65, 128], [65, NKV * NT], [1, 1]]), 1.0))
    OP("v", lambda e: e.memset(qacc[:], 0.0))
    OP("v", lambda e: e.memset(qpad[:], 0.0))
    OP("v", lambda e: e.memset(kn_acc[:], 0.0), inc="sV")
    v_ones = N("sV")

    qproj_done = [0] * 16
    qevac_done = [0] * 16
    rope_done_q = [0] * 16
    v_sq = [0] * 16
    t_qnorm = [0] * 16
    v_qred = [0] * 16
    qw_ld = [0] * 16
    mh_ready = [0] * NKV
    scores_done = [0] * NKV

    ln_done = [0] * NT
    vstats_done = [0] * NT
    stat2_free = [0] * NT
    xn_done = [[] for _ in range(NT)]
    xnT_done = [0] * NT
    rope_done = [0] * NT
    psT_free = [None, None, None, None]
    psKG_free = [0, 0]
    kvk_done = [0] * 4
    kvKT_done = [None, None, None, None]
    vav_done = [0] * NT
    K_done = [0] * NT
    t_upkv_last = [0]
    t_down_last = [0]
    s_ckv_box = [0]

    def tile_of(s_):
        g_, t_ = divmod(s_, 4)
        if t_ % 2 == 1:
            return ("odd", 2 * g_ + (t_ // 2))
        return ("grp", t_ // 2)

    def xn_stat_ap(s_, hc):
        kind, idx = tile_of(s_)
        if kind == "odd":
            return xn_odd[:, hc * 1024 + idx * 128: hc * 1024 + idx * 128 + 128]
        return xn_grp[:, hc * 512 + idx * 128: hc * 512 + idx * 128 + 128]

    W("t", "sTbl", w_tables)
    W("t", "sWsm", w_smallw)

    def p1_head_front(s_):
        g, t = divmod(s_, 4)
        sx = ["sXa", "sXb"][s_ % 2]
        if s_ >= 2:
            W("g", "sV", vstats_done[s_ - 2])
            W("g", "sS", ln_done[s_ - 2])
        OP("g", lambda e, d=xbuf[s_ % 2], s2=s_: e.dma_start(
            out=d[:], in_=D["x_kv"][s2 * 128:(s2 + 1) * 128, :]), inc=sx, incn=16)
        w_x = N(sx)

        W("v", sx, w_x)
        xb = xbuf[s_ % 2]
        for i4 in range(4):
            OP("v", lambda e, x=xb, i4=i4: e.bn_stats(
                stat1[:, i4 * 6:(i4 + 1) * 6], x[:, i4 * 512:(i4 + 1) * 512]))
        DR("v")
        if s_ >= 1:
            W("v", "sS", stat2_free[s_ - 1])
        OP("v", lambda e: e.bn_aggr(stat2[:], stat1[:]))
        DR("v")
        OP("v", lambda e: e.tensor_scalar_add(stat2[:, 1:2], stat2[:, 1:2], EPS),
           inc="sV")
        vstats_done[s_] = N("sV")
        W("s", "sV", vstats_done[s_])
        if s_ >= 3:
            W("s", "sT", xnT_done[s_ - 3])
        OP("s", lambda e: e.activation(out=stat3[:], in_=stat2[:, 1:2], func=AF.Ln))
        DR("s")
        OP("s", lambda e: e.activation(out=stat3[:], in_=stat3[:], func=AF.Exp,
                                       scale=-0.5))
        DR("s")
        OP("s", lambda e: e.activation(out=stat4[:], in_=stat2[:, 0:1],
                                       func=AF.Copy, scale=stat3[:, 0:1]), inc="sS")
        stat2_free[s_] = N("sS")
        DR("s")
        OP("s", lambda e: e.activation(out=stat4[:], in_=stat4[:],
                                       func=AF.Copy, scale=-1.0))
        DR("s")
        OP("s", lambda e, x=xb, s2=s_: e.activation(
            out=sqb[s2 % 3][:], in_=x[:], func=AF.Identity,
            scale=stat3[:, 0:1], bias=stat4[:, 0:1]), inc="sS")
        ln_done[s_] = N("sS")
        DR("s")

    def p1_head_tr(s_):
        g, t = divmod(s_, 4)
        W("t", "sS", ln_done[s_])
        kind, idx = tile_of(s_)
        t_tr = 0
        for batch in range(2):
            slot = (s_ % 2) * 2 + batch
            if psT_free[slot]:
                W("t", psT_free[slot][0], psT_free[slot][1])
            for k in range(8):
                hc = batch * 8 + k
                OP("t", lambda e, hc=hc, b=slot, s2=s_: e.transpose(
                    psT[b][:, (hc % 8) * 128:(hc % 8) * 128 + 128],
                    sqb[s2 % 3][:, hc * 128:hc * 128 + 128], identb_sb[:]),
                   inc=("sT" if k == 7 else None))
            t_tr = N("sT")
            ee = "v" if batch == 0 else "s"
            W(ee, "sT", t_tr)
            if kind == "grp" and g >= 1:
                W(ee, "sT", t_down_last[0])
            if kind == "odd":
                dst = AP(xn_odd, (batch * 8) * 1024 + idx * 128,
                         [[NHC * 1024, 128], [1024, 8], [1, 128]])
            else:
                dst = AP(xn_grp, (batch * 8) * 512 + idx * 128,
                         [[NHC * 512, 128], [512, 8], [1, 128]])
            if batch == 0:
                OP("v", lambda e, d=dst, b=slot: e.tensor_copy(d, psT[b][:, 0:1024]),
                   inc="sV")
                psT_free[slot] = ("sV", N("sV"))
            else:
                OP("s", lambda e, d=dst, b=slot: e.copy(out=d, in_=psT[b][:, 0:1024]),
                   inc="sS")
                psT_free[slot] = ("sS", N("sS"))
        xn_done[s_] = [psT_free[(s_ % 2) * 2], psT_free[(s_ % 2) * 2 + 1]]
        xnT_done[s_] = t_tr

    v_cw_done = [0] * NOWN

    def emit_gate_softmax(j):
        lg = logits_sb[:, j * 8:j * 8 + 8]
        DR("v")
        OP("v", lambda e, l=lg: e.reduce_max(gm1[:], l, axis=AX.X))
        OP("v", lambda e, l=lg: e.reduce_max(gm1n[:], l, axis=AX.X, negate=True))
        DR("v")
        OP("v", lambda e, l=lg: e.tensor_scalar(out=gs1[:], in0=l, scalar1=gm1[:],
                                                scalar2=None, op0=ALU.is_ge))
        DR("v")
        OP("v", lambda e, l=lg: e.scalar_tensor_tensor(
            out=gs2[:], in0=gs1[:], scalar=NEG_BIG, in1=l, op0=ALU.mult, op1=ALU.add))
        DR("v")
        OP("v", lambda e: e.reduce_max(gm2[:], gs2[:], axis=AX.X))
        DR("v")
        OP("v", lambda e, l=lg: e.tensor_scalar(out=gs2[:], in0=l, scalar1=gm2[:],
                                                scalar2=None, op0=ALU.is_ge), inc="sV")
        v_g1 = N("sV")
        W("s", "sV", v_g1)
        OP("s", lambda e, l=lg: e.activation(out=gs1[:], in_=l, func=AF.Exp,
                                             bias=gm1n[:]), inc="sS")
        s_g = N("sS")
        W("v", "sS", s_g)
        OP("v", lambda e: e.tensor_mul(gs1[:], gs1[:], gs2[:]))
        DR("v")
        OP("v", lambda e: e.reduce_sum(gden[:], gs1[:], axis=AX.X))
        DR("v")
        OP("v", lambda e: e.reciprocal(gden[:], gden[:]))
        DR("v")
        OP("v", lambda e, j=j: e.tensor_scalar(
            out=cwb_sb[:, j * 8:j * 8 + 8], in0=gs1[:], scalar1=gden[:],
            scalar2=16.0, op0=ALU.mult, op1=ALU.mult), inc="sV")
        v_cw_done[j] = N("sV")

    def p1_krgate_rope(s_):
        g, t = divmod(s_, 4)
        kind, idx = tile_of(s_)
        for se2, v2 in xn_done[s_]:
            W("t", se2, v2)
        W("t", "sV", psKG_free[s_ % 2])
        for hc in range(NHC):
            OP("t", lambda e, hc=hc, s2=s_: e.matmul(
                psKG[s2 % 2][:, 0:264], xn_stat_ap(s2, hc),
                krg_sb[:, hc * 264:hc * 264 + 264],
                start=(hc == 0), stop=(hc == NHC - 1)),
               inc=("sT" if hc == NHC - 1 else None))
        krg_done = N("sT")

        W("v", "sT", krg_done)
        W("v", "sTbl", w_tables)
        if kvKT_done[t]:
            W("v", kvKT_done[t][0], kvKT_done[t][1])
        pk = psKG[s_ % 2]
        cosrep = AP(cosk_sb, s_ * 32, [[NT * 32, 128], [0, 8], [1, 32]])
        sinrep = AP(sink_sb, s_ * 32, [[NT * 32, 128], [0, 8], [1, 32]])
        OP("v", lambda e, p=pk, c=cosrep: e.tensor_mul(krt1[:], p[:, 0:256], c))
        shuf = AP(pk, 16, [[264, 128], [32, 8], [-16, 2], [1, 16]])
        OP("v", lambda e, sh=shuf, sr=sinrep: e.tensor_mul(krt2[:], sh, sr))
        DR("v")
        kdst = AP(kvK[t], 0, [[512, 128], [64, 8], [32, 2], [1, 16]])
        OP("v", lambda e, d=kdst: e.tensor_add(d, krt1[:], krt2[:]))
        if kind == "odd":
            OP("v", lambda e, p=pk, j=idx: e.tensor_copy(
                logits_sb[:, j * 8:j * 8 + 8], p[:, 256:264]))
        OP("v", lambda e: e.engine_nop(), inc="sV")
        rope_done[s_] = N("sV")
        psKG_free[s_ % 2] = rope_done[s_]
        if kind == "odd":
            emit_gate_softmax(idx)

    def p1_down(g):
        for s_ in range(4 * g, 4 * g + 4):
            for se2, v2 in xn_done[s_]:
                W("t", se2, v2)
        for cc in range(2):
            if g >= 1:
                W("t", "sV", vav_done[4 * g - 2 + cc])
            tgt = psC if cc == 0 else psD
            for hc in range(NHC):
                stat = down_sb[:, hc * 256 + cc * 128: hc * 256 + cc * 128 + 128]
                mov_o = xn_odd[:, hc * 1024 + 2 * g * 128: hc * 1024 + 2 * g * 128 + 256]
                mov_g = xn_grp[:, hc * 512: hc * 512 + 256]
                OP("t", lambda e, p=tgt, st=stat, m=mov_o, hc=hc: e.matmul(
                    p[:, 0:256], st, m, start=(hc == 0), stop=False))
                OP("t", lambda e, p=tgt, st=stat, m=mov_g, hc=hc: e.matmul(
                    p[:, 256:512], st, m, start=False, stop=(hc == NHC - 1)),
                   inc=("sT" if hc == NHC - 1 else None))
        t_down_last[0] = N("sT")

        W("s", "sT", t_down_last[0])
        if g >= 1:
            W("s", "sT", t_upkv_last[0])
        OP("s", lambda e: e.copy(out=ckv_sb[:, 0:512], in_=psC[:]))
        DR("s")
        OP("s", lambda e: e.copy(out=ckv_sb[:, 512:1024], in_=psD[:]), inc="sS")
        s_ckv_box[0] = N("sS")

    def ckv_col(t_):
        return (t_ // 2) * 128 if t_ % 2 == 1 else 256 + (t_ // 2) * 128

    def p1_upkv(g, t):
        s_ = 4 * g + t
        pk_ = psKG[t % 2]
        pv_ = psC if t % 2 == 0 else psD
        W("t", "sS", s_ckv_box[0])
        W("t", "sV", rope_done[s_])
        if 4 * g + t + 2 < NT:
            W("t", "sV", rope_done[4 * g + t + 2])
        if t >= 2:
            W("t", "sV", vav_done[4 * g + t - 2])
        for cc in range(2):
            stat = ckv_sb[:, cc * 512 + ckv_col(t): cc * 512 + ckv_col(t) + 128]
            OP("t", lambda e, p=pk_, st=stat, cc=cc: e.matmul(
                p[:, 0:256], st, upk_sb[:, cc * 256:cc * 256 + 256],
                start=(cc == 0), stop=(cc == 1)))
            OP("t", lambda e, p=pv_, st=stat, cc=cc: e.matmul(
                p[:, 0:512], st, upv_sb[:, cc * 512:cc * 512 + 512],
                start=(cc == 0), stop=(cc == 1)),
               inc=("sT" if cc == 1 else None))
        t_upkv_last[0] = N("sT")

        W("v", "sT", t_upkv_last[0])
        W("v", "sV", v_ones)
        kcdst = AP(kvK[t], 16, [[512, 128], [64, 8], [32, 2], [1, 16]])
        OP("v", lambda e, d=kcdst, p=pk_: e.tensor_copy(d, p[:, 0:256]), inc="sV")
        kvk_done[t] = N("sV")
        psKG_free[t % 2] = kvk_done[t]
        DR("v")
        OP("v", lambda e, t2=t: e.tensor_mul(ksq_sb[:], kvK[t2][:], kvK[t2][:]))
        DR("v")
        OP("v", lambda e: e.reduce_sum(
            kntmp[:], AP(ksq_sb, 0, [[512, 128], [64, 8], [1, 64]]), axis=AX.X))
        DR("v")
        OP("v", lambda e: e.tensor_max(kn_acc[:], kn_acc[:], kntmp[:]))
        DR("v")
        for h in range(NKV):
            OP("v", lambda e, p=pv_, h=h, s2=s_: e.tensor_copy(
                V_sb[:, (h * NT + s2) * 65:(h * NT + s2) * 65 + 64],
                p[:, h * 64:h * 64 + 64]),
               inc=("sV" if h == NKV - 1 else None))
        vav_done[s_] = N("sV")

        se_ = "sKTa" if t % 2 == 0 else "sKTb"
        W("y", "sV", kvk_done[t])
        for p in range(4):
            OP("y", lambda e, p=p, t2=t, s2=s_: e.dma_start_transpose(
                out=Kp_sb[:, p * S + s2 * 128: p * S + s2 * 128 + 128],
                in_=kvK[t2][:, p * 128:p * 128 + 128]),
               inc=se_, incn=16)
        kvKT_done[t] = (se_, N(se_))
        K_done[s_] = kvKT_done[t]

    for i in range(21):
        if i < 16:
            p1_head_front(i)
        if 1 <= i <= 16:
            p1_head_tr(i - 1)
        if i >= 5 and (i - 5) // 4 <= 3:
            p1_upkv((i - 5) // 4, (i - 5) % 4)
        if 2 <= i <= 17:
            p1_krgate_rope(i - 2)
        if i >= 4 and (i - 4) % 4 == 0 and i <= 16:
            p1_down((i - 4) // 4)

    W("t", "sV", vav_done[NT - 1])
    if psT_free[0]:
        W("t", psT_free[0][0], psT_free[0][1])
    OP("t", lambda e: e.transpose(psT[0][0:8, 0:128], kn_acc[:], identb_sb[:]), inc="sT")
    t_kn = N("sT")
    W("v", "sT", t_kn)
    OP("v", lambda e: e.reduce_max(kmax8_sb[:], psT[0][0:8, 0:128], axis=AX.X))
    DR("v")
    OP("v", lambda e: e.tensor_copy(kmax8b_sb[:], kmax8_sb[:]), inc="sV")
    v_nk = N("sV")
    W("t", "sV", v_nk)
    OP("t", lambda e: e.transpose(psT[0][0:1, 256:264], kmax8b_sb[:],
                                  identb_sb[0:8, 0:8]), inc="sT")
    t_kr = N("sT")
    W("v", "sT", t_kr)
    OP("v", lambda e: e.tensor_copy(kmaxrow_sb[:], psT[0][0:1, 256:264]), inc="sV")
    v_nkr = N("sV")

    W("t", "sV", vav_done[NT - 1])
    for se2, v2 in xn_done[NT - 1]:
        W("t", se2, v2)
    for g2 in range(2):
        tgtA = psC if g2 == 0 else psD
        for hc in range(NHC):
            stat = acat_sb[:, hc * 128:hc * 128 + 128]
            mov = xn_odd[:, hc * 1024 + g2 * 512: hc * 1024 + g2 * 512 + 512]
            OP("t", lambda e, p=tgtA, st=stat, m=mov, hc=hc: e.matmul(
                p[:, 0:512], st, m, start=(hc == 0), stop=(hc == NHC - 1)),
               inc=("sT" if hc == NHC - 1 else None))
    t_u = N("sT")
    W("s", "sT", t_u)
    OP("s", lambda e: e.copy(out=u_sb[:, 0:512], in_=psC[:]))
    DR("s")
    OP("s", lambda e: e.copy(out=u_sb[:, 512:1024], in_=psD[:]), inc="sS")
    s_u = N("sS")

    QWPRE_COL = {0: 14336, 1: 12288}
    qraw_pre = [g_fm[:, 10240 + 1024 * qc: 10240 + 1024 * qc + 1024] for qc in range(2)]
    sqq_pre = [g_fm[:, 8192 + 1024 * qc: 8192 + 1024 * qc + 1024] for qc in range(2)]
    qpart_pre = g_fm[:, 7168:8192]
    for qc in range(2):
        sq_ = "sQa" if qc == 0 else "sQb"
        OP("g", lambda e, qc=qc: e.dma_start(
            out=g_fm[:, QWPRE_COL[qc]:QWPRE_COL[qc] + 2048],
            in_=D["qw_s"][:, qc * 2048:(qc + 1) * 2048]), inc=sq_, incn=16)
        qw_ld[qc] = N(sq_)

    pre_evac = [[0, 0], [0, 0]]
    v_rope_pre = [0, 0]

    def pre_qp(qc, g2):
        sq_ = "sQa" if qc == 0 else "sQb"
        tgt = psC if g2 == 0 else psD
        W("t", sq_, qw_ld[qc])
        if qc == 0 and g2 == 0:
            for se2, v2 in xn_done[NT - 1]:
                W("t", se2, v2)
        W("t", "sS", s_u)
        if qc == 1:
            W("t", "sS", pre_evac[0][g2])
        for hc in range(NHC):
            stat = g_fm[:, QWPRE_COL[qc] + hc * 128: QWPRE_COL[qc] + hc * 128 + 128]
            mov = xn_odd[:, hc * 1024 + g2 * 512: hc * 1024 + g2 * 512 + 512]
            OP("t", lambda e, p=tgt, st=stat, m=mov, hc=hc: e.matmul(
                p[:, 0:512], st, m, start=(hc == 0), stop=(hc == NHC - 1)),
               inc=("sT" if hc == NHC - 1 else None))
        qp_h = N("sT")
        if g2 == 1:
            qproj_done[qc] = qp_h
        W("s", "sT", qp_h)
        OP("s", lambda e, qc=qc, g2=g2, p=tgt: e.copy(
            out=qraw_pre[qc][:, g2 * 512:g2 * 512 + 512], in_=p[:, 0:512]),
           inc="sS")
        pre_evac[qc][g2] = N("sS")
        if g2 == 1:
            qevac_done[qc] = pre_evac[qc][1]

    def pre_shuffle_sq(qc):
        W("v", "sS", pre_evac[qc][1])
        for blk in range(4):
            OP("v", lambda e, qc=qc, b=blk: e.tensor_copy(
                qpart_pre[b * 32:b * 32 + 32, :],
                qraw_pre[qc][(b ^ 1) * 32:(b ^ 1) * 32 + 32, :]))
        OP("v", lambda e, qc=qc: e.tensor_mul(
            sqq_pre[qc][:], qraw_pre[qc][:], qraw_pre[qc][:]))
        DR("v")
        OP("v", lambda e, qc=qc: e.tensor_max(
            sqq_pre[qc][:, 0:512], sqq_pre[qc][:, 0:512], sqq_pre[qc][:, 512:1024]),
           inc="sV")
        v_sq[qc] = N("sV")
        DR("v")

    def pre_rope(qc):
        if qc == 0:
            W("v", "sWsm", w_smallw)
        OP("v", lambda e, qc=qc: e.tensor_mul(qraw_pre[qc][:], qraw_pre[qc][:],
                                              cosq_sb[:]))
        OP("v", lambda e: e.tensor_mul(qpart_pre[:], qpart_pre[:], sinq_sb[:]))
        DR("v")
        for half in range(2):
            gl = 2 * qc + half
            dst = qkv_sb[0:64, gl * 1024: gl * 1024 + 1024]
            OP("v", lambda e, qc=qc, d=dst, hf=half: e.tensor_add(
                d, qraw_pre[qc][hf * 64:hf * 64 + 64, :],
                qpart_pre[hf * 64:hf * 64 + 64, :]))
        DR("v")
        OP("v", lambda e: e.engine_nop(), inc="sV")
        rope_done_q[qc] = N("sV")

    pre_chunks = [
        lambda: pre_qp(0, 0), lambda: pre_qp(0, 1),
        lambda: pre_qp(1, 0), lambda: pre_qp(1, 1),
        lambda: pre_shuffle_sq(0), lambda: pre_rope(0),
        lambda: pre_shuffle_sq(1), lambda: pre_rope(1),
    ]

    tsc_done = [0] * NOWN
    tsc_vread = [0, 0]
    for j in range(NOWN):
        if j < len(pre_chunks):
            pre_chunks[j]()
        bj = j % 2
        W("t", "sV", v_cw_done[j])
        if j == 0:
            W("t", "sV", v_nkr)
        if j >= 2:
            W("t", "sV", tsc_vread[bj])
        OP("t", lambda e, j=j, bj=bj: e.transpose(
            psT[bj][0:8, 128:256], cwb_sb[:, j * 8:j * 8 + 8], identb_sb[:]),
           inc="sT")
        t_cwT = N("sT")
        W("v", "sT", t_cwT)
        OP("v", lambda e, bj=bj: e.tensor_copy(
            cwf_sb[0:8, bj * 128:bj * 128 + 128], psT[bj][0:8, 128:256]), inc="sV")
        v_cwf = N("sV")
        W("t", "sV", v_cwf)
        if j == 0:
            W("t", "sV", kvk_done[3])
        if j == 1:
            W("t", "sV", kvk_done[2])
        OP("t", lambda e, bj=bj: e.matmul(
            psKG[bj][:, 0:128], e8_sb[:], cwf_sb[0:8, bj * 128:bj * 128 + 128],
            start=True, stop=True), inc="sT")
        t_e8 = N("sT")
        W("v", "sT", t_e8)
        W("v", "sS", s_u)
        OP("v", lambda e, j=j, bj=bj: e.tensor_mul(
            t_sb[:, j * 128:j * 128 + 128], u_sb[:, j * 128:j * 128 + 128],
            psKG[bj][:, 0:128]), inc="sV")
        tsc_done[j] = N("sV")
        tsc_vread[bj] = tsc_done[j]
    for ch in pre_chunks[NOWN:]:
        ch()

    for qc in range(2):
        tgt = psC if qc == 0 else psD
        W("t", "sV", v_sq[qc])
        W("t", "sS", pre_evac[1][qc])
        OP("t", lambda e, qc=qc, p=tgt: e.matmul(
            p[0:2, 0:512], ones2c_sb[:], sqq_pre[qc][:, 0:512],
            start=True, stop=True), inc="sT")
        t_qnorm[qc] = N("sT")
        W("v", "sT", t_qnorm[qc])
        OP("v", lambda e, p=tgt: e.reduce_max(qnt[:], p[0:2, 0:512], axis=AX.X))
        DR("v")
        OP("v", lambda e: e.tensor_max(qacc[:, 0:1], qacc[:, 0:1], qnt[:]))
        DR("v")
        OP("v", lambda e: e.engine_nop(), inc="sV")
        v_qred[qc] = N("sV")
    OP("v", lambda e: e.tensor_copy(qpad[0:2, 0:1], qacc[:, 0:1]))
    DR("v")
    OP("v", lambda e: e.transpose(qpadT[:], qpad[:]))
    DR("v")
    OP("v", lambda e: e.reduce_max(mtmpA[:], qpadT[0:1, 0:2], axis=AX.X))
    DR("v")
    OP("v", lambda e: e.tensor_mul(mtmpA[:], mtmpA[:], kmaxrow_sb[0:1, 0:1]),
       inc="sV")
    v_m1 = N("sV")
    W("s", "sV", v_m1)
    OP("s", lambda e: e.activation(out=mtmpA[:], in_=mtmpA[:], func=AF.Ln))
    DR("s")
    OP("s", lambda e: e.activation(out=mtmpA[:], in_=mtmpA[:], func=AF.Exp,
                                   scale=0.5), inc="sS")
    s_m0 = N("sS")
    W("v", "sS", s_m0)
    OP("v", lambda e: e.tensor_scalar_mul(mtmpA[:], mtmpA[:], -0.1259765625),
       inc="sV")
    v_m2 = N("sV")
    W("g", "sV", v_m2)
    OP("g", lambda e: e.partition_broadcast(negm_sb[:, 0:1], mtmpA[0:1, 0:1]),
       inc="sG")
    mh_ready[0] = N("sG")

    DBG("xn_odd", xn_odd, list(xn_done[NT - 1]))
    DBG("Kp", Kp_sb, [K_done[NT - 1]])
    DBG("V", V_sb, [("sV", vav_done[NT - 1])])
    DBG("logits", logits_sb, [("sV", rope_done[NT - 1])])

    kta_total = N("sKTa")
    ktb_total = N("sKTb")
    p1_t_end = N("sT")
    p1_v_end = N("sV")
    p1_s_end = N("sS")
    psum1.close()
    p1.close()
    p01.close()

    p2 = ExitStack()
    PT = [alloc(p2, f"PT{i}", [128, 1024], BF16) for i in range(4)]
    attn_fm = alloc(p2, "attn_fm", [128, 16 * 1024], BF16)
    rlbc_sb = alloc(p2, "rlbc_sb", [64, 512], F32)
    lpad = alloc(p2, "lpad", [32, 512], F32)
    lT = alloc(p2, "lT", [32, 512], F32)
    rT = alloc(p2, "rT", [32, 512], F32)
    rrow = alloc(p2, "rrow", [32, 512], F32)
    fwbuf = [alloc(p12, f"fwbuf{i}", [128, 2048], BF16, side="right") for i in range(2)]
    qwbuf = [alloc(p12, f"qwbuf{i}", [128, 2048], BF16, side="right") for i in range(2)]
    sqq = [alloc(p12, f"sqq{i}", [128, 1024], BF16, side="right") for i in range(2)]
    qraw = [alloc(p12, f"qraw{i}", [128, 1024], BF16, side="right") for i in range(2)]
    qpart = alloc(p2, "qpart", [128, 1024], BF16)

    QR_PITCH = 8192

    psum2 = ExitStack()
    psS_ = [psum2.enter_context(nc.psum_tensor(f"psS{i}", [128, 1024], F32)) for i in range(2)]
    psAV = [psum2.enter_context(nc.psum_tensor(f"psAV{i}", [65, 512], F32))
            for i in range(2)]
    psQP = psum2.enter_context(nc.psum_tensor("psQP", [128, 512], F32))
    psDen = psum2.enter_context(nc.psum_tensor("psDen", [128, 512], F32))

    OP("v", lambda e: e.memset(lpad[:], 1.0))
    OP("v", lambda e: e.memset(rT[:], 1.0), inc="sV")
    v_rlinit = N("sV")

    psQP_vfree = [0]

    def emit_qw_dma(qc):
        sq_ = "sQa" if qc % 2 == 0 else "sQb"
        if qc < 2:
            W("g", "sT", p1_t_end)
            W("g", "sV", N("sV"))
            W("g", "sS", N("sS"))
        else:
            W("g", "sT", qproj_done[qc - 2])
        OP("g", lambda e, qc=qc: e.dma_start(
            out=qwbuf[qc % 2][:], in_=D["qw_s"][:, qc * 2048:(qc + 1) * 2048]),
           inc=sq_, incn=16)
        qw_ld[qc] = N(sq_)

    def gen_qpair(qc):
        h = qc // 2
        sq_ = "sQa" if qc % 2 == 0 else "sQb"
        qevac_half = [0, 0]
        for g2 in range(2):
            for u in range(4):
                if u == 0:
                    if g2 == 0:
                        W("t", sq_, qw_ld[qc])
                        if qc == 0:
                            W("t", "sT", p1_t_end)
                        if qc >= 1:
                            W("t", "sS", qevac_done[qc - 1])
                        W("t", "sV", psQP_vfree[0])
                    else:
                        W("t", "sS", qevac_half[0])
                for hc in range(4 * u, 4 * u + 4):
                    stat = qwbuf[qc % 2][:, hc * 128:hc * 128 + 128]
                    mov = xn_odd[:, hc * 1024 + g2 * 512: hc * 1024 + g2 * 512 + 512]
                    OP("t", lambda e, st=stat, m=mov, hc=hc: e.matmul(
                        psQP[:, 0:512], st, m,
                        start=(hc == 0), stop=(hc == NHC - 1)),
                       inc=("sT" if hc == NHC - 1 else None))
                if u == 3:
                    qp_half = N("sT")
                    if g2 == 1:
                        qproj_done[qc] = qp_half
                    W("s", "sT", qp_half)
                    if g2 == 0 and qc >= 2:
                        W("s", "sV", rope_done_q[qc - 2])
                    OP("s", lambda e, qc=qc, g2=g2: e.copy(
                        out=qraw[qc % 2][:, g2 * 512:g2 * 512 + 512],
                        in_=psQP[:, 0:512]), inc="sS")
                    qevac_half[g2] = N("sS")
                yield
        qevac_done[qc] = qevac_half[1]

        W("v", "sS", qevac_done[qc])
        if qc >= 2:
            W("v", "sT", t_qnorm[qc - 2])
        if h >= 4:
            W("v", "sT", scores_done[h - 4])
        for blk in range(4):
            OP("v", lambda e, qc=qc, b=blk: e.tensor_copy(
                qpart[b * 32:b * 32 + 32, :],
                qraw[qc % 2][(b ^ 1) * 32:(b ^ 1) * 32 + 32, :]))
        yield
        OP("v", lambda e, qc=qc: e.tensor_mul(
            sqq[qc % 2][:], qraw[qc % 2][:], qraw[qc % 2][:]))
        DR("v")
        OP("v", lambda e, qc=qc: e.tensor_max(
            sqq[qc % 2][:, 0:512], sqq[qc % 2][:, 0:512], sqq[qc % 2][:, 512:1024]),
           inc="sV")
        v_sq[qc] = N("sV")
        DR("v")
        yield
        OP("v", lambda e, qc=qc: e.tensor_mul(qraw[qc % 2][:], qraw[qc % 2][:],
                                              cosq_sb[:]))
        OP("v", lambda e: e.tensor_mul(qpart[:], qpart[:], sinq_sb[:]))
        DR("v")
        rofs_q = (h % 2) * 64
        colb = ((h % 4) // 2) * 4096
        for half in range(2):
            gl = 2 * (qc % 2) + half
            dst = qkv_sb[rofs_q:rofs_q + 64, colb + gl * 1024: colb + gl * 1024 + 1024]
            OP("v", lambda e, qc=qc, d=dst, hf=half: e.tensor_add(
                d, qraw[qc % 2][hf * 64:hf * 64 + 64, :],
                qpart[hf * 64:hf * 64 + 64, :]))
        DR("v")
        OP("v", lambda e: e.engine_nop(), inc="sV")
        rope_done_q[qc] = N("sV")
        yield

    def emit_norm(qc, evac_tgt):
        W("t", "sV", v_sq[qc])
        W("t", "sS", evac_tgt)
        W("t", "sV", psQP_vfree[0])
        OP("t", lambda e, qc=qc: e.matmul(
            psQP[0:2, 0:512], ones2c_sb[:], sqq[qc % 2][:, 0:512],
            start=True, stop=True), inc="sT")
        t_qnorm[qc] = N("sT")
        W("v", "sT", t_qnorm[qc])
        OP("v", lambda e: e.reduce_max(qnt[:], psQP[0:2, 0:512], axis=AX.X))
        DR("v")
        hcol = qc // 2
        OP("v", lambda e, hp=hcol: e.tensor_max(
            qacc[:, hp:hp + 1], qacc[:, hp:hp + 1], qnt[:]))
        DR("v")
        OP("v", lambda e: e.engine_nop(), inc="sV")
        v_qred = N("sV")
        psQP_vfree[0] = v_qred

    def emit_mhat(hh):
        DR("v")
        OP("v", lambda e, hh=hh: e.tensor_copy(qpad[0:2, 0:1], qacc[:, hh:hh + 1]))
        DR("v")
        OP("v", lambda e: e.transpose(qpadT[:], qpad[:]))
        DR("v")
        OP("v", lambda e: e.reduce_max(mtmpA[:], qpadT[0:1, 0:2], axis=AX.X))
        DR("v")
        OP("v", lambda e, hh=hh: e.tensor_mul(
            mtmpA[:], mtmpA[:], kmaxrow_sb[0:1, hh:hh + 1]), inc="sV")
        v_m1 = N("sV")
        W("s", "sV", v_m1)
        OP("s", lambda e: e.activation(out=mtmpA[:], in_=mtmpA[:], func=AF.Ln))
        DR("s")
        OP("s", lambda e: e.activation(out=mtmpA[:], in_=mtmpA[:], func=AF.Exp,
                                       scale=0.5), inc="sS")
        s_m = N("sS")
        W("v", "sS", s_m)
        OP("v", lambda e: e.tensor_scalar_mul(mtmpA[:], mtmpA[:], -0.1259765625),
           inc="sV")
        v_m2 = N("sV")
        W("g", "sV", v_m2)
        OP("g", lambda e, hh=hh: e.partition_broadcast(
            negm_sb[:, hh:hh + 1], mtmpA[0:1, 0:1]), inc="sG")
        mh_ready[hh] = N("sG")

    def gen_qblock(h):
        yield from gen_qpair(2 * h + 2)
        if 2 * h + 4 < 16:
            emit_qw_dma(2 * h + 4)
        yield from gen_qpair(2 * h + 3)
        if 2 * h + 5 < 16:
            emit_qw_dma(2 * h + 5)
        yield
        yield
        yield
        yield
        emit_norm(2 * h + 2, qevac_done[2 * h + 3])
        yield
        emit_norm(2 * h + 3, qevac_done[2 * h + 3])
        yield
        emit_mhat(h + 1)
        yield

    dense_done = [0] * 16
    evac_g = [None] * 16

    fw_ld = [0] * 16

    def emit_fw_dma(oc):
        sf = "sFa" if oc % 2 == 0 else "sFb"
        if oc >= 2:
            W("g", "sT", dense_done[oc - 2])
        else:
            W("g", "sT", p1_t_end)
            W("g", "sV", N("sV"))
            W("g", "sS", N("sS"))
        OP("g", lambda e, oc=oc: e.dma_start(
            out=fwbuf[oc % 2][:], in_=D["fw_s"][:, oc * 2048:(oc + 1) * 2048]),
           inc=sf, incn=16)
        fw_ld[oc] = N(sf)

    def dense_units():
        emit_fw_dma(0)
        emit_fw_dma(1)
        for oc in range(16):
            sf = "sFa" if oc % 2 == 0 else "sFb"
            v_half = [0, 0]
            for g2 in range(2):
                for u in range(8):
                    if u == 0:
                        if g2 == 0:
                            W("t", sf, fw_ld[oc])
                            if oc == 0:
                                W("t", "sV", tsc_done[NOWN - 1])
                            else:
                                W("t", evac_g[oc - 1][0], evac_g[oc - 1][1])
                        else:
                            W("t", v_half[0][0], v_half[0][1])
                    for hc in range(2 * u, 2 * u + 2):
                        stat = fwbuf[oc % 2][:, hc * 128:hc * 128 + 128]
                        mov = xn_odd[:, hc * 1024 + g2 * 512:
                                     hc * 1024 + g2 * 512 + 512]
                        OP("t", lambda e, st=stat, m=mov, hc=hc: e.matmul(
                            psDen[:, 0:512], st, m, start=(hc == 0), stop=False),
                           inc=("sT" if (hc == 15 and g2 == 1) else None))
                    yield
                if g2 == 1:
                    dense_done[oc] = N("sT")
                    if oc + 2 < 16:
                        emit_fw_dma(oc + 2)
                statB = bcat_sb[:, oc * 128:oc * 128 + 128]
                movB = t_sb[:, g2 * 512:g2 * 512 + 512]
                OP("t", lambda e, st=statB, m=movB: e.matmul(
                    psDen[:, 0:512], st, m, start=False, stop=True), inc="sT")
                t_b = N("sT")
                W("s", "sT", t_b)
                if oc in (8, 9):
                    W("s", "sT", t_qnorm[oc - 8])
                elif oc >= 12:
                    W("s", "sT", qproj_done[1])
                OP("s", lambda e, oc=oc, g2=g2: e.copy(
                    out=g_fm[:, oc * 1024 + g2 * 512: oc * 1024 + g2 * 512 + 512],
                    in_=psDen[:, 0:512]), inc="sS")
                v_half[g2] = ("sS", N("sS"))
                yield
                if g2 == 0:
                    yield
                    yield
            evac_g[oc] = v_half[1]

    dense_it = dense_units()
    dense_emitted = [0]
    DENSE_TOTAL = 16 * 20
    TOTAL_ITEMS = NKV * (NOWN * (NOWN + 1) // 2)

    exp_cnt_half = [0, 0]
    av_cnt_pt = [0, 0, 0, 0]
    mask_cnt = {}
    exp_item = {}
    evac_done = [0, 0]
    slot_par = {}
    av_last = {}
    first_item_of_head = [True] * NKV
    first_exp_of_head = [True] * NKV
    last_bc_box = [0]
    lpad_init_box = [v_rlinit]
    nslots = 0

    def emit_scores(i, it):
        h, j, sub = it
        p_pair = h // 2
        rofs = (h % 2) * 64
        half = i % 2
        W("t", "sS", exp_cnt_half[half])
        if first_item_of_head[h]:
            W("t", "sV", rope_done_q[2 * h + 1])
            if h == 0:
                W("t", "sKTa", kta_total)
                W("t", "sKTb", ktb_total)
            first_item_of_head[h] = False
        for ktl in range(2):
            kt = 2 * sub + ktl
            stat = Kp_sb[rofs:rofs + 64,
                         p_pair * S + kt * 128: p_pair * S + kt * 128 + 128]
            mov = AP(qkv_sb, (h % 2) * 64 * QR_PITCH + ((h % 4) // 2) * 4096 + j * 128,
                     [[QR_PITCH, 64], [1024, 4], [1, 128]])
            OP("t", lambda e, st=stat, m=mov, hf=half, k=ktl: e.matmul(
                psS_[hf][:, k * 512:k * 512 + 512], st, m, start=True, stop=True),
               inc=("sT" if ktl == 1 else None))
        t_sc = N("sT")
        scores_done[h] = t_sc
        W("s", "sT", t_sc)
        W("s", "sT", av_cnt_pt[i % 4])
        if first_exp_of_head[h]:
            W("s", "sG", mh_ready[h])
            first_exp_of_head[h] = False
        OP("s", lambda e, hf=half, pr=i % 4, h2=h: e.activation(
            out=PT[pr][:, 0:1024], in_=psS_[hf][:, 0:1024], func=AF.Exp,
            bias=negm_sb[:, h2:h2 + 1], scale=1.0), inc="sS")
        exp_cnt_half[half] = N("sS")
        exp_item[it] = exp_cnt_half[half]
        if sub == j:
            W("v", "sS", exp_item[it])
            OP("v", lambda e, pr=i % 4: e.tensor_scalar(
                out=PT[pr][:, 0:512], in0=PT[pr][:, 0:512],
                scalar1=flag_sb[:], scalar2=None, op0=ALU.mult))
            trirep = AP(trim_sb, 0, [[128, 128], [0, 4], [1, 128]])
            DR("v")
            OP("v", lambda e, pr=i % 4, tr=trirep: e.tensor_mul(
                PT[pr][:, 512:1024], PT[pr][:, 512:1024], tr), inc="sV")
            mask_cnt[(h, j)] = N("sV")

    def emit_av(i, it):
        h, j, sub = it
        L = 2 * j + 2
        par = slot_par[(h, j)]
        if sub == j:
            W("t", "sV", mask_cnt[(h, j)])
        else:
            W("t", "sS", exp_item[it])
        if sub == 0:
            W("t", "sV", evac_done[par])
        for ktl in range(2):
            kt = 2 * sub + ktl
            vstat = V_sb[:, (h * NT + kt) * 65:(h * NT + kt) * 65 + 65]
            OP("t", lambda e, vs=vstat, pr=i % 4, k=ktl, pa=par,
               st=(kt == 0), sp=(kt == L - 1): e.matmul(
                psAV[pa][:, 0:512], vs, PT[pr][:, k * 512:k * 512 + 512],
                start=st, stop=sp),
               inc=("sT" if ktl == 1 else None))
        av_cnt_pt[i % 4] = N("sT")

    def emit_slotend(h, j):
        par = slot_par[(h, j)]
        t_av = av_last[(h, j)]
        W("v", "sT", t_av)
        W("v", "sG", last_bc_box[0])
        W("v", "sV", lpad_init_box[0])
        OP("v", lambda e, pa=par: e.tensor_copy(lpad[0:1, 0:512],
                                                psAV[pa][64:65, 0:512]))
        DR("v")
        OP("v", lambda e: e.transpose(lT[:], lpad[:]))
        DR("v")
        OP("v", lambda e: e.reciprocal(AP(rT, 0, [[512, 32], [32, 16]]),
                                       AP(lT, 0, [[512, 32], [32, 16]])))
        DR("v")
        OP("v", lambda e: e.transpose(rrow[:], rT[:]), inc="sV")
        v_rl = N("sV")
        W("g", "sV", v_rl)
        OP("g", lambda e: e.partition_broadcast(rlbc_sb[:], rrow[0:1, 0:512]),
           inc="sG")
        last_bc_box[0] = N("sG")
        W("v", "sG", last_bc_box[0])
        for g_ in range(4):
            c = 2 * h + g_ // 2
            pofs = (g_ % 2) * 64
            OP("v", lambda e, g2=g_, c=c, po=pofs, j=j, pa=par: e.tensor_mul(
                attn_fm[po:po + 64, c * 1024 + j * 128: c * 1024 + j * 128 + 128],
                psAV[pa][0:64, g2 * 128:g2 * 128 + 128],
                rlbc_sb[0:64, g2 * 128:g2 * 128 + 128]),
               inc=("sV" if g_ == 3 else None))
        evac_done[par] = N("sV")

    from collections import deque
    pend = deque()

    def flush_av(all_=False):
        while len(pend) > (0 if all_ else 3):
            pi, pit = pend.popleft()
            emit_av(pi, pit)
            av_last[(pit[0], pit[1])] = N("sT")
            if pit[2] == pit[1]:
                emit_slotend(pit[0], pit[1])

    emit_qw_dma(2)
    emit_qw_dma(3)

    item_idx = 0
    for h in range(NKV):
        qgen = gen_qblock(h) if h <= 6 else None
        for j in range(NOWN):
            for sub in range(j + 1):
                it = (h, j, sub)
                if sub == 0:
                    slot_par[(h, j)] = nslots % 2
                    nslots += 1
                i = len(exp_item)
                flush_av()
                emit_scores(i, it)
                pend.append((i, it))
                if qgen is not None and next(qgen, "END") == "END":
                    qgen = None
                target = DENSE_TOTAL * (item_idx + 1) // TOTAL_ITEMS
                while dense_emitted[0] < target and \
                        next(dense_it, "END") != "END":
                    dense_emitted[0] += 1
                item_idx += 1
        while qgen is not None and next(qgen, "END") != "END":
            pass
    flush_av(all_=True)
    for _ in dense_it:
        pass
    v_attn_done = N("sV")

    DBG("attn", attn_fm, [("sV", v_attn_done)])
    DBG("g", g_fm, [evac_g[15]])

    p2_t_end = N("sT")
    psum2.close()
    p12.close()

    p4 = ExitStack()
    owbuf = [alloc(p4, f"owbuf{i}", [128, 2048], BF16) for i in range(4)]
    out_sb = [alloc(p4, f"out_sb{i}", [128, HID], F32) for i in range(2)]
    xf32 = [alloc(p4, f"xf32_{i}", [128, HID], F32, side="right") for i in range(4)]

    psum4 = ExitStack()
    psO = [psum4.enter_context(nc.psum_tensor(f"psO{i}", [128, 1024], F32)) for i in range(2)]
    psG = [psum4.enter_context(nc.psum_tensor(f"psG{i}", [128, 2048], BF16)) for i in range(2)]

    silu_done = [0] * 16
    for oc in range(16):
        W("s", evac_g[oc][0], evac_g[oc][1])
        OP("s", lambda e, oc=oc: e.activation(
            out=g_fm[:, oc * 1024:(oc + 1) * 1024],
            in_=g_fm[:, oc * 1024:(oc + 1) * 1024], func=AF.Silu), inc="sS")
        silu_done[oc] = N("sS")

    sXF = ["sXFa", "sXFb", "sXFc", "sXFd"]
    x_ld = [0] * NOWN
    comb_done = [0] * NOWN
    out_dma_cnt = [0] * NOWN

    def emit_xload(j):
        if j < 4:
            W("y", "sT", p2_t_end)
            W("y", "sV", v_attn_done)
        else:
            W("y", "sV", comb_done[j - 4])
        OP("y", lambda e, j=j: e.dma_start(
            out=xf32[j % 4][:], in_=D["x_kv"][(2 * j + 1) * 128:(2 * j + 2) * 128, :]),
           inc=sXF[j % 4], incn=16)
        x_ld[j] = N(sXF[j % 4])

    for j in range(4):
        emit_xload(j)

    oproj_done = [0] * 16
    sg_done = [0] * 16
    psG_free = [0, 0]
    OW_SEMS = ["sOWa", "sOWb", "sQa", "sQb"]
    ow_ld = [0] * 16

    def emit_ow_dma(oc):
        so_ = OW_SEMS[oc % 4]
        if oc >= 4:
            W("g", "sT", oproj_done[oc - 4])
        else:
            W("g", "sV", rope_done_q[15])
            W("g", "sT", t_qnorm[15])
            W("g", "sS", qevac_done[15])
        OP("g", lambda e, oc=oc: e.dma_start(
            out=owbuf[oc % 4][:], in_=D["ow_s"][:, oc * 2048:(oc + 1) * 2048]),
           inc=so_, incn=16)
        ow_ld[oc] = N(so_)

    for oc in range(4):
        emit_ow_dma(oc)
    for oc in range(16):
        so_ = OW_SEMS[oc % 4]
        W("t", so_, ow_ld[oc])
        if oc == 0:
            W("t", "sS", silu_done[0])
            W("t", "sV", v_attn_done)
            W("t", "sT", p2_t_end)
        if oc >= 2:
            W("t", "sV", sg_done[oc - 2])
        for ac in range(16):
            stat = owbuf[oc % 4][:, ac * 128:ac * 128 + 128]
            for g2 in range(2):
                mov = attn_fm[:, ac * 1024 + g2 * 512: ac * 1024 + g2 * 512 + 512]
                OP("t", lambda e, st=stat, m=mov, g2=g2, oc=oc, ac=ac: e.matmul(
                    psO[oc % 2][:, g2 * 512:g2 * 512 + 512], st, m,
                    start=(ac == 0), stop=(ac == 15)),
                   inc=("sT" if (ac == 15 and g2 == 1) else None))
        oproj_done[oc] = N("sT")
        if oc + 4 < 16:
            emit_ow_dma(oc + 4)
        W("v", "sT", oproj_done[oc])
        W("v", "sS", silu_done[oc])
        OP("v", lambda e, oc=oc: e.tensor_mul(
            g_fm[:, oc * 1024:(oc + 1) * 1024], psO[oc % 2][:, 0:1024],
            g_fm[:, oc * 1024:(oc + 1) * 1024]), inc="sV")
        sg_done[oc] = N("sV")
        oc2 = oc - 1
        if oc2 >= 0:
            W("t", "sV", sg_done[oc2])
            for j01 in range(2):
                OP("t", lambda e, oc2=oc2, j=j01: e.transpose(
                    psG[j][:, oc2 * 128:oc2 * 128 + 128],
                    g_fm[:, oc2 * 1024 + j * 128: oc2 * 1024 + j * 128 + 128],
                    identb_sb[:]))

    for j in range(NOWN):
        W("t", "sV", sg_done[15])
        W("t", "sV", psG_free[j % 2])
        ocs = [15] if j < 2 else list(range(16))
        for k2, oc in enumerate(ocs):
            OP("t", lambda e, oc=oc, j=j: e.transpose(
                psG[j % 2][:, oc * 128:oc * 128 + 128],
                g_fm[:, oc * 1024 + j * 128: oc * 1024 + j * 128 + 128],
                identb_sb[:]),
               inc=("sT" if k2 == len(ocs) - 1 else None))
        t_tr4 = N("sT")
        W("v", "sT", t_tr4)
        W("v", sXF[j % 4], x_ld[j])
        if j >= 2:
            W("v", "sOa" if j % 2 == 0 else "sOb", out_dma_cnt[j - 2])
        OP("v", lambda e, j=j: e.tensor_add(
            out_sb[j % 2][:], psG[j % 2][:, 0:2048], xf32[j % 4][:]), inc="sV")
        comb_done[j] = N("sV")
        psG_free[j % 2] = comb_done[j]
        W("y", "sV", comb_done[j])
        sout = "sOa" if j % 2 == 0 else "sOb"
        OP("y", lambda e, j=j: e.dma_start(
            out=out_d[j * 128:(j + 1) * 128, :], in_=out_sb[j % 2][:]),
           inc=sout, incn=16)
        out_dma_cnt[j] = N(sout)
        if j + 4 < NOWN:
            emit_xload(j + 4)

    W("y", "sOa", N("sOa"))
    W("y", "sOb", N("sOb"))

    psum4.close()
    p4.close()
    p2.close()

    with nc.Block() as blk:
        @blk.tensor
        def _(e):
            for f in OPS["t"]:
                f(e)

        @blk.vector
        def _(e):
            for f in OPS["v"]:
                f(e)

        @blk.scalar
        def _(e):
            for f in OPS["s"]:
                f(e)

        @blk.gpsimd
        def _(e):
            for f in OPS["g"]:
                f(e)

        @blk.sync
        def _(e):
            for f in OPS["y"]:
                f(e)

    ctx.close()
    lower_extended_insts(nc)
    return nc


_NC_CACHE = {}


def _get_nc(dbg=()):
    key = tuple(sorted(dbg))
    if key not in _NC_CACHE:
        _NC_CACHE[key] = _build(key)
    return _NC_CACHE[key]


def kernel(**inputs):
    in_maps = _host_prepare(inputs)
    nc = _get_nc()
    res = run_bass_kernel_spmd(nc, in_maps, core_ids=list(range(8)))
    odt = np.result_type(np.float32, res.results[0]["out"].dtype)
    out = np.empty((4, S, HID), dtype=odt)
    for core in range(8):
        seq, b = core % 4, core // 4
        o = res.results[core]["out"]
        for j in range(NOWN):
            out[seq, (2 * j + b) * 128:(2 * j + b + 1) * 128, :] = o[j * 128:(j + 1) * 128, :]
    return out


# version 37
# speedup vs baseline: 1.0338x; 1.0151x over previous
import numpy as np
import ml_dtypes
import concourse.bass as bass
import concourse.mybir as mybir
from concourse.bass import AP
from concourse.bass_utils import run_bass_kernel_spmd
from concourse.library_config import attn as _ATTN_LIB
from concourse.library_overlay import lower_extended_insts
from contextlib import ExitStack

F32 = mybir.dt.float32
BF16 = mybir.dt.bfloat16
AF = mybir.ActivationFunctionType
ALU = mybir.AluOpType
AX = mybir.AxisListType
BFNP = ml_dtypes.bfloat16

S = 2048
HID = 2048
NHC = 16
NT = 16
NOWN = 8
NKV = 8
THETA = 10000.0
NEG_BIG = -1.0e30
EPS = 1e-5
QKV_PITCH = 2 * 4 * 1024


def _tile_fm(wt):
    K, N = wt.shape
    return np.ascontiguousarray(wt.reshape(K // 128, 128, N).transpose(1, 0, 2).reshape(128, -1))


def _tile_stream(wt, ob=128):
    K, N = wt.shape
    return np.ascontiguousarray(
        wt.reshape(K // 128, 128, N // ob, ob).transpose(1, 2, 0, 3).reshape(128, -1))


def _storage_order(b):
    return [s ^ 1 if b == 0 else s for s in range(NT)]


def _host_prepare(inputs):
    x = np.asarray(inputs["L0_attn_hidden_states"], dtype=np.float32)
    q_w = np.asarray(inputs["L0_attn_q_w"], dtype=np.float32)
    kr_w = np.asarray(inputs["L0_attn_kr_w"], dtype=np.float32)
    down_w = np.asarray(inputs["L0_attn_down_w"], dtype=np.float32)
    upk_w = np.asarray(inputs["L0_attn_upk_w"], dtype=np.float32)
    upv_w = np.asarray(inputs["L0_attn_upv_w"], dtype=np.float32)
    o_w = np.asarray(inputs["L0_attn_o_w"], dtype=np.float32)
    ffn_w = np.asarray(inputs["L0_ffn_weight"], dtype=np.float32)
    gate_w = np.asarray(inputs["L0_ffn_gate_w"], dtype=np.float32)
    A_w = np.asarray(inputs["L0_ffn_A_w"], dtype=np.float32)
    B_w = np.asarray(inputs["L0_ffn_B_w"], dtype=np.float32)

    shared = {
        "qw_s": _tile_stream(q_w.T, 128),
        "fw_s": _tile_stream(ffn_w.T, 128),
        "ow_s": _tile_stream(o_w.T, 128),
        "down_t": _tile_fm(down_w.T),
        "upk_t": _tile_fm(upk_w.T),
        "upv_t": _tile_fm(upv_w.T),
        "krg_t": _tile_fm(np.concatenate([kr_w.T, gate_w.T], axis=1)),
        "acat_t": _tile_fm(A_w.reshape(128, HID).T),
        "bcat_t": np.ascontiguousarray(B_w.transpose(0, 2, 1).reshape(128, HID)),
        "trimask": np.ascontiguousarray(np.tril(np.ones((128, 128), dtype=BFNP)).T),
        "ones2c": np.concatenate([
            np.concatenate([np.ones((64, 1)), np.zeros((64, 1))], 0),
            np.concatenate([np.zeros((64, 1)), np.ones((64, 1))], 0)], 1).astype(BFNP),
        "e8": np.ascontiguousarray(np.repeat(np.eye(8, dtype=BFNP), 16, axis=1)),
        "identb": np.eye(128, dtype=BFNP),
    }

    import jax as _jax
    import jax.numpy as _jnp
    with _jax.default_device(_jax.devices("cpu")[0]):
        _pos = _jnp.arange(S, dtype=_jnp.float32)
        _invf = 1.0 / THETA ** (_jnp.arange(0, 64, 2, dtype=_jnp.float32) / 64)
        _freqs = _pos[:, None] * _invf[None, :]
        COS_T = np.asarray(_jnp.cos(_freqs), dtype=np.float32)
        SIN_T = np.asarray(_jnp.sin(_freqs), dtype=np.float32)

    in_maps = []
    for core in range(8):
        seq, b = core % 4, core // 4
        order = _storage_order(b)
        xs = x[seq].reshape(NT, 128, HID)
        x_kv = np.ascontiguousarray(xs[order].reshape(S, HID))

        pos_own = np.concatenate(
            [np.arange((2 * j + b) * 128, (2 * j + b + 1) * 128) for j in range(NOWN)]
        )
        d = np.arange(128) % 64
        sgn = np.where(d < 32, -1.0, 1.0).astype(np.float32)[:, None]
        cosq = (COS_T[pos_own][:, d % 32].T * np.float32(0.125))
        sinq = (SIN_T[pos_own][:, d % 32].T * np.float32(0.125) * sgn)

        cosk = np.empty((128, NT * 32), dtype=np.float32)
        sink = np.empty((128, NT * 32), dtype=np.float32)
        c = np.arange(32)
        sgnk = np.where(c < 16, -1.0, 1.0).astype(np.float32)[None, :]
        for s_ in range(NT):
            pos = order[s_] * 128 + np.arange(128)
            cosk[:, s_ * 32:(s_ + 1) * 32] = COS_T[pos][:, c % 16]
            sink[:, s_ * 32:(s_ + 1) * 32] = SIN_T[pos][:, c % 16] * sgnk

        flag = np.full((128, 1), 0.0 if b == 0 else 1.0, dtype=np.float32)
        m = {"x_kv": x_kv, "cosq": np.ascontiguousarray(cosq),
             "sinq": np.ascontiguousarray(sinq), "cosk": cosk, "sink": sink,
             "flag": flag}
        m.update(shared)
        in_maps.append(m)
    return in_maps


def _build(dbg=()):
    nc = bass.Bass("TRN2", target_bir_lowering=False)

    D = {}
    for name, shape, dt in [
        ("x_kv", [S, HID], F32),
        ("qw_s", [128, 16 * 2048], F32),
        ("fw_s", [128, 16 * 2048], F32),
        ("ow_s", [128, 16 * 2048], F32),
        ("down_t", [128, 16 * 256], F32),
        ("upk_t", [128, 2 * 256], F32),
        ("upv_t", [128, 2 * 512], F32),
        ("krg_t", [128, 16 * 264], F32),
        ("acat_t", [128, 16 * 128], F32),
        ("bcat_t", [128, HID], F32),
        ("trimask", [128, 128], BF16),
        ("ones2c", [128, 2], BF16),
        ("e8", [8, 128], BF16),
        ("identb", [128, 128], BF16),
        ("cosq", [128, 1024], F32),
        ("sinq", [128, 1024], F32),
        ("cosk", [128, NT * 32], F32),
        ("sink", [128, NT * 32], F32),
        ("flag", [128, 1], F32),
    ]:
        D[name] = nc.dram_tensor(name, shape, dt, kind="ExternalInput").ap()
    out_d = nc.dram_tensor("out", [NOWN * 128, HID], F32, kind="ExternalOutput").ap()

    OPS = {"t": [], "v": [], "s": [], "g": [], "y": []}
    CNT = {}
    SEMS = {}

    def I(sem, n=1):
        CNT[sem] = CNT.get(sem, 0) + n
        return CNT[sem]

    def N(sem):
        return CNT.get(sem, 0)

    def W(eng, sem, tgt):
        if tgt and tgt > 0:
            OPS[eng].append(lambda e, s=sem, t=tgt: e.wait_ge(SEMS[s], t))

    def OP(eng, fn, inc=None, incn=1):
        if inc is None:
            OPS[eng].append(fn)
            return None
        v = I(inc, incn)
        OPS[eng].append(lambda e, f=fn, s=inc, n=incn: f(e).then_inc(SEMS[s], n))
        return v

    def DR(eng):
        OPS[eng].append(lambda e: e.drain())

    def DBG(name, tensor, waits):
        if name not in dbg:
            return
        od = nc.dram_tensor("dbg_" + name, list(tensor.shape), tensor.dtype,
                            kind="ExternalOutput").ap()
        for eng_sem, tgt in waits:
            W("y", eng_sem, tgt)
        OP("y", lambda e, o=od, t2=tensor: e.dma_start(out=o[:], in_=t2[:]),
           inc="sDbg", incn=16)

    ctx = ExitStack()

    def alloc(st, name, shape, dt, side=None):
        if side:
            return st.enter_context(nc.sbuf_tensor(name, shape, dt, side=side))
        return st.enter_context(nc.sbuf_tensor(name, shape, dt))

    acat_sb = alloc(ctx, "acat_sb", [128, 16 * 128], BF16)
    bcat_sb = alloc(ctx, "bcat_sb", [128, HID], BF16)
    cosq_sb = alloc(ctx, "cosq_sb", [128, 1024], BF16)
    sinq_sb = alloc(ctx, "sinq_sb", [128, 1024], BF16)
    trim_sb = alloc(ctx, "trim_sb", [128, 128], BF16)
    ones2c_sb = alloc(ctx, "ones2c_sb", [128, 2], BF16)
    e8_sb = alloc(ctx, "e8_sb", [8, 128], BF16)
    identb_sb = alloc(ctx, "identb_sb", [128, 128], BF16)
    flag_sb = alloc(ctx, "flag_sb", [128, 1], F32)
    xn_odd = alloc(ctx, "xn_odd", [128, NHC * 1024], BF16)
    logits_sb = alloc(ctx, "logits_sb", [128, 64], F32)
    kmax8_sb = alloc(ctx, "kmax8_sb", [8, 1], F32)
    qmax8_sb = alloc(ctx, "qmax8_sb", [8, 1], F32)
    mtmp8_sb = alloc(ctx, "mtmp8_sb", [8, 1], F32)
    mtmp8b_sb = alloc(ctx, "mtmp8b_sb", [8, 1], BF16)
    negmrow_sb = alloc(ctx, "negmrow_sb", [1, 8], F32)
    negm_sb = alloc(ctx, "negm_sb", [128, 8], F32)
    kmax8b_sb = alloc(ctx, "kmax8b_sb", [8, 1], BF16)
    kmaxrow_sb = alloc(ctx, "kmaxrow_sb", [1, 8], F32)
    u_sb = alloc(ctx, "u_sb", [128, 1024], BF16)
    t_sb = alloc(ctx, "t_sb", [128, 1024], BF16)
    cwb_sb = alloc(ctx, "cwb_sb", [128, 64], BF16)
    cwf_sb = alloc(ctx, "cwf_sb", [8, 256], BF16)
    gs1 = alloc(ctx, "gs1", [128, 8], F32)
    gs2 = alloc(ctx, "gs2", [128, 8], F32)
    gm1 = alloc(ctx, "gm1", [128, 1], F32)
    gm1n = alloc(ctx, "gm1n", [128, 1], F32)
    gm2 = alloc(ctx, "gm2", [128, 1], F32)
    gden = alloc(ctx, "gden", [128, 1], F32)
    qacc = alloc(ctx, "qacc", [2, 8], F32)
    qnt = alloc(ctx, "qnt", [2, 1], F32)
    mtmpA = alloc(ctx, "mtmpA", [1, 1], F32)
    qpad = alloc(ctx, "qpad", [32, 32], F32)
    qpadT = alloc(ctx, "qpadT", [32, 32], F32)
    g_fm = alloc(ctx, "g_fm", [128, 16 * 1024], BF16)

    p12 = ExitStack()
    p01 = ExitStack()
    Kp_sb = alloc(p12, "Kp_sb", [128, 4 * S], BF16, side="right")
    V_sb = alloc(p12, "V_sb", [128, NKV * NT * 65], BF16, side="right")
    qkv_sb = alloc(p12, "qkv_sb", [128, QKV_PITCH], BF16, side="right")
    down_sb = alloc(p01, "down_sb", [128, 16 * 256], BF16, side="right")
    upk_sb = alloc(p01, "upk_sb", [128, 2 * 256], BF16, side="right")
    upv_sb = alloc(p01, "upv_sb", [128, 2 * 512], BF16, side="right")
    krg_sb = alloc(p01, "krg_sb", [128, 16 * 264], BF16, side="right")
    cosk_sb = alloc(p01, "cosk_sb", [128, NT * 32], F32, side="right")
    sink_sb = alloc(p01, "sink_sb", [128, NT * 32], F32, side="right")

    sem_names = ["sWsm", "sTbl", "sXa", "sXb", "sQa", "sQb", "sFa", "sFb",
                 "sOWa", "sOWb", "sXFa", "sXFb", "sXFc", "sXFd", "sOa", "sOb",
                 "sKTa", "sKTb", "sDbg", "sT", "sV", "sS", "sG"]
    for n_, h_ in zip(sem_names, [ctx.enter_context(nc.semaphore(n_)) for n_ in sem_names]):
        SEMS[n_] = h_

    OP("g", lambda e: e.load_library(_ATTN_LIB))
    for dst, src in [(down_sb, "down_t"), (upk_sb, "upk_t"), (upv_sb, "upv_t"),
                     (krg_sb, "krg_t"), (acat_sb, "acat_t"), (bcat_sb, "bcat_t")]:
        OP("g", lambda e, d=dst, s=src: e.dma_start(out=d[:], in_=D[s]), inc="sWsm", incn=16)
    for dst, src in [(trim_sb, "trimask"), (ones2c_sb, "ones2c"),
                     (e8_sb, "e8"), (identb_sb, "identb"), (flag_sb, "flag"),
                     (cosk_sb, "cosk"), (sink_sb, "sink")]:
        OP("y", lambda e, d=dst, s=src: e.dma_start(out=d[:], in_=D[s]), inc="sTbl", incn=16)
    for dst, src in [(cosq_sb, "cosq"), (sinq_sb, "sinq")]:
        OP("g", lambda e, d=dst, s=src: e.dma_start(out=d[:], in_=D[s]), inc="sWsm", incn=16)
    w_smallw = N("sWsm")
    w_tables = N("sTbl")

    p1 = ExitStack()
    xbuf = [alloc(p1, f"xbuf{i}", [128, HID], BF16, side="right") for i in range(2)]
    xn_grp = alloc(p1, "xn_grp", [128, NHC * 512], BF16, side="right")
    kvK = [alloc(p1, f"kvK{t}", [128, 512], BF16, side="right") for t in range(4)]
    ckv_sb = alloc(p1, "ckv_sb", [128, 1024], BF16, side="right")
    krt1 = alloc(p1, "krt1", [128, 256], BF16, side="right")
    krt2 = alloc(p1, "krt2", [128, 256], BF16, side="right")
    sqb = [alloc(p1, f"sqb{i}", [128, HID], BF16, side="right") for i in range(3)]
    stat1 = alloc(p1, "stat1", [128, 24], F32, side="right")
    stat2 = alloc(p1, "stat2", [128, 2], F32, side="right")
    stat3 = alloc(p1, "stat3", [128, 1], F32, side="right")
    stat4 = alloc(p1, "stat4", [128, 1], F32, side="right")
    ksq_sb = alloc(p1, "ksq_sb", [128, 512], BF16, side="right")
    kn_acc = alloc(p1, "kn_acc", [128, 8], BF16, side="right")
    kntmp = alloc(p1, "kntmp", [128, 8], F32, side="right")

    psum1 = ExitStack()
    psT = [psum1.enter_context(nc.psum_tensor(f"psT{i}", [128, 1024], BF16)) for i in range(4)]
    psKG = [psum1.enter_context(nc.psum_tensor(f"psKG{i}", [128, 264], F32)) for i in range(2)]
    psC = psum1.enter_context(nc.psum_tensor("psC", [128, 512], F32))
    psD = psum1.enter_context(nc.psum_tensor("psD", [128, 512], F32))

    OP("v", lambda e: e.memset(
        AP(V_sb, 64, [[NKV * NT * 65, 128], [65, NKV * NT], [1, 1]]), 1.0))
    OP("v", lambda e: e.memset(qacc[:], 0.0))
    OP("v", lambda e: e.memset(qpad[:], 0.0))
    OP("v", lambda e: e.memset(kn_acc[:], 0.0), inc="sV")
    v_ones = N("sV")

    qproj_done = [0] * 16
    qevac_done = [0] * 16
    rope_done_q = [0] * 16
    v_sq = [0] * 16
    t_qnorm = [0] * 16
    v_qred = [0] * 16
    qw_ld = [0] * 16
    mh_ready = [0] * NKV
    scores_done = [0] * NKV

    ln_done = [0] * NT
    vstats_done = [0] * NT
    stat2_free = [0] * NT
    xn_done = [[] for _ in range(NT)]
    xnT_done = [0] * NT
    rope_done = [0] * NT
    psT_free = [None, None, None, None]
    psKG_free = [0, 0]
    kvk_done = [0] * 4
    kvKT_done = [None, None, None, None]
    vav_done = [0] * NT
    K_done = [0] * NT
    t_upkv_last = [0]
    t_down_last = [0]
    s_ckv_box = [0]

    def tile_of(s_):
        g_, t_ = divmod(s_, 4)
        if t_ % 2 == 1:
            return ("odd", 2 * g_ + (t_ // 2))
        return ("grp", t_ // 2)

    def xn_stat_ap(s_, hc):
        kind, idx = tile_of(s_)
        if kind == "odd":
            return xn_odd[:, hc * 1024 + idx * 128: hc * 1024 + idx * 128 + 128]
        return xn_grp[:, hc * 512 + idx * 128: hc * 512 + idx * 128 + 128]

    W("t", "sTbl", w_tables)
    W("t", "sWsm", w_smallw)

    def p1_head_front(s_):
        g, t = divmod(s_, 4)
        sx = ["sXa", "sXb"][s_ % 2]
        if s_ >= 2:
            W("g", "sV", vstats_done[s_ - 2])
            W("g", "sS", ln_done[s_ - 2])
        OP("g", lambda e, d=xbuf[s_ % 2], s2=s_: e.dma_start(
            out=d[:], in_=D["x_kv"][s2 * 128:(s2 + 1) * 128, :]), inc=sx, incn=16)
        w_x = N(sx)

        W("v", sx, w_x)
        xb = xbuf[s_ % 2]
        for i4 in range(4):
            OP("v", lambda e, x=xb, i4=i4: e.bn_stats(
                stat1[:, i4 * 6:(i4 + 1) * 6], x[:, i4 * 512:(i4 + 1) * 512]))
        DR("v")
        if s_ >= 1:
            W("v", "sS", stat2_free[s_ - 1])
        OP("v", lambda e: e.bn_aggr(stat2[:], stat1[:]))
        DR("v")
        OP("v", lambda e: e.tensor_scalar_add(stat2[:, 1:2], stat2[:, 1:2], EPS),
           inc="sV")
        vstats_done[s_] = N("sV")
        W("s", "sV", vstats_done[s_])
        if s_ >= 3:
            W("s", "sT", xnT_done[s_ - 3])
        OP("s", lambda e: e.activation(out=stat3[:], in_=stat2[:, 1:2], func=AF.Ln))
        DR("s")
        OP("s", lambda e: e.activation(out=stat3[:], in_=stat3[:], func=AF.Exp,
                                       scale=-0.5))
        DR("s")
        OP("s", lambda e: e.activation(out=stat4[:], in_=stat2[:, 0:1],
                                       func=AF.Copy, scale=stat3[:, 0:1]), inc="sS")
        stat2_free[s_] = N("sS")
        DR("s")
        OP("s", lambda e: e.activation(out=stat4[:], in_=stat4[:],
                                       func=AF.Copy, scale=-1.0))
        DR("s")
        OP("s", lambda e, x=xb, s2=s_: e.activation(
            out=sqb[s2 % 3][:], in_=x[:], func=AF.Identity,
            scale=stat3[:, 0:1], bias=stat4[:, 0:1]), inc="sS")
        ln_done[s_] = N("sS")
        DR("s")

    def p1_head_tr(s_):
        g, t = divmod(s_, 4)
        W("t", "sS", ln_done[s_])
        kind, idx = tile_of(s_)
        t_tr = 0
        for batch in range(2):
            slot = (s_ % 2) * 2 + batch
            if psT_free[slot]:
                W("t", psT_free[slot][0], psT_free[slot][1])
            for k in range(8):
                hc = batch * 8 + k
                OP("t", lambda e, hc=hc, b=slot, s2=s_: e.transpose(
                    psT[b][:, (hc % 8) * 128:(hc % 8) * 128 + 128],
                    sqb[s2 % 3][:, hc * 128:hc * 128 + 128], identb_sb[:]),
                   inc=("sT" if k == 7 else None))
            t_tr = N("sT")
            ee = "v" if batch == 0 else "s"
            W(ee, "sT", t_tr)
            if kind == "grp" and g >= 1:
                W(ee, "sT", t_down_last[0])
            if kind == "odd":
                dst = AP(xn_odd, (batch * 8) * 1024 + idx * 128,
                         [[NHC * 1024, 128], [1024, 8], [1, 128]])
            else:
                dst = AP(xn_grp, (batch * 8) * 512 + idx * 128,
                         [[NHC * 512, 128], [512, 8], [1, 128]])
            if batch == 0:
                OP("v", lambda e, d=dst, b=slot: e.tensor_copy(d, psT[b][:, 0:1024]),
                   inc="sV")
                psT_free[slot] = ("sV", N("sV"))
            else:
                OP("s", lambda e, d=dst, b=slot: e.copy(out=d, in_=psT[b][:, 0:1024]),
                   inc="sS")
                psT_free[slot] = ("sS", N("sS"))
        xn_done[s_] = [psT_free[(s_ % 2) * 2], psT_free[(s_ % 2) * 2 + 1]]
        xnT_done[s_] = t_tr

    v_cw_done = [0] * NOWN

    def emit_gate_softmax(j):
        lg = logits_sb[:, j * 8:j * 8 + 8]
        DR("v")
        OP("v", lambda e, l=lg: e.reduce_max(gm1[:], l, axis=AX.X))
        OP("v", lambda e, l=lg: e.reduce_max(gm1n[:], l, axis=AX.X, negate=True))
        DR("v")
        OP("v", lambda e, l=lg: e.tensor_scalar(out=gs1[:], in0=l, scalar1=gm1[:],
                                                scalar2=None, op0=ALU.is_ge))
        DR("v")
        OP("v", lambda e, l=lg: e.scalar_tensor_tensor(
            out=gs2[:], in0=gs1[:], scalar=NEG_BIG, in1=l, op0=ALU.mult, op1=ALU.add))
        DR("v")
        OP("v", lambda e: e.reduce_max(gm2[:], gs2[:], axis=AX.X))
        DR("v")
        OP("v", lambda e, l=lg: e.tensor_scalar(out=gs2[:], in0=l, scalar1=gm2[:],
                                                scalar2=None, op0=ALU.is_ge), inc="sV")
        v_g1 = N("sV")
        W("s", "sV", v_g1)
        OP("s", lambda e, l=lg: e.activation(out=gs1[:], in_=l, func=AF.Exp,
                                             bias=gm1n[:]), inc="sS")
        s_g = N("sS")
        W("v", "sS", s_g)
        OP("v", lambda e: e.tensor_mul(gs1[:], gs1[:], gs2[:]))
        DR("v")
        OP("v", lambda e: e.reduce_sum(gden[:], gs1[:], axis=AX.X))
        DR("v")
        OP("v", lambda e: e.reciprocal(gden[:], gden[:]))
        DR("v")
        OP("v", lambda e, j=j: e.tensor_scalar(
            out=cwb_sb[:, j * 8:j * 8 + 8], in0=gs1[:], scalar1=gden[:],
            scalar2=16.0, op0=ALU.mult, op1=ALU.mult), inc="sV")
        v_cw_done[j] = N("sV")

    def p1_krgate_rope(s_):
        g, t = divmod(s_, 4)
        kind, idx = tile_of(s_)
        for se2, v2 in xn_done[s_]:
            W("t", se2, v2)
        W("t", "sV", psKG_free[s_ % 2])
        for hc in range(NHC):
            OP("t", lambda e, hc=hc, s2=s_: e.matmul(
                psKG[s2 % 2][:, 0:264], xn_stat_ap(s2, hc),
                krg_sb[:, hc * 264:hc * 264 + 264],
                start=(hc == 0), stop=(hc == NHC - 1)),
               inc=("sT" if hc == NHC - 1 else None))
        krg_done = N("sT")

        W("v", "sT", krg_done)
        W("v", "sTbl", w_tables)
        if kvKT_done[t]:
            W("v", kvKT_done[t][0], kvKT_done[t][1])
        pk = psKG[s_ % 2]
        cosrep = AP(cosk_sb, s_ * 32, [[NT * 32, 128], [0, 8], [1, 32]])
        sinrep = AP(sink_sb, s_ * 32, [[NT * 32, 128], [0, 8], [1, 32]])
        OP("v", lambda e, p=pk, c=cosrep: e.tensor_mul(krt1[:], p[:, 0:256], c))
        shuf = AP(pk, 16, [[264, 128], [32, 8], [-16, 2], [1, 16]])
        OP("v", lambda e, sh=shuf, sr=sinrep: e.tensor_mul(krt2[:], sh, sr))
        DR("v")
        kdst = AP(kvK[t], 0, [[512, 128], [64, 8], [32, 2], [1, 16]])
        OP("v", lambda e, d=kdst: e.tensor_add(d, krt1[:], krt2[:]))
        if kind == "odd":
            OP("v", lambda e, p=pk, j=idx: e.tensor_copy(
                logits_sb[:, j * 8:j * 8 + 8], p[:, 256:264]))
        OP("v", lambda e: e.engine_nop(), inc="sV")
        rope_done[s_] = N("sV")
        psKG_free[s_ % 2] = rope_done[s_]
        if kind == "odd":
            emit_gate_softmax(idx)

    def p1_down(g):
        for s_ in range(4 * g, 4 * g + 4):
            for se2, v2 in xn_done[s_]:
                W("t", se2, v2)
        for cc in range(2):
            if g >= 1:
                W("t", "sV", vav_done[4 * g - 2 + cc])
            tgt = psC if cc == 0 else psD
            for hc in range(NHC):
                stat = down_sb[:, hc * 256 + cc * 128: hc * 256 + cc * 128 + 128]
                mov_o = xn_odd[:, hc * 1024 + 2 * g * 128: hc * 1024 + 2 * g * 128 + 256]
                mov_g = xn_grp[:, hc * 512: hc * 512 + 256]
                OP("t", lambda e, p=tgt, st=stat, m=mov_o, hc=hc: e.matmul(
                    p[:, 0:256], st, m, start=(hc == 0), stop=False))
                OP("t", lambda e, p=tgt, st=stat, m=mov_g, hc=hc: e.matmul(
                    p[:, 256:512], st, m, start=False, stop=(hc == NHC - 1)),
                   inc=("sT" if hc == NHC - 1 else None))
        t_down_last[0] = N("sT")

        W("s", "sT", t_down_last[0])
        if g >= 1:
            W("s", "sT", t_upkv_last[0])
        OP("s", lambda e: e.copy(out=ckv_sb[:, 0:512], in_=psC[:]))
        DR("s")
        OP("s", lambda e: e.copy(out=ckv_sb[:, 512:1024], in_=psD[:]), inc="sS")
        s_ckv_box[0] = N("sS")

    def ckv_col(t_):
        return (t_ // 2) * 128 if t_ % 2 == 1 else 256 + (t_ // 2) * 128

    def p1_upkv(g, t):
        s_ = 4 * g + t
        pk_ = psKG[t % 2]
        pv_ = psC if t % 2 == 0 else psD
        W("t", "sS", s_ckv_box[0])
        W("t", "sV", rope_done[s_])
        if 4 * g + t + 2 < NT:
            W("t", "sV", rope_done[4 * g + t + 2])
        if t >= 2:
            W("t", "sV", vav_done[4 * g + t - 2])
        for cc in range(2):
            stat = ckv_sb[:, cc * 512 + ckv_col(t): cc * 512 + ckv_col(t) + 128]
            OP("t", lambda e, p=pk_, st=stat, cc=cc: e.matmul(
                p[:, 0:256], st, upk_sb[:, cc * 256:cc * 256 + 256],
                start=(cc == 0), stop=(cc == 1)))
            OP("t", lambda e, p=pv_, st=stat, cc=cc: e.matmul(
                p[:, 0:512], st, upv_sb[:, cc * 512:cc * 512 + 512],
                start=(cc == 0), stop=(cc == 1)),
               inc=("sT" if cc == 1 else None))
        t_upkv_last[0] = N("sT")

        W("v", "sT", t_upkv_last[0])
        W("v", "sV", v_ones)
        kcdst = AP(kvK[t], 16, [[512, 128], [64, 8], [32, 2], [1, 16]])
        OP("v", lambda e, d=kcdst, p=pk_: e.tensor_copy(d, p[:, 0:256]), inc="sV")
        kvk_done[t] = N("sV")
        psKG_free[t % 2] = kvk_done[t]
        DR("v")
        OP("v", lambda e, t2=t: e.tensor_mul(ksq_sb[:], kvK[t2][:], kvK[t2][:]))
        DR("v")
        OP("v", lambda e: e.reduce_sum(
            kntmp[:], AP(ksq_sb, 0, [[512, 128], [64, 8], [1, 64]]), axis=AX.X))
        DR("v")
        OP("v", lambda e: e.tensor_max(kn_acc[:], kn_acc[:], kntmp[:]))
        DR("v")
        for h in range(NKV):
            OP("v", lambda e, p=pv_, h=h, s2=s_: e.tensor_copy(
                V_sb[:, (h * NT + s2) * 65:(h * NT + s2) * 65 + 64],
                p[:, h * 64:h * 64 + 64]),
               inc=("sV" if h == NKV - 1 else None))
        vav_done[s_] = N("sV")

        se_ = "sKTa" if t % 2 == 0 else "sKTb"
        W("y", "sV", kvk_done[t])
        for p in range(4):
            OP("y", lambda e, p=p, t2=t, s2=s_: e.dma_start_transpose(
                out=Kp_sb[:, p * S + s2 * 128: p * S + s2 * 128 + 128],
                in_=kvK[t2][:, p * 128:p * 128 + 128]),
               inc=se_, incn=16)
        kvKT_done[t] = (se_, N(se_))
        K_done[s_] = kvKT_done[t]

    for i in range(21):
        if i < 16:
            p1_head_front(i)
        if 1 <= i <= 16:
            p1_head_tr(i - 1)
        if i >= 5 and (i - 5) // 4 <= 3:
            p1_upkv((i - 5) // 4, (i - 5) % 4)
        if 2 <= i <= 17:
            p1_krgate_rope(i - 2)
        if i >= 4 and (i - 4) % 4 == 0 and i <= 16:
            p1_down((i - 4) // 4)

    W("t", "sV", vav_done[NT - 1])
    if psT_free[0]:
        W("t", psT_free[0][0], psT_free[0][1])
    OP("t", lambda e: e.transpose(psT[0][0:8, 0:128], kn_acc[:], identb_sb[:]), inc="sT")
    t_kn = N("sT")
    W("v", "sT", t_kn)
    OP("v", lambda e: e.reduce_max(kmax8_sb[:], psT[0][0:8, 0:128], axis=AX.X))
    DR("v")
    OP("v", lambda e: e.tensor_copy(kmax8b_sb[:], kmax8_sb[:]), inc="sV")
    v_nk = N("sV")
    W("t", "sV", v_nk)
    OP("t", lambda e: e.transpose(psT[0][0:1, 256:264], kmax8b_sb[:],
                                  identb_sb[0:8, 0:8]), inc="sT")
    t_kr = N("sT")
    W("v", "sT", t_kr)
    OP("v", lambda e: e.tensor_copy(kmaxrow_sb[:], psT[0][0:1, 256:264]), inc="sV")
    v_nkr = N("sV")

    W("t", "sV", vav_done[NT - 1])
    for se2, v2 in xn_done[NT - 1]:
        W("t", se2, v2)
    for g2 in range(2):
        tgtA = psC if g2 == 0 else psD
        for hc in range(NHC):
            stat = acat_sb[:, hc * 128:hc * 128 + 128]
            mov = xn_odd[:, hc * 1024 + g2 * 512: hc * 1024 + g2 * 512 + 512]
            OP("t", lambda e, p=tgtA, st=stat, m=mov, hc=hc: e.matmul(
                p[:, 0:512], st, m, start=(hc == 0), stop=(hc == NHC - 1)),
               inc=("sT" if hc == NHC - 1 else None))
    t_u = N("sT")
    W("s", "sT", t_u)
    OP("s", lambda e: e.copy(out=u_sb[:, 0:512], in_=psC[:]))
    DR("s")
    OP("s", lambda e: e.copy(out=u_sb[:, 512:1024], in_=psD[:]), inc="sS")
    s_u = N("sS")

    QWPRE_COL = {0: 14336, 1: 12288}
    qraw_pre = [g_fm[:, 10240 + 1024 * qc: 10240 + 1024 * qc + 1024] for qc in range(2)]
    sqq_pre = [g_fm[:, 8192 + 1024 * qc: 8192 + 1024 * qc + 1024] for qc in range(2)]
    qpart_pre = g_fm[:, 7168:8192]
    for qc in range(2):
        sq_ = "sQa" if qc == 0 else "sQb"
        OP("g", lambda e, qc=qc: e.dma_start(
            out=g_fm[:, QWPRE_COL[qc]:QWPRE_COL[qc] + 2048],
            in_=D["qw_s"][:, qc * 2048:(qc + 1) * 2048]), inc=sq_, incn=16)
        qw_ld[qc] = N(sq_)

    pre_evac = [[0, 0], [0, 0]]
    v_rope_pre = [0, 0]

    def pre_qp(qc, g2):
        sq_ = "sQa" if qc == 0 else "sQb"
        tgt = psC if g2 == 0 else psD
        W("t", sq_, qw_ld[qc])
        if qc == 0 and g2 == 0:
            for se2, v2 in xn_done[NT - 1]:
                W("t", se2, v2)
        W("t", "sS", s_u)
        if qc == 1:
            W("t", "sS", pre_evac[0][g2])
        for hc in range(NHC):
            stat = g_fm[:, QWPRE_COL[qc] + hc * 128: QWPRE_COL[qc] + hc * 128 + 128]
            mov = xn_odd[:, hc * 1024 + g2 * 512: hc * 1024 + g2 * 512 + 512]
            OP("t", lambda e, p=tgt, st=stat, m=mov, hc=hc: e.matmul(
                p[:, 0:512], st, m, start=(hc == 0), stop=(hc == NHC - 1)),
               inc=("sT" if hc == NHC - 1 else None))
        qp_h = N("sT")
        if g2 == 1:
            qproj_done[qc] = qp_h
        W("s", "sT", qp_h)
        OP("s", lambda e, qc=qc, g2=g2, p=tgt: e.copy(
            out=qraw_pre[qc][:, g2 * 512:g2 * 512 + 512], in_=p[:, 0:512]),
           inc="sS")
        pre_evac[qc][g2] = N("sS")
        if g2 == 1:
            qevac_done[qc] = pre_evac[qc][1]

    def pre_shuffle_sq(qc):
        W("v", "sS", pre_evac[qc][1])
        for blk in range(4):
            OP("v", lambda e, qc=qc, b=blk: e.tensor_copy(
                qpart_pre[b * 32:b * 32 + 32, :],
                qraw_pre[qc][(b ^ 1) * 32:(b ^ 1) * 32 + 32, :]))
        OP("v", lambda e, qc=qc: e.tensor_mul(
            sqq_pre[qc][:], qraw_pre[qc][:], qraw_pre[qc][:]))
        DR("v")
        OP("v", lambda e, qc=qc: e.tensor_max(
            sqq_pre[qc][:, 0:512], sqq_pre[qc][:, 0:512], sqq_pre[qc][:, 512:1024]),
           inc="sV")
        v_sq[qc] = N("sV")
        DR("v")

    def pre_rope(qc):
        if qc == 0:
            W("v", "sWsm", w_smallw)
        OP("v", lambda e, qc=qc: e.tensor_mul(qraw_pre[qc][:], qraw_pre[qc][:],
                                              cosq_sb[:]))
        OP("v", lambda e: e.tensor_mul(qpart_pre[:], qpart_pre[:], sinq_sb[:]))
        DR("v")
        for half in range(2):
            gl = 2 * qc + half
            dst = qkv_sb[0:64, gl * 1024: gl * 1024 + 1024]
            OP("v", lambda e, qc=qc, d=dst, hf=half: e.tensor_add(
                d, qraw_pre[qc][hf * 64:hf * 64 + 64, :],
                qpart_pre[hf * 64:hf * 64 + 64, :]))
        DR("v")
        OP("v", lambda e: e.engine_nop(), inc="sV")
        rope_done_q[qc] = N("sV")

    pre_chunks = [
        lambda: pre_qp(0, 0), lambda: pre_qp(0, 1),
        lambda: pre_qp(1, 0), lambda: pre_qp(1, 1),
        lambda: pre_shuffle_sq(0), lambda: pre_rope(0),
        lambda: pre_shuffle_sq(1), lambda: pre_rope(1),
    ]

    tsc_done = [0] * NOWN
    tsc_vread = [0, 0]
    for j in range(NOWN):
        if j < len(pre_chunks):
            pre_chunks[j]()
        bj = j % 2
        W("t", "sV", v_cw_done[j])
        if j == 0:
            W("t", "sV", v_nkr)
        if j >= 2:
            W("t", "sV", tsc_vread[bj])
        OP("t", lambda e, j=j, bj=bj: e.transpose(
            psT[bj][0:8, 128:256], cwb_sb[:, j * 8:j * 8 + 8], identb_sb[:]),
           inc="sT")
        t_cwT = N("sT")
        W("v", "sT", t_cwT)
        OP("v", lambda e, bj=bj: e.tensor_copy(
            cwf_sb[0:8, bj * 128:bj * 128 + 128], psT[bj][0:8, 128:256]), inc="sV")
        v_cwf = N("sV")
        W("t", "sV", v_cwf)
        if j == 0:
            W("t", "sV", kvk_done[3])
        if j == 1:
            W("t", "sV", kvk_done[2])
        OP("t", lambda e, bj=bj: e.matmul(
            psKG[bj][:, 0:128], e8_sb[:], cwf_sb[0:8, bj * 128:bj * 128 + 128],
            start=True, stop=True), inc="sT")
        t_e8 = N("sT")
        W("v", "sT", t_e8)
        W("v", "sS", s_u)
        OP("v", lambda e, j=j, bj=bj: e.tensor_mul(
            t_sb[:, j * 128:j * 128 + 128], u_sb[:, j * 128:j * 128 + 128],
            psKG[bj][:, 0:128]), inc="sV")
        tsc_done[j] = N("sV")
        tsc_vread[bj] = tsc_done[j]
    for ch in pre_chunks[NOWN:]:
        ch()

    for qc in range(2):
        tgt = psC if qc == 0 else psD
        W("t", "sV", v_sq[qc])
        W("t", "sS", pre_evac[1][qc])
        OP("t", lambda e, qc=qc, p=tgt: e.matmul(
            p[0:2, 0:512], ones2c_sb[:], sqq_pre[qc][:, 0:512],
            start=True, stop=True), inc="sT")
        t_qnorm[qc] = N("sT")
        W("v", "sT", t_qnorm[qc])
        OP("v", lambda e, p=tgt: e.reduce_max(qnt[:], p[0:2, 0:512], axis=AX.X))
        DR("v")
        OP("v", lambda e: e.tensor_max(qacc[:, 0:1], qacc[:, 0:1], qnt[:]))
        DR("v")
        OP("v", lambda e: e.engine_nop(), inc="sV")
        v_qred[qc] = N("sV")
    OP("v", lambda e: e.tensor_copy(qpad[0:2, 0:1], qacc[:, 0:1]))
    DR("v")
    OP("v", lambda e: e.transpose(qpadT[:], qpad[:]))
    DR("v")
    OP("v", lambda e: e.reduce_max(mtmpA[:], qpadT[0:1, 0:2], axis=AX.X))
    DR("v")
    OP("v", lambda e: e.tensor_mul(mtmpA[:], mtmpA[:], kmaxrow_sb[0:1, 0:1]),
       inc="sV")
    v_m1 = N("sV")
    W("s", "sV", v_m1)
    OP("s", lambda e: e.activation(out=mtmpA[:], in_=mtmpA[:], func=AF.Ln))
    DR("s")
    OP("s", lambda e: e.activation(out=mtmpA[:], in_=mtmpA[:], func=AF.Exp,
                                   scale=0.5), inc="sS")
    s_m0 = N("sS")
    W("v", "sS", s_m0)
    OP("v", lambda e: e.tensor_scalar_mul(mtmpA[:], mtmpA[:], -0.1259765625),
       inc="sV")
    v_m2 = N("sV")
    W("g", "sV", v_m2)
    OP("g", lambda e: e.partition_broadcast(negm_sb[:, 0:1], mtmpA[0:1, 0:1]),
       inc="sG")
    mh_ready[0] = N("sG")

    DBG("xn_odd", xn_odd, list(xn_done[NT - 1]))
    DBG("Kp", Kp_sb, [K_done[NT - 1]])
    DBG("V", V_sb, [("sV", vav_done[NT - 1])])
    DBG("logits", logits_sb, [("sV", rope_done[NT - 1])])

    kta_total = N("sKTa")
    ktb_total = N("sKTb")
    p1_t_end = N("sT")
    p1_v_end = N("sV")
    p1_s_end = N("sS")
    psum1.close()
    p1.close()
    p01.close()

    p2 = ExitStack()
    PT = [alloc(p2, f"PT{i}", [128, 1024], BF16) for i in range(4)]
    attn_fm = alloc(p2, "attn_fm", [128, 16 * 1024], BF16)
    rlbc_sb = alloc(p2, "rlbc_sb", [64, 512], F32)
    lpad = alloc(p2, "lpad", [32, 512], F32)
    lT = alloc(p2, "lT", [32, 512], F32)
    rT = alloc(p2, "rT", [32, 512], F32)
    rrow = alloc(p2, "rrow", [32, 512], F32)
    fwbuf = [alloc(p12, f"fwbuf{i}", [128, 2048], BF16, side="right") for i in range(2)]
    qwbuf = [alloc(p12, f"qwbuf{i}", [128, 2048], BF16, side="right") for i in range(2)]
    sqq = [alloc(p12, f"sqq{i}", [128, 1024], BF16, side="right") for i in range(2)]
    qraw = [alloc(p12, f"qraw{i}", [128, 1024], BF16, side="right") for i in range(2)]
    qpart = alloc(p2, "qpart", [128, 1024], BF16)

    QR_PITCH = 8192

    psum2 = ExitStack()
    psS_ = [psum2.enter_context(nc.psum_tensor(f"psS{i}", [128, 1024], F32)) for i in range(2)]
    psAV = [psum2.enter_context(nc.psum_tensor(f"psAV{i}", [65, 512], F32))
            for i in range(2)]
    psQP = psum2.enter_context(nc.psum_tensor("psQP", [128, 512], F32))
    psDen = psum2.enter_context(nc.psum_tensor("psDen", [128, 512], F32))

    OP("v", lambda e: e.memset(lpad[:], 1.0))
    OP("v", lambda e: e.memset(rT[:], 1.0), inc="sV")
    v_rlinit = N("sV")

    psQP_vfree = [0]

    def emit_qw_dma(qc):
        sq_ = "sQa" if qc % 2 == 0 else "sQb"
        if qc < 2:
            W("g", "sT", p1_t_end)
            W("g", "sV", N("sV"))
            W("g", "sS", N("sS"))
        else:
            W("g", "sT", qproj_done[qc - 2])
        OP("g", lambda e, qc=qc: e.dma_start(
            out=qwbuf[qc % 2][:], in_=D["qw_s"][:, qc * 2048:(qc + 1) * 2048]),
           inc=sq_, incn=16)
        qw_ld[qc] = N(sq_)

    def gen_qpair(qc):
        h = qc // 2
        sq_ = "sQa" if qc % 2 == 0 else "sQb"
        qevac_half = [0, 0]
        for g2 in range(2):
            for u in range(4):
                if u == 0:
                    if g2 == 0:
                        W("t", sq_, qw_ld[qc])
                        if qc == 0:
                            W("t", "sT", p1_t_end)
                        if qc >= 1:
                            W("t", "sS", qevac_done[qc - 1])
                        W("t", "sV", psQP_vfree[0])
                    else:
                        W("t", "sS", qevac_half[0])
                for hc in range(4 * u, 4 * u + 4):
                    stat = qwbuf[qc % 2][:, hc * 128:hc * 128 + 128]
                    mov = xn_odd[:, hc * 1024 + g2 * 512: hc * 1024 + g2 * 512 + 512]
                    OP("t", lambda e, st=stat, m=mov, hc=hc: e.matmul(
                        psQP[:, 0:512], st, m,
                        start=(hc == 0), stop=(hc == NHC - 1)),
                       inc=("sT" if hc == NHC - 1 else None))
                if u == 3:
                    qp_half = N("sT")
                    if g2 == 1:
                        qproj_done[qc] = qp_half
                    W("s", "sT", qp_half)
                    if g2 == 0 and qc >= 2:
                        W("s", "sV", rope_done_q[qc - 2])
                    OP("s", lambda e, qc=qc, g2=g2: e.copy(
                        out=qraw[qc % 2][:, g2 * 512:g2 * 512 + 512],
                        in_=psQP[:, 0:512]), inc="sS")
                    qevac_half[g2] = N("sS")
                yield
        qevac_done[qc] = qevac_half[1]

        W("v", "sS", qevac_done[qc])
        if qc >= 2:
            W("v", "sT", t_qnorm[qc - 2])
        if h >= 4:
            W("v", "sT", scores_done[h - 4])
        for blk in range(4):
            OP("v", lambda e, qc=qc, b=blk: e.tensor_copy(
                qpart[b * 32:b * 32 + 32, :],
                qraw[qc % 2][(b ^ 1) * 32:(b ^ 1) * 32 + 32, :]))
        yield
        OP("v", lambda e, qc=qc: e.tensor_mul(
            sqq[qc % 2][:], qraw[qc % 2][:], qraw[qc % 2][:]))
        DR("v")
        OP("v", lambda e, qc=qc: e.tensor_max(
            sqq[qc % 2][:, 0:512], sqq[qc % 2][:, 0:512], sqq[qc % 2][:, 512:1024]),
           inc="sV")
        v_sq[qc] = N("sV")
        DR("v")
        yield
        OP("v", lambda e, qc=qc: e.tensor_mul(qraw[qc % 2][:], qraw[qc % 2][:],
                                              cosq_sb[:]))
        OP("v", lambda e: e.tensor_mul(qpart[:], qpart[:], sinq_sb[:]))
        DR("v")
        rofs_q = (h % 2) * 64
        colb = ((h % 4) // 2) * 4096
        for half in range(2):
            gl = 2 * (qc % 2) + half
            dst = qkv_sb[rofs_q:rofs_q + 64, colb + gl * 1024: colb + gl * 1024 + 1024]
            OP("v", lambda e, qc=qc, d=dst, hf=half: e.tensor_add(
                d, qraw[qc % 2][hf * 64:hf * 64 + 64, :],
                qpart[hf * 64:hf * 64 + 64, :]))
        DR("v")
        OP("v", lambda e: e.engine_nop(), inc="sV")
        rope_done_q[qc] = N("sV")
        yield

    def emit_norm(qc, evac_tgt):
        W("t", "sV", v_sq[qc])
        W("t", "sS", evac_tgt)
        W("t", "sV", psQP_vfree[0])
        OP("t", lambda e, qc=qc: e.matmul(
            psQP[0:2, 0:512], ones2c_sb[:], sqq[qc % 2][:, 0:512],
            start=True, stop=True), inc="sT")
        t_qnorm[qc] = N("sT")
        W("v", "sT", t_qnorm[qc])
        OP("v", lambda e: e.reduce_max(qnt[:], psQP[0:2, 0:512], axis=AX.X))
        DR("v")
        hcol = qc // 2
        OP("v", lambda e, hp=hcol: e.tensor_max(
            qacc[:, hp:hp + 1], qacc[:, hp:hp + 1], qnt[:]))
        DR("v")
        OP("v", lambda e: e.engine_nop(), inc="sV")
        v_qred = N("sV")
        psQP_vfree[0] = v_qred

    def emit_mhat(hh):
        DR("v")
        OP("v", lambda e, hh=hh: e.tensor_copy(qpad[0:2, 0:1], qacc[:, hh:hh + 1]))
        DR("v")
        OP("v", lambda e: e.transpose(qpadT[:], qpad[:]))
        DR("v")
        OP("v", lambda e: e.reduce_max(mtmpA[:], qpadT[0:1, 0:2], axis=AX.X))
        DR("v")
        OP("v", lambda e, hh=hh: e.tensor_mul(
            mtmpA[:], mtmpA[:], kmaxrow_sb[0:1, hh:hh + 1]), inc="sV")
        v_m1 = N("sV")
        W("s", "sV", v_m1)
        OP("s", lambda e: e.activation(out=mtmpA[:], in_=mtmpA[:], func=AF.Ln))
        DR("s")
        OP("s", lambda e: e.activation(out=mtmpA[:], in_=mtmpA[:], func=AF.Exp,
                                       scale=0.5), inc="sS")
        s_m = N("sS")
        W("v", "sS", s_m)
        OP("v", lambda e: e.tensor_scalar_mul(mtmpA[:], mtmpA[:], -0.1259765625),
           inc="sV")
        v_m2 = N("sV")
        W("g", "sV", v_m2)
        OP("g", lambda e, hh=hh: e.partition_broadcast(
            negm_sb[:, hh:hh + 1], mtmpA[0:1, 0:1]), inc="sG")
        mh_ready[hh] = N("sG")

    def gen_qblock(h):
        yield from gen_qpair(2 * h + 2)
        if 2 * h + 4 < 16:
            emit_qw_dma(2 * h + 4)
        yield from gen_qpair(2 * h + 3)
        if 2 * h + 5 < 16:
            emit_qw_dma(2 * h + 5)
        yield
        yield
        yield
        yield
        yield
        yield
        emit_norm(2 * h + 2, qevac_done[2 * h + 3])
        yield
        emit_norm(2 * h + 3, qevac_done[2 * h + 3])
        yield
        emit_mhat(h + 1)
        yield

    dense_done = [0] * 16
    evac_g = [None] * 16

    fw_ld = [0] * 16

    def emit_fw_dma(oc):
        sf = "sFa" if oc % 2 == 0 else "sFb"
        if oc >= 2:
            W("g", "sT", dense_done[oc - 2])
        else:
            W("g", "sT", p1_t_end)
            W("g", "sV", N("sV"))
            W("g", "sS", N("sS"))
        OP("g", lambda e, oc=oc: e.dma_start(
            out=fwbuf[oc % 2][:], in_=D["fw_s"][:, oc * 2048:(oc + 1) * 2048]),
           inc=sf, incn=16)
        fw_ld[oc] = N(sf)

    def dense_units():
        emit_fw_dma(0)
        emit_fw_dma(1)
        for oc in range(16):
            sf = "sFa" if oc % 2 == 0 else "sFb"
            v_half = [0, 0]
            for g2 in range(2):
                for u in range(8):
                    if u == 0:
                        if g2 == 0:
                            W("t", sf, fw_ld[oc])
                            if oc == 0:
                                W("t", "sV", tsc_done[NOWN - 1])
                            else:
                                W("t", evac_g[oc - 1][0], evac_g[oc - 1][1])
                        else:
                            W("t", v_half[0][0], v_half[0][1])
                    for hc in range(2 * u, 2 * u + 2):
                        stat = fwbuf[oc % 2][:, hc * 128:hc * 128 + 128]
                        mov = xn_odd[:, hc * 1024 + g2 * 512:
                                     hc * 1024 + g2 * 512 + 512]
                        OP("t", lambda e, st=stat, m=mov, hc=hc: e.matmul(
                            psDen[:, 0:512], st, m, start=(hc == 0), stop=False),
                           inc=("sT" if (hc == 15 and g2 == 1) else None))
                    yield
                if g2 == 1:
                    dense_done[oc] = N("sT")
                    if oc + 2 < 16:
                        emit_fw_dma(oc + 2)
                statB = bcat_sb[:, oc * 128:oc * 128 + 128]
                movB = t_sb[:, g2 * 512:g2 * 512 + 512]
                OP("t", lambda e, st=statB, m=movB: e.matmul(
                    psDen[:, 0:512], st, m, start=False, stop=True), inc="sT")
                t_b = N("sT")
                W("s", "sT", t_b)
                if oc in (8, 9):
                    W("s", "sT", t_qnorm[oc - 8])
                elif oc >= 12:
                    W("s", "sT", qproj_done[1])
                OP("s", lambda e, oc=oc, g2=g2: e.copy(
                    out=g_fm[:, oc * 1024 + g2 * 512: oc * 1024 + g2 * 512 + 512],
                    in_=psDen[:, 0:512]), inc="sS")
                v_half[g2] = ("sS", N("sS"))
                yield
                if g2 == 0:
                    yield
                    yield
            evac_g[oc] = v_half[1]

    dense_it = dense_units()
    dense_emitted = [0]
    DENSE_TOTAL = 16 * 20
    TOTAL_ITEMS = NKV * (NOWN * (NOWN + 1) // 2)

    exp_cnt_half = [0, 0]
    av_cnt_pt = [0, 0, 0, 0]
    mask_cnt = {}
    exp_item = {}
    evac_done = [0, 0]
    slot_par = {}
    av_last = {}
    first_item_of_head = [True] * NKV
    first_exp_of_head = [True] * NKV
    last_bc_box = [0]
    lpad_init_box = [v_rlinit]
    nslots = 0

    def emit_scores(i, it):
        h, j, sub = it
        p_pair = h // 2
        rofs = (h % 2) * 64
        half = i % 2
        W("t", "sS", exp_cnt_half[half])
        if first_item_of_head[h]:
            W("t", "sV", rope_done_q[2 * h + 1])
            if h == 0:
                W("t", "sKTa", kta_total)
                W("t", "sKTb", ktb_total)
            first_item_of_head[h] = False
        for ktl in range(2):
            kt = 2 * sub + ktl
            stat = Kp_sb[rofs:rofs + 64,
                         p_pair * S + kt * 128: p_pair * S + kt * 128 + 128]
            mov = AP(qkv_sb, (h % 2) * 64 * QR_PITCH + ((h % 4) // 2) * 4096 + j * 128,
                     [[QR_PITCH, 64], [1024, 4], [1, 128]])
            OP("t", lambda e, st=stat, m=mov, hf=half, k=ktl: e.matmul(
                psS_[hf][:, k * 512:k * 512 + 512], st, m, start=True, stop=True),
               inc=("sT" if ktl == 1 else None))
        t_sc = N("sT")
        scores_done[h] = t_sc
        W("s", "sT", t_sc)
        W("s", "sT", av_cnt_pt[i % 4])
        if first_exp_of_head[h]:
            W("s", "sG", mh_ready[h])
            first_exp_of_head[h] = False
        OP("s", lambda e, hf=half, pr=i % 4, h2=h: e.activation(
            out=PT[pr][:, 0:1024], in_=psS_[hf][:, 0:1024], func=AF.Exp,
            bias=negm_sb[:, h2:h2 + 1], scale=1.0), inc="sS")
        exp_cnt_half[half] = N("sS")
        exp_item[it] = exp_cnt_half[half]
        if sub == j:
            W("v", "sS", exp_item[it])
            OP("v", lambda e, pr=i % 4: e.tensor_scalar(
                out=PT[pr][:, 0:512], in0=PT[pr][:, 0:512],
                scalar1=flag_sb[:], scalar2=None, op0=ALU.mult))
            trirep = AP(trim_sb, 0, [[128, 128], [0, 4], [1, 128]])
            DR("v")
            OP("v", lambda e, pr=i % 4, tr=trirep: e.tensor_mul(
                PT[pr][:, 512:1024], PT[pr][:, 512:1024], tr), inc="sV")
            mask_cnt[(h, j)] = N("sV")

    def emit_av(i, it):
        h, j, sub = it
        L = 2 * j + 2
        par = slot_par[(h, j)]
        if sub == j:
            W("t", "sV", mask_cnt[(h, j)])
        else:
            W("t", "sS", exp_item[it])
        if sub == 0:
            W("t", "sV", evac_done[par])
        for ktl in range(2):
            kt = 2 * sub + ktl
            vstat = V_sb[:, (h * NT + kt) * 65:(h * NT + kt) * 65 + 65]
            OP("t", lambda e, vs=vstat, pr=i % 4, k=ktl, pa=par,
               st=(kt == 0), sp=(kt == L - 1): e.matmul(
                psAV[pa][:, 0:512], vs, PT[pr][:, k * 512:k * 512 + 512],
                start=st, stop=sp),
               inc=("sT" if ktl == 1 else None))
        av_cnt_pt[i % 4] = N("sT")

    def emit_slotend(h, j):
        par = slot_par[(h, j)]
        t_av = av_last[(h, j)]
        W("v", "sT", t_av)
        W("v", "sG", last_bc_box[0])
        W("v", "sV", lpad_init_box[0])
        OP("v", lambda e, pa=par: e.tensor_copy(lpad[0:1, 0:512],
                                                psAV[pa][64:65, 0:512]))
        DR("v")
        OP("v", lambda e: e.transpose(lT[:], lpad[:]))
        DR("v")
        OP("v", lambda e: e.reciprocal(AP(rT, 0, [[512, 32], [32, 16]]),
                                       AP(lT, 0, [[512, 32], [32, 16]])))
        DR("v")
        OP("v", lambda e: e.transpose(rrow[:], rT[:]), inc="sV")
        v_rl = N("sV")
        W("g", "sV", v_rl)
        OP("g", lambda e: e.partition_broadcast(rlbc_sb[:], rrow[0:1, 0:512]),
           inc="sG")
        last_bc_box[0] = N("sG")
        W("v", "sG", last_bc_box[0])
        for g_ in range(4):
            c = 2 * h + g_ // 2
            pofs = (g_ % 2) * 64
            OP("v", lambda e, g2=g_, c=c, po=pofs, j=j, pa=par: e.tensor_mul(
                attn_fm[po:po + 64, c * 1024 + j * 128: c * 1024 + j * 128 + 128],
                psAV[pa][0:64, g2 * 128:g2 * 128 + 128],
                rlbc_sb[0:64, g2 * 128:g2 * 128 + 128]),
               inc=("sV" if g_ == 3 else None))
        evac_done[par] = N("sV")

    from collections import deque
    pend = deque()

    def flush_av(all_=False):
        while len(pend) > (0 if all_ else 3):
            pi, pit = pend.popleft()
            emit_av(pi, pit)
            av_last[(pit[0], pit[1])] = N("sT")
            if pit[2] == pit[1]:
                emit_slotend(pit[0], pit[1])

    emit_qw_dma(2)
    emit_qw_dma(3)

    item_idx = 0
    for h in range(NKV):
        qgen = gen_qblock(h) if h <= 6 else None
        for j in range(NOWN):
            for sub in range(j + 1):
                it = (h, j, sub)
                if sub == 0:
                    slot_par[(h, j)] = nslots % 2
                    nslots += 1
                i = len(exp_item)
                flush_av()
                emit_scores(i, it)
                pend.append((i, it))
                if qgen is not None and next(qgen, "END") == "END":
                    qgen = None
                target = DENSE_TOTAL * (item_idx + 1) // TOTAL_ITEMS
                while dense_emitted[0] < target and \
                        next(dense_it, "END") != "END":
                    dense_emitted[0] += 1
                item_idx += 1
        while qgen is not None and next(qgen, "END") != "END":
            pass
    flush_av(all_=True)
    for _ in dense_it:
        pass
    v_attn_done = N("sV")

    DBG("attn", attn_fm, [("sV", v_attn_done)])
    DBG("g", g_fm, [evac_g[15]])

    p2_t_end = N("sT")
    psum2.close()
    p12.close()

    p4 = ExitStack()
    owbuf = [alloc(p4, f"owbuf{i}", [128, 2048], BF16) for i in range(4)]
    out_sb = [alloc(p4, f"out_sb{i}", [128, HID], F32) for i in range(2)]
    xf32 = [alloc(p4, f"xf32_{i}", [128, HID], F32, side="right") for i in range(4)]

    psum4 = ExitStack()
    psO = [psum4.enter_context(nc.psum_tensor(f"psO{i}", [128, 1024], F32)) for i in range(2)]
    psG = [psum4.enter_context(nc.psum_tensor(f"psG{i}", [128, 2048], BF16)) for i in range(2)]

    silu_done = [0] * 16
    for oc in range(16):
        W("s", evac_g[oc][0], evac_g[oc][1])
        OP("s", lambda e, oc=oc: e.activation(
            out=g_fm[:, oc * 1024:(oc + 1) * 1024],
            in_=g_fm[:, oc * 1024:(oc + 1) * 1024], func=AF.Silu), inc="sS")
        silu_done[oc] = N("sS")

    sXF = ["sXFa", "sXFb", "sXFc", "sXFd"]
    x_ld = [0] * NOWN
    comb_done = [0] * NOWN
    out_dma_cnt = [0] * NOWN

    def emit_xload(j):
        if j < 4:
            W("y", "sT", p2_t_end)
            W("y", "sV", v_attn_done)
        else:
            W("y", "sV", comb_done[j - 4])
        OP("y", lambda e, j=j: e.dma_start(
            out=xf32[j % 4][:], in_=D["x_kv"][(2 * j + 1) * 128:(2 * j + 2) * 128, :]),
           inc=sXF[j % 4], incn=16)
        x_ld[j] = N(sXF[j % 4])

    for j in range(4):
        emit_xload(j)

    oproj_done = [0] * 16
    sg_done = [0] * 16
    psG_free = [0, 0]
    OW_SEMS = ["sOWa", "sOWb", "sQa", "sQb"]
    ow_ld = [0] * 16

    def emit_ow_dma(oc):
        so_ = OW_SEMS[oc % 4]
        if oc >= 4:
            W("g", "sT", oproj_done[oc - 4])
        else:
            W("g", "sV", rope_done_q[15])
            W("g", "sT", t_qnorm[15])
            W("g", "sS", qevac_done[15])
        OP("g", lambda e, oc=oc: e.dma_start(
            out=owbuf[oc % 4][:], in_=D["ow_s"][:, oc * 2048:(oc + 1) * 2048]),
           inc=so_, incn=16)
        ow_ld[oc] = N(so_)

    for oc in range(4):
        emit_ow_dma(oc)
    for oc in range(16):
        so_ = OW_SEMS[oc % 4]
        W("t", so_, ow_ld[oc])
        if oc == 0:
            W("t", "sS", silu_done[0])
            W("t", "sV", v_attn_done)
            W("t", "sT", p2_t_end)
        if oc >= 2:
            W("t", "sV", sg_done[oc - 2])
        for ac in range(16):
            stat = owbuf[oc % 4][:, ac * 128:ac * 128 + 128]
            for g2 in range(2):
                mov = attn_fm[:, ac * 1024 + g2 * 512: ac * 1024 + g2 * 512 + 512]
                OP("t", lambda e, st=stat, m=mov, g2=g2, oc=oc, ac=ac: e.matmul(
                    psO[oc % 2][:, g2 * 512:g2 * 512 + 512], st, m,
                    start=(ac == 0), stop=(ac == 15)),
                   inc=("sT" if (ac == 15 and g2 == 1) else None))
        oproj_done[oc] = N("sT")
        if oc + 4 < 16:
            emit_ow_dma(oc + 4)
        W("v", "sT", oproj_done[oc])
        W("v", "sS", silu_done[oc])
        OP("v", lambda e, oc=oc: e.tensor_mul(
            g_fm[:, oc * 1024:(oc + 1) * 1024], psO[oc % 2][:, 0:1024],
            g_fm[:, oc * 1024:(oc + 1) * 1024]), inc="sV")
        sg_done[oc] = N("sV")
        oc2 = oc - 1
        if oc2 >= 0:
            W("t", "sV", sg_done[oc2])
            for j01 in range(2):
                OP("t", lambda e, oc2=oc2, j=j01: e.transpose(
                    psG[j][:, oc2 * 128:oc2 * 128 + 128],
                    g_fm[:, oc2 * 1024 + j * 128: oc2 * 1024 + j * 128 + 128],
                    identb_sb[:]))

    for j in range(NOWN):
        W("t", "sV", sg_done[15])
        W("t", "sV", psG_free[j % 2])
        ocs = [15] if j < 2 else list(range(16))
        for k2, oc in enumerate(ocs):
            OP("t", lambda e, oc=oc, j=j: e.transpose(
                psG[j % 2][:, oc * 128:oc * 128 + 128],
                g_fm[:, oc * 1024 + j * 128: oc * 1024 + j * 128 + 128],
                identb_sb[:]),
               inc=("sT" if k2 == len(ocs) - 1 else None))
        t_tr4 = N("sT")
        W("v", "sT", t_tr4)
        W("v", sXF[j % 4], x_ld[j])
        if j >= 2:
            W("v", "sOa" if j % 2 == 0 else "sOb", out_dma_cnt[j - 2])
        OP("v", lambda e, j=j: e.tensor_add(
            out_sb[j % 2][:], psG[j % 2][:, 0:2048], xf32[j % 4][:]), inc="sV")
        comb_done[j] = N("sV")
        psG_free[j % 2] = comb_done[j]
        W("y", "sV", comb_done[j])
        sout = "sOa" if j % 2 == 0 else "sOb"
        OP("y", lambda e, j=j: e.dma_start(
            out=out_d[j * 128:(j + 1) * 128, :], in_=out_sb[j % 2][:]),
           inc=sout, incn=16)
        out_dma_cnt[j] = N(sout)
        if j + 4 < NOWN:
            emit_xload(j + 4)

    W("y", "sOa", N("sOa"))
    W("y", "sOb", N("sOb"))

    psum4.close()
    p4.close()
    p2.close()

    with nc.Block() as blk:
        @blk.tensor
        def _(e):
            for f in OPS["t"]:
                f(e)

        @blk.vector
        def _(e):
            for f in OPS["v"]:
                f(e)

        @blk.scalar
        def _(e):
            for f in OPS["s"]:
                f(e)

        @blk.gpsimd
        def _(e):
            for f in OPS["g"]:
                f(e)

        @blk.sync
        def _(e):
            for f in OPS["y"]:
                f(e)

    ctx.close()
    lower_extended_insts(nc)
    return nc


_NC_CACHE = {}


def _get_nc(dbg=()):
    key = tuple(sorted(dbg))
    if key not in _NC_CACHE:
        _NC_CACHE[key] = _build(key)
    return _NC_CACHE[key]


def kernel(**inputs):
    in_maps = _host_prepare(inputs)
    nc = _get_nc()
    res = run_bass_kernel_spmd(nc, in_maps, core_ids=list(range(8)))
    odt = np.result_type(np.float32, res.results[0]["out"].dtype)
    out = np.empty((4, S, HID), dtype=odt)
    for core in range(8):
        seq, b = core % 4, core // 4
        o = res.results[core]["out"]
        for j in range(NOWN):
            out[seq, (2 * j + b) * 128:(2 * j + b + 1) * 128, :] = o[j * 128:(j + 1) * 128, :]
    return out
